# version 18
# baseline (speedup 1.0000x reference)
import numpy as np
import ml_dtypes
from contextlib import ExitStack
import concourse.bass as bass
import concourse.mybir as mybir
from concourse.bass_utils import run_bass_kernel_spmd

F32 = mybir.dt.float32
BF16 = mybir.dt.bfloat16
AF = mybir.ActivationFunctionType
ALU = mybir.AluOpType
AX = mybir.AxisListType

T, C = 2048, 1024
NT = T // 128
NCH = C // 128
XO = 4
XCW = T + XO
MAGIC = 12582912.0
NCORE = 8


class Ctr:
    __slots__ = ("h", "n")

    def __init__(self, h):
        self.h = h
        self.n = 0

    def inc(self, inst, k=1):
        inst.then_inc(self.h, k)
        self.n += k
        return inst




class _Rec:
    __slots__ = ("meth", "args", "kwargs", "incs")

    def __init__(self, meth, args, kwargs):
        self.meth = meth
        self.args = args
        self.kwargs = kwargs
        self.incs = []

    def then_inc(self, h, k):
        self.incs.append((h, k))
        return self


class _ShimEngine:
    def __init__(self, recs):
        self._recs = recs

    def wait_ge(self, h, v):
        r = _Rec("wait_ge", (h, v), {})
        self._recs.append(r)
        return r

    def __getattr__(self, meth):
        def call(*a, **k):
            r = _Rec(meth, a, k)
            self._recs.append(r)
            return r
        return call


class _FakeBlk:

    def __init__(self, shims):
        for name, sh in shims.items():
            setattr(self, name, (lambda sh: (lambda fn: fn(sh)))(sh))


class _ProxyNC:

    def __init__(self, real, shims):
        object.__setattr__(self, "_real", real)
        object.__setattr__(self, "_shims", shims)

    def __getattr__(self, name):
        sh = self._shims.get(name)
        if sh is not None:
            return sh
        return getattr(self._real, name)


class B:
    def __init__(self, blocks):
        self.blocks = sorted(blocks)

    def build(self):
        nc = self.nc = bass.Bass("TRN2", target_bir_lowering=False)
        io = self.io = {}

        def dram(name, shape, dt=F32, kind="ExternalInput"):
            io[name] = nc.dram_tensor(name, shape, dt, kind=kind).ap()

        dram("x_in", [T, C])
        dram("y_out", [T, C], kind="ExternalOutput")
        dram("ident32", [128, 128])
        dram("identbf", [128, 128], BF16)
        dram("maskbf", [128, 128], BF16)
        dram("onesbf", [1, 128], BF16)
        nblk = len(self.blocks)
        self.xbuf, self.xsrc = {}, {}
        prev = io["x_in"]
        for i, b in enumerate(self.blocks):
            self.xsrc[b] = prev
            if i == nblk - 1:
                self.xbuf[b] = io["y_out"]
            else:
                dram(f"xb{b}", [T, C], kind="Internal")
                self.xbuf[b] = io[f"xb{b}"]
            prev = self.xbuf[b]
        if 0 in self.blocks:
            dram("W_attn", [C, 3 * C])
            dram("W_proj0", [C, C])
        if 1 in self.blocks:
            dram("W_r", [C, C]); dram("W_k", [C, C]); dram("W_v", [C, C])
            dram("mu_r", [C]); dram("mu_k", [C])
        if 2 in self.blocks:
            dram("Wq", [C, C]); dram("Wk", [C, C]); dram("Wv", [C, C])
            dram("bq", [C]); dram("bk", [C]); dram("bv", [C])
        if 3 in self.blocks:
            dram("W_fc", [4 * C, C]); dram("b_fc", [4 * C])
            dram("A_fc", [16, C]); dram("B_fc", [4 * C, 16])
            dram("W_proj1", [C, 4 * C]); dram("b_proj", [C])
            dram("A_proj", [16, 4 * C]); dram("B_proj", [C, 16])
            dram("wpcache", [4 * C, C], BF16, kind="Internal")

        with ExitStack() as ctx:
            self.ctx = ctx
            self.guard = {}
            self.alloc_common()
            self.prologue()
            engs = ("tensor", "vector", "scalar", "gpsimd", "sync")
            self.chunks = []
            self._lists = {e: [] for e in engs}
            shims = {e: _ShimEngine(self._lists[e]) for e in engs}
            self._engs = engs
            self._shims = shims
            self.cut()
            realnc = self.nc
            self.nc = _ProxyNC(realnc, shims)
            self.blk = _FakeBlk(shims)
            self.load_consts()
            self.phase_build_xt(self.xsrc[self.blocks[0]])
            for b in self.blocks:
                self.barrier()
                self.cut()
                getattr(self, f"phase_block{b}")()
            self.barrier()
            self.epilogue()
            self.cut()
            self.nc = realnc
            with realnc.Block() as blk:
                for chunk in self.chunks:
                    for e in engs:
                        if not chunk[e]:
                            continue

                        def body(eng, recs=chunk[e]):
                            for r in recs:
                                if r.meth == "wait_ge":
                                    eng.wait_ge(*r.args)
                                else:
                                    inst = getattr(eng, r.meth)(*r.args,
                                                                **r.kwargs)
                                    for h, k in r.incs:
                                        inst.then_inc(h, k)
                        getattr(blk, e)(body)
        return nc

    def cut(self):
        chunk = {}
        for e in self._engs:
            lst = self._lists[e]
            chunk[e] = lst
            self._lists[e] = []
            self._shims[e]._recs = self._lists[e]
        if any(chunk.values()):
            self.chunks.append(chunk)

    def sb(self, name, shape, dt=F32, ctx=None):
        return (ctx or self.ctx).enter_context(
            self.nc.sbuf_tensor("sb_" + name, shape, dt))

    def ps(self, name, shape, dt=F32):
        return self.ctx.enter_context(self.nc.psum_tensor(name, shape, dt))

    def sem(self, name):
        return Ctr(self.ctx.enter_context(self.nc.semaphore(name)))

    def alloc_common(self):
        self.s_dma = self.sem("s_dma")
        self.s_gdma = self.sem("s_gdma")
        self.s_mm = self.sem("s_mm")
        self.s_v = self.sem("s_v")
        self.s_act = self.sem("s_act")
        self.s_out = self.sem("s_out")
        self.s_x = [self.sem("s_x0"), self.sem("s_x1")]
        self.s_w = [self.sem("s_w0"), self.sem("s_w1")]
        self.s_wld = self.sem("s_wld")
        self.s_p = [self.sem("s_p0"), self.sem("s_p1"),
                    self.sem("s_p2"), self.sem("s_p3")]
        self.s_pool = self.sem("s_pool")
        self.all_sems = [self.s_dma, self.s_gdma, self.s_mm, self.s_v,
                         self.s_act, self.s_out, self.s_wld,
                         self.s_x[0], self.s_x[1], self.s_w[0], self.s_w[1],
                         self.s_p[0], self.s_p[1], self.s_p[2], self.s_p[3],
                         self.s_pool]
        self.ident32 = self.sb("ident32", [128, 128])
        self.identbf = self.sb("identbf", [128, 128], BF16)
        self.maskbf = self.sb("maskbf", [128, 128], BF16)
        self.onesbf = self.sb("onesbf", [1, 128], BF16)
        self.cols = self.sb("cols", [128, 8])
        self.lncols = self.sb("lncols", [128, 48])
        self.atcols = self.sb("atcols", [128, 24])
        self.attn_it = 0
        self.xstage = self.sb("xstage", [128, 2 * C])
        self.lnout = self.sb("lnout", [128, C])
        self.sqtmp = self.sb("sqtmp", [128, C], BF16)
        self.ffntmp = self.sb("ffntmp", [128, C])
        self.xT = self.sb("xT", [128, NCH * XCW], BF16)
        if 0 in self.blocks or 2 in self.blocks:
            self.P_s2 = [self.sb("P_s", [128, T], BF16),
                         self.sb("P_sb", [128, T], BF16)]
            self.PT = self.sb("PT", [128, T], BF16)
        self.PS_A = self.ps("PS_A", [128, 2048])
        self.PS_B = self.ps("PS_B", [128, 1024])
        self.PS_C = self.ps("PS_C", [128, 512])
        self.PS_D = self.ps("PS_D", [128, 1024], BF16)
        self.PS_A1k = self.PS_A[:, 0:1024]
        self.PS_B1k = self.PS_B[:, 0:1024]

    def prologue(self):
        for s in self.all_sems:
            self.nc.gpsimd.sem_clear(s.h)
        self.nc.all_engine_barrier()

    def epilogue(self):
        s_out = self.s_out

        @self.blk.sync
        def _(sync):
            sync.wait_ge(s_out.h, s_out.n)

    def barrier(self):
        blk = self.blk
        vals = [(c.h, c.n) for c in self.all_sems]
        for eng in ("tensor", "vector", "scalar", "sync", "gpsimd"):
            def mk(e, vals=vals):
                for h, v in vals:
                    e.wait_ge(h, v)
            getattr(blk, eng)(mk)

    def vsync(self, vector, inst):
        self.s_v.inc(inst)
        vector.wait_ge(self.s_v.h, self.s_v.n)

    def vpad(self, vector):
        return vector.tensor_copy(self.cols[:, 4:5], self.cols[:, 3:4])

    def spad(self, scalar):
        return scalar.copy(self.cols[:, 5:6], self.cols[:, 3:4])

    _PSA_KEYS = ("PS_A", "PSAh0", "PSAh1")

    def wait_guard(self, eng, key):
        keys = self._PSA_KEYS if key in self._PSA_KEYS else (key,)
        for kk2 in keys:
            g = self.guard.get(kk2)
            if g is not None:
                eng.wait_ge(g[0].h, g[1])

    def set_guard(self, key, ctr):
        self.guard[key] = (ctr, ctr.n)

    def load_consts(self):
        blk, io = self.blk, self.io
        s_dma, s_v = self.s_dma, self.s_v

        @blk.sync
        def _(sync):
            s_dma.inc(sync.dma_start(out=self.ident32[:], in_=io["ident32"]), 16)
            s_dma.inc(sync.dma_start(out=self.identbf[:], in_=io["identbf"]), 16)
            s_dma.inc(sync.dma_start(out=self.maskbf[:], in_=io["maskbf"]), 16)
            s_dma.inc(sync.dma_start(out=self.onesbf[:], in_=io["onesbf"]), 16)

        @blk.vector
        def _(vector):
            vector.memset(self.cols[:, 0:1], 1e-5)
            vector.memset(self.cols[:, 1:2], 0.125)
            vector.memset(self.cols[:, 2:3], 0.03125)
            vector.memset(self.cols[:, 6:7], -0.5)
            vector.memset(self.lncols[:], 0.0)
            vector.memset(self.atcols[:], 0.0)
            for ci in range(NCH):
                vector.memset(self.xT[:, XCW * ci:XCW * ci + XO], 0.0)
            if 0 in self.blocks or 2 in self.blocks:
                vector.memset(self.P_s2[0][:, 0:2], 0.0)
            s_v.inc(vector.memset(self.cols[:, 3:4], 1.0))

    def xtv(self):
        return self.xT[:, :].rearrange("p (ci w) -> p ci w", w=XCW)

    def emit_xt_tile(self, src2d, ti, pre=()):
        blk, nc = self.blk, self.nc
        s_mm, s_v = self.s_mm, self.s_v
        for half in (0, 1):
            @blk.tensor
            def _(tensor, half=half):
                if half == 0:
                    for cc, vv in pre:
                        tensor.wait_ge(cc.h, vv)
                self.wait_guard(tensor, "PS_C")
                for k in range(4):
                    ci = 4 * half + k
                    inst = nc.tensor.transpose(
                        self.PS_C[:, 128 * k:128 * k + 128],
                        src2d[:, 128 * ci:128 * ci + 128], self.ident32[:])
                s_mm.inc(inst)

            @blk.vector
            def _(vector, half=half):
                vector.wait_ge(s_mm.h, s_mm.n)
                dst = self.xtv()[:, 4 * half:4 * half + 4,
                                 XO + 128 * ti:XO + 128 * ti + 128]
                src = self.PS_C[:, :].rearrange("p (k t) -> p k t", t=128)
                s_v.inc(vector.tensor_copy(dst, src))
            self.set_guard("PS_C", s_v)

    def phase_build_xt(self, xsrc):
        cval = self.s_dma.n
        for ti in range(NT):
            sx, dv = self.prefetch_x(xsrc, ti)
            k = ti % 2
            self.emit_xt_tile(self.xstage[:, C * k:C * k + C], ti,
                              pre=[(sx, dv), (self.s_dma, cval)])
            self.set_guard(f"xst{k}", self.s_mm)

    def prefetch_x(self, xsrc, ti):
        blk = self.blk
        k = ti % 2
        sx = self.s_x[k]

        @blk.sync
        def _(sync):
            self.wait_guard(sync, f"xst{k}")
            sx.inc(sync.dma_start(
                out=self.xstage[:, C * k:C * k + C],
                in_=xsrc[128 * ti:128 * ti + 128, :]), 16)
        return (sx, sx.n)

    def emit_ln_tile(self, ti, psum_ap, out_hbm, build_xt, mm_val,
                     produce_y=None, dma_val=None):
        blk = self.blk
        s_v, s_act, s_dma, s_out = self.s_v, self.s_act, self.s_dma, self.s_out
        xsl = self.xstage[:, C * (ti % 2):C * (ti % 2) + C]
        lc = 12 * (ti % 4)
        L = self.lncols

        @blk.vector
        def _(vector):
            vector.wait_ge(dma_val[0].h, dma_val[1])
            vector.wait_ge(self.s_mm.h, mm_val)
            if produce_y is None:
                self.vsync(vector, vector.tensor_add(xsl, xsl, psum_ap))
            else:
                produce_y(vector, xsl)
            s_v.inc(vector.reduce_sum(L[:, lc:lc + 1], xsl, axis=AX.X))
        self.set_guard(self._ln_psum_key, s_v)

        @blk.scalar
        def _(scalar):
            scalar.wait_ge(s_v.h, s_v.n)
            self.wait_guard(scalar, "ffntmp")
            s_act.inc(scalar.activation(self.ffntmp[:], xsl, AF.Square))

        @blk.vector
        def _(vector):
            vector.wait_ge(s_act.h, s_act.n)
            self.vsync(vector, vector.reduce_sum(L[:, lc + 1:lc + 2],
                                                 self.ffntmp[:], axis=AX.X))
            vector.tensor_scalar(L[:, lc + 2:lc + 3], L[:, lc:lc + 1],
                                 1.0 / C, None, ALU.mult)
            self.vsync(vector, vector.tensor_scalar(
                L[:, lc + 3:lc + 4], L[:, lc + 1:lc + 2],
                1.0 / C, None, ALU.mult))
            self.vsync(vector, vector.tensor_scalar(
                L[:, lc + 4:lc + 5], L[:, lc + 2:lc + 3],
                L[:, lc + 2:lc + 3], None, ALU.mult))
            s_v.inc(vector.tensor_sub(L[:, lc + 5:lc + 6], L[:, lc + 3:lc + 4],
                                      L[:, lc + 4:lc + 5]))
        self.set_guard("ffntmp", s_v)

        @blk.scalar
        def _(scalar):
            scalar.wait_ge(s_v.h, s_v.n)
            s_act.inc(scalar.activation(L[:, lc + 6:lc + 7],
                                        L[:, lc + 5:lc + 6],
                                        AF.Ln, bias=self.cols[:, 0:1]))
            scalar.wait_ge(s_act.h, s_act.n)
            s_act.inc(scalar.activation(L[:, lc + 7:lc + 8],
                                        L[:, lc + 6:lc + 7],
                                        AF.Exp, scale=self.cols[:, 6:7]))

        @blk.vector
        def _(vector):
            vector.wait_ge(s_act.h, s_act.n)
            s_v.inc(vector.tensor_scalar(
                L[:, lc + 8:lc + 9], L[:, lc + 2:lc + 3],
                L[:, lc + 7:lc + 8], -1.0, ALU.mult, ALU.mult))

        @blk.scalar
        def _(scalar):
            scalar.wait_ge(s_v.h, s_v.n)
            self.wait_guard(scalar, "lnout_mm")
            self.wait_guard(scalar, "lnout_out")
            s_act.inc(scalar.activation(self.lnout[:], xsl, AF.Identity,
                                        scale=L[:, lc + 7:lc + 8],
                                        bias=L[:, lc + 8:lc + 9]))
        self.set_guard(f"xst{ti % 2}", s_act)
        act_val = s_act.n

        @blk.sync
        def _(sync):
            sync.wait_ge(s_act.h, act_val)
            s_out.inc(sync.dma_start(
                out=out_hbm[128 * ti:128 * ti + 128, :], in_=self.lnout[:]), 16)
        self.set_guard("lnout_out", s_out)
        if build_xt:
            self.emit_xt_tile(self.lnout[:], ti, pre=[(s_act, act_val)])
            self.set_guard("lnout_mm", self.s_mm)

    def attn_pre(self, qi, scale_col, emit_S):
        blk = self.blk
        s_mm, s_v, s_act = self.s_mm, self.s_v, self.s_act
        W = (qi + 1) * 128
        halves = [(0, min(W, 1024))]
        if W > 1024:
            halves.append((1024, W - 1024))
        a0 = 12 * (self.attn_it % 2)
        P_s = self.P_s2[self.attn_it % 2]
        pskey = f"P_s{self.attn_it % 2}"
        self.attn_it += 1
        A = self.atcols
        for sh, (c0, cw) in enumerate(halves):
            it2 = self.attn_it * 2 + sh
            poff = 1024 * (it2 % 2)
            pk = f"PSAh{it2 % 2}"

            @blk.tensor
            def _(tensor, c0=c0, cw=cw, poff=poff, pk=pk, W=W):
                self.wait_guard(tensor, pk)
                emit_S(tensor, c0, cw, poff, c0 + cw == W)

            @blk.vector
            def _(vector, sh=sh, poff=poff, cw=cw):
                vector.wait_ge(s_mm.h, s_mm.n)
                self.vsync(vector, vector.reduce_max(
                    A[:, a0 + sh:a0 + sh + 1],
                    self.PS_A[:, poff:poff + cw], axis=AX.X))
                s_v.inc(vector.tensor_scalar(
                    A[:, a0 + 2 + sh:a0 + 3 + sh], A[:, a0 + sh:a0 + sh + 1],
                    scale_col, -1.0, ALU.mult, ALU.mult))

            @blk.scalar
            def _(scalar, sh=sh, c0=c0, cw=cw, poff=poff, P_s=P_s, pskey=pskey):
                scalar.wait_ge(s_v.h, s_v.n)
                if sh == 0:
                    self.wait_guard(scalar, pskey)
                s_act.inc(scalar.activation(
                    P_s[:, c0:c0 + cw], self.PS_A[:, poff:poff + cw],
                    AF.Exp, scale=scale_col, bias=A[:, a0 + 2 + sh:a0 + 3 + sh]))
            self.set_guard(pk, s_act)
        return dict(qi=qi, W=W, halves=halves, a0=a0, P_s=P_s, pskey=pskey,
                    scale_col=scale_col, exp_val=s_act.n)

    def attn_post(self, ctx):
        blk = self.blk
        s_v, s_act, s_pool = self.s_v, self.s_act, self.s_pool
        qi, W, halves, a0 = ctx["qi"], ctx["W"], ctx["halves"], ctx["a0"]
        P_s, scale_col = ctx["P_s"], ctx["scale_col"]
        A = self.atcols

        @blk.vector
        def _(vector):
            vector.wait_ge(s_act.h, ctx["exp_val"])
            for sh, (c0, cw) in enumerate(halves):
                self.vsync(vector, vector.reduce_sum(
                    A[:, a0 + 4 + sh:a0 + 5 + sh],
                    P_s[:, c0:c0 + cw], axis=AX.X))
            if len(halves) == 1:
                self.vsync(vector, vector.reciprocal(A[:, a0 + 6:a0 + 7],
                                                     A[:, a0 + 4:a0 + 5]))
                s_v.inc(vector.tensor_scalar(
                    P_s[:, 0:W], P_s[:, 0:W],
                    A[:, a0 + 6:a0 + 7], None, ALU.mult))
            else:
                self.vsync(vector, vector.tensor_max(
                    A[:, a0 + 6:a0 + 7], A[:, a0:a0 + 1], A[:, a0 + 1:a0 + 2]))
                s_v.inc(vector.tensor_scalar(
                    A[:, a0 + 7:a0 + 9], A[:, a0:a0 + 2],
                    A[:, a0 + 6:a0 + 7], scale_col, ALU.subtract, ALU.mult))
        if len(halves) > 1:
            @blk.scalar
            def _(scalar):
                scalar.wait_ge(s_v.h, s_v.n)
                s_act.inc(scalar.activation(A[:, a0 + 9:a0 + 11],
                                            A[:, a0 + 7:a0 + 9], AF.Exp))

            @blk.vector
            def _(vector):
                vector.wait_ge(s_act.h, s_act.n)
                self.vsync(vector, vector.tensor_mul(
                    A[:, a0 + 7:a0 + 9], A[:, a0 + 9:a0 + 11],
                    A[:, a0 + 4:a0 + 6]))
                self.vsync(vector, vector.tensor_add(
                    A[:, a0 + 11:a0 + 12], A[:, a0 + 7:a0 + 8],
                    A[:, a0 + 8:a0 + 9]))
                self.vsync(vector, vector.reciprocal(A[:, a0 + 6:a0 + 7],
                                                     A[:, a0 + 11:a0 + 12]))
                self.vsync(vector, vector.tensor_scalar(
                    A[:, a0 + 2:a0 + 4], A[:, a0 + 9:a0 + 11],
                    A[:, a0 + 6:a0 + 7], None, ALU.mult))
                vector.tensor_scalar(P_s[:, 0:1024], P_s[:, 0:1024],
                                     A[:, a0 + 2:a0 + 3], None, ALU.mult)
                s_v.inc(vector.tensor_scalar(
                    P_s[:, 1024:W], P_s[:, 1024:W],
                    A[:, a0 + 3:a0 + 4], None, ALU.mult))
        return s_v.n

    def emit_p_transposes(self, qi, pnorm_val, P_s=None, pskey="P_s"):
        if P_s is None:
            P_s = self.P_s2[0]
        blk, nc = self.blk, self.nc
        s_mm, s_v = self.s_mm, self.s_v
        for w0 in range(0, qi + 1, 8):
            w1 = min(w0 + 8, qi + 1)

            @blk.tensor
            def _(tensor, w0=w0, w1=w1):
                tensor.wait_ge(s_v.h, pnorm_val)
                self.wait_guard(tensor, "PS_D")
                for si in range(w0, w1):
                    inst = nc.tensor.transpose(
                        self.PS_D[:, 128 * (si - w0):128 * (si - w0) + 128],
                        P_s[:, 128 * si:128 * si + 128], self.identbf[:])
                s_mm.inc(inst)

            @blk.scalar
            def _(scalar, w0=w0, w1=w1):
                scalar.wait_ge(self.s_mm.h, self.s_mm.n)
                self.s_act.inc(scalar.activation(
                    self.PT[:, 128 * w0:128 * w1],
                    self.PS_D[:, 0:128 * (w1 - w0)], AF.Identity))
            self.set_guard("PS_D", self.s_act)
        self.set_guard(pskey, self.s_act)
        return ("act", self.s_act.n)

    def emit_wT_transposes(self, wstage, dst, wval, busykey=None):
        blk, nc = self.blk, self.nc
        s_mm, s_v = self.s_mm, self.s_v
        for dj in range(NCH):
            @blk.tensor
            def _(tensor, dj=dj):
                tensor.wait_ge(self.s_wld.h, wval)
                self.wait_guard(tensor, "PS_D")
                if dj == 0 and busykey:
                    self.wait_guard(tensor, busykey)
                for ci in range(NCH):
                    inst = nc.tensor.transpose(
                        self.PS_D[:, 128 * ci:128 * ci + 128],
                        wstage[:, 1024 * dj + 128 * ci:
                               1024 * dj + 128 * ci + 128],
                        self.identbf[:])
                s_mm.inc(inst)

            @blk.vector
            def _(vector, dj=dj):
                vector.wait_ge(s_mm.h, s_mm.n)
                dv_ = dst[:, :].rearrange("p (ci d) -> p ci d", d=1024)[
                    :, :, 128 * dj:128 * dj + 128]
                sv_ = self.PS_D[:, :].rearrange("p (ci c) -> p ci c", c=128)
                s_v.inc(vector.tensor_copy(dv_, sv_))
            self.set_guard("PS_D", s_v)
        return s_v.n

    def phase_block0(self):
        blk, nc, io = self.blk, self.nc, self.io
        s_dma, s_gdma, s_mm, s_v, s_act = (self.s_dma, self.s_gdma, self.s_mm,
                                           self.s_v, self.s_act)
        with ExitStack() as pctx:
            qTp = [self.sb(f"qTp{k}", [128, T], BF16, pctx) for k in (0, 1)]
            kTp = [self.sb(f"kTp{k}", [128, T], BF16, pctx) for k in (0, 1)]
            vTp = [self.sb(f"vTp{k}", [128, T], BF16, pctx) for k in (0, 1)]
            wAq = [self.sb(f"wAq{k}", [128, NCH * 128], BF16, pctx)
                   for k in (0, 1)]
            wAk = [self.sb(f"wAk{k}", [128, NCH * 128], BF16, pctx)
                   for k in (0, 1)]
            wAv = [self.sb(f"wAv{k}", [128, NCH * 128], BF16, pctx)
                   for k in (0, 1)]
            oT = self.sb("oT", [128, NCH * T], BF16, pctx)
            wP = self.sb("wP", [128, NCH * 1024], BF16, pctx)

            WA = io["W_attn"].rearrange("(ci p) e -> p ci e", p=128)
            for p in range(8):
                self.cut()
                k = p % 2

                sw = self.s_w[k]

                @blk.gpsimd
                def _(gpsimd, p=p, k=k, sw=sw):
                    self.wait_guard(gpsimd, f"wA{k}")
                    for dst, off in ((wAq[k], 0), (wAk[k], C), (wAv[k], 2 * C)):
                        sw.inc(gpsimd.dma_start(
                            out=dst[:, :].rearrange("p (ci c) -> p ci c", c=128),
                            in_=WA[:, :, off + 128 * p:off + 128 * p + 128]), 16)
                waval = sw.n

                for wsrc, dst in ((wAq[k], qTp[k]), (wAk[k], kTp[k])):
                    @blk.tensor
                    def _(tensor, wsrc=wsrc, waval=waval, k=k, sw=sw):
                        tensor.wait_ge(sw.h, waval)
                        self.wait_guard(tensor, "PS_A")
                        self.wait_guard(tensor, f"qkv{k}")
                        for ci in range(NCH):
                            for n in range(4):
                                inst = nc.tensor.matmul(
                                    self.PS_A[:, 512 * n:512 * n + 512],
                                    wsrc[:, 128 * ci:128 * ci + 128],
                                    self.xT[:, XCW * ci + XO + 512 * n:
                                            XCW * ci + XO + 512 * n + 512],
                                    start=(ci == 0), stop=(ci == NCH - 1))
                        s_mm.inc(inst)

                    @blk.vector
                    def _(vector, dst=dst):
                        vector.wait_ge(s_mm.h, s_mm.n)
                        s_v.inc(vector.tensor_copy(dst[:], self.PS_A[:]))
                    self.set_guard("PS_A", s_v)

                for g in range(4):
                    @blk.tensor
                    def _(tensor, g=g, k=k, waval=waval, sw=sw):
                        tensor.wait_ge(sw.h, waval)
                        self.wait_guard(tensor, "PS_B")
                        self.wait_guard(tensor, f"qkv{k}")
                        for j in range(4):
                            ti = 4 * g + j
                            for ci in range(NCH):
                                inst = nc.tensor.matmul(
                                    self.PS_B[:, 128 * j:128 * j + 128],
                                    self.xT[:, XCW * ci + XO + 128 * ti:
                                            XCW * ci + XO + 128 * ti + 128],
                                    wAv[k][:, 128 * ci:128 * ci + 128],
                                    start=(ci == 0), stop=(ci == NCH - 1))
                        s_mm.inc(inst)

                    @blk.vector
                    def _(vector, g=g, k=k):
                        vector.wait_ge(s_mm.h, s_mm.n)
                        s_v.inc(vector.tensor_copy(
                            vTp[k][:, 512 * g:512 * g + 512],
                            self.PS_B[:, 0:512]))
                    self.set_guard("PS_B", s_v)
                self.set_guard(f"wA{k}", s_v)

                def finish0(ctx):
                    qi, hl, kk = ctx["qi"], ctx["hl"], ctx["k"]
                    pp = ctx["p"]
                    pn = self.attn_post(ctx)
                    self.emit_p_transposes(qi, pn, ctx["P_s"], ctx["pskey"])
                    qc, qr = qi // 4, qi % 4

                    @blk.tensor
                    def _(tensor, qi=qi, kk=kk, hl=hl, qr=qr):
                        tensor.wait_ge(s_v.h, s_v.n)
                        tensor.wait_ge(s_act.h, s_act.n)
                        if qr == 0:
                            self.wait_guard(tensor, "PS_C")
                        for si in range(qi + 1):
                            inst = nc.tensor.matmul(
                                self.PS_C[64 * hl:64 * hl + 64,
                                          128 * qr:128 * qr + 128],
                                vTp[kk][:, 128 * si + 64 * hl:
                                        128 * si + 64 * hl + 64],
                                self.PT[:, 128 * si:128 * si + 128],
                                start=(si == 0), stop=(si == qi),
                                tile_position=(0, 64 * hl))
                        s_mm.inc(inst)

                    if qr == 3:
                        @blk.scalar
                        def _(scalar, pp=pp, hl=hl, qc=qc):
                            scalar.wait_ge(s_mm.h, s_mm.n)
                            s_act.inc(scalar.activation(
                                oT[64 * hl:64 * hl + 64,
                                   T * pp + 512 * qc:T * pp + 512 * qc + 512],
                                self.PS_C[64 * hl:64 * hl + 64, 0:512],
                                AF.Identity))
                        self.set_guard("PS_C", s_act)

                pend = None
                for hl in (0, 1):
                    hs = slice(64 * hl, 64 * hl + 64)
                    for qi in range(NT):
                        def emit_S(tensor, c0, cw, poff, has_diag,
                                   k=k, hs=hs, hl=hl, qi=qi):
                            for n0 in range(0, cw, 512):
                                nw = min(512, cw - n0)
                                inst = nc.tensor.matmul(
                                    self.PS_A[:, poff + n0:poff + n0 + nw],
                                    qTp[k][hs, 128 * qi:128 * qi + 128],
                                    kTp[k][hs, c0 + n0:c0 + n0 + nw],
                                    start=True, stop=False,
                                    tile_position=(64 * hl, 0))
                            if has_diag:
                                inst = nc.tensor.matmul(
                                    self.PS_A[:, poff + cw - 128:poff + cw],
                                    self.identbf[:], self.maskbf[:],
                                    start=False, stop=True)
                            s_mm.inc(inst)

                        ctx = self.attn_pre(qi, self.cols[:, 1:2], emit_S)
                        ctx.update(hl=hl, k=k, p=p)
                        if pend is not None:
                            finish0(pend)
                        pend = ctx
                finish0(pend)
                self.set_guard(f"qkv{k}", s_mm)

            @blk.gpsimd
            def _(gpsimd):
                s_gdma.inc(gpsimd.dma_start(
                    out=wP[:, :].rearrange("p (ci e) -> p ci e", e=1024),
                    in_=io["W_proj0"].rearrange("(ci p) e -> p ci e", p=128)),
                    16)
            wpval = s_gdma.n

            xb, xsrc = self.xbuf[0], self.xsrc[0]
            build_xt = (1 in self.blocks)
            for ti in range(NT):
                dv = self.prefetch_x(xsrc, ti)
                pa = self.PS_B1k if ti % 2 == 0 else self.PS_A1k
                pkey = "PS_B" if ti % 2 == 0 else "PS_A"
                self._ln_psum_key = pkey

                @blk.tensor
                def _(tensor, ti=ti, pa=pa, pkey=pkey, wpval=wpval):
                    tensor.wait_ge(s_gdma.h, wpval)
                    self.wait_guard(tensor, pkey)
                    for ci in range(NCH):
                        for n in range(2):
                            inst = nc.tensor.matmul(
                                pa[:, 512 * n:512 * n + 512],
                                oT[:, T * ci + 128 * ti:T * ci + 128 * ti + 128],
                                wP[:, 1024 * ci + 512 * n:
                                   1024 * ci + 512 * n + 512],
                                start=(ci == 0), stop=(ci == NCH - 1))
                    s_mm.inc(inst)

                self.emit_ln_tile(ti, pa, xb, build_xt, s_mm.n, dma_val=dv)

    def phase_block1(self):
        blk, nc, io = self.blk, self.nc, self.io
        s_dma, s_gdma, s_mm, s_v, s_act = (self.s_dma, self.s_gdma, self.s_mm,
                                           self.s_v, self.s_act)
        dbg = getattr(self, "debug_b1", False)
        with ExitStack() as pctx:
            if dbg:
                pctx = self.ctx
            wrT = self.sb("wrT", [128, NCH * 1024], BF16, pctx)
            wkT = self.sb("wkT", [128, NCH * 1024], BF16, pctx)
            wvT = self.sb("wvT", [128, NCH * 1024], BF16, pctx)
            wstage = self.sb("wstage", [128, NCH * 1024], BF16, pctx)
            xrS = self.sb("xrS", [128, 2 * NCH * 512], BF16, pctx)
            xkS = self.sb("xkS", [128, 2 * NCH * 512], BF16, pctx)
            rS = self.sb("rS", [128, 4 * 1024], BF16, pctx)
            kS = self.sb("kS", [128, NCH * 512], BF16, pctx)
            mixtmp = self.sb("mixtmp", [128, 512], BF16, pctx)
            murow = self.sb("murow", [8, 256], F32, pctx)
            mucols = self.sb("mucols", [128, 8], F32, pctx)
            mu1cols = self.sb("mu1cols", [128, 8], F32, pctx)
            mkcols = self.sb("mkcols", [128, 8], F32, pctx)
            mk1cols = self.sb("mk1cols", [128, 8], F32, pctx)

            @blk.sync
            def _(sync):
                s_dma.inc(sync.dma_start(
                    out=murow[:, 0:128],
                    in_=io["mu_r"].rearrange("(a p) -> a p", p=128)), 16)
                s_dma.inc(sync.dma_start(
                    out=murow[:, 128:256],
                    in_=io["mu_k"].rearrange("(a p) -> a p", p=128)), 16)
            muval = s_dma.n

            @blk.tensor
            def _(tensor):
                tensor.wait_ge(s_dma.h, muval)
                self.wait_guard(tensor, "PS_C")
                nc.tensor.transpose(self.PS_C[:, 0:8], murow[:, 0:128],
                                    self.ident32[0:8, 0:8])
                s_mm.inc(nc.tensor.transpose(self.PS_C[:, 8:16],
                                             murow[:, 128:256],
                                             self.ident32[0:8, 0:8]))

            @blk.vector
            def _(vector):
                vector.wait_ge(s_mm.h, s_mm.n)
                vector.tensor_copy(mucols[:], self.PS_C[:, 0:8])
                self.vsync(vector, vector.tensor_copy(mkcols[:],
                                                      self.PS_C[:, 8:16]))
                vector.tensor_scalar(mu1cols[:], mucols[:], 1.0, -1.0,
                                     ALU.subtract, ALU.mult)
                s_v.inc(vector.tensor_scalar(mk1cols[:], mkcols[:], 1.0, -1.0,
                                             ALU.subtract, ALU.mult))
            self.set_guard("PS_C", s_v)

            for wname, dst in (("W_r", wrT), ("W_k", wkT), ("W_v", wvT)):
                @blk.gpsimd
                def _(gpsimd, wname=wname):
                    self.wait_guard(gpsimd, "wstage")
                    self.s_wld.inc(gpsimd.dma_start(
                        out=wstage[:, :].rearrange("p (dj c) -> p dj c", c=1024),
                        in_=io[wname].rearrange("(dj p) c -> p dj c", p=128)),
                        16)
                self.emit_wT_transposes(wstage, dst, self.s_wld.n)
                self.set_guard("wstage", s_mm)

            if dbg:
                self.dbgx = self.sb("dbgx", [128, 2048], F32, pctx)
                self.dbg_t = dict(xrS=xrS, xkS=xkS, rS=rS, kS=kS, wrT=wrT,
                                  wkT=wkT, wvT=wvT, dbgx=self.dbgx)
            xb, xsrc = self.xbuf[1], self.xsrc[1]
            build_xt = (2 in self.blocks)
            mixvals = {}

            def emit_mix(tc):
                par = tc % 2
                moff = 4096 * par

                @blk.vector
                def _(vector, tc=tc, moff=moff, par=par):
                    self.wait_guard(vector, f"mixbuf{par}")
                    vector.wait_ge(s_v.h, s_v.n)
                    for ci in range(NCH):
                        base = XCW * ci + XO + 512 * tc
                        for cm, c1, dstS in ((mucols, mu1cols, xrS),
                                             (mkcols, mk1cols, xkS)):
                            self.vsync(vector, vector.tensor_scalar(
                                mixtmp[:], self.xT[:, base - 1:base - 1 + 512],
                                c1[:, ci:ci + 1], None, ALU.mult))
                            inst = vector.scalar_tensor_tensor(
                                dstS[:, moff + 512 * ci:moff + 512 * ci + 512],
                                self.xT[:, base:base + 512],
                                cm[:, ci:ci + 1], mixtmp[:],
                                ALU.mult, ALU.add)
                    s_v.inc(inst)
                mixvals[tc] = s_v.n

            emit_mix(0)
            for tc in range(1 if dbg else 4):
                self.cut()
                moff = 4096 * (tc % 2)
                mixval = mixvals[tc]

                for tl in range(4):
                    pa = self.PS_B1k if tl % 2 == 0 else self.PS_A1k
                    pkey = "PS_B" if tl % 2 == 0 else "PS_A"

                    @blk.tensor
                    def _(tensor, tl=tl, pa=pa, pkey=pkey, mixval=mixval,
                          moff=moff):
                        tensor.wait_ge(s_v.h, mixval)
                        self.wait_guard(tensor, pkey)
                        for ci in range(NCH):
                            for n in range(2):
                                inst = nc.tensor.matmul(
                                    pa[:, 512 * n:512 * n + 512],
                                    xrS[:, moff + 512 * ci + 128 * tl:
                                        moff + 512 * ci + 128 * tl + 128],
                                    wrT[:, 1024 * ci + 512 * n:
                                        1024 * ci + 512 * n + 512],
                                    start=(ci == 0), stop=(ci == NCH - 1))
                        s_mm.inc(inst)

                    @blk.scalar
                    def _(scalar, tl=tl, pa=pa):
                        scalar.wait_ge(s_mm.h, s_mm.n)
                        self.wait_guard(scalar, f"rSbuf{tl}")
                        s_act.inc(scalar.activation(
                            rS[:, 1024 * tl:1024 * tl + 1024], pa[:],
                            AF.Sigmoid))
                    self.set_guard(pkey, s_act)

                if tc + 1 < 4 and not dbg:
                    emit_mix(tc + 1)
                for dp in range(NCH // 2):
                    slot = 1024 * (dp % 2)

                    @blk.tensor
                    def _(tensor, dp=dp, slot=slot, mixval=mixval, moff=moff):
                        tensor.wait_ge(s_v.h, mixval)
                        self.wait_guard(tensor, f"kslot{dp % 2}")
                        if dp < 1:
                            self.wait_guard(tensor, "PS_A")
                        for j in range(2):
                            dj = 2 * dp + j
                            for ci in range(NCH):
                                inst = nc.tensor.matmul(
                                    self.PS_A[:, slot + 512 * j:
                                              slot + 512 * j + 512],
                                    wkT[:, 1024 * ci + 128 * dj:
                                        1024 * ci + 128 * dj + 128],
                                    xkS[:, moff + 512 * ci:
                                        moff + 512 * ci + 512],
                                    start=(ci == 0), stop=(ci == NCH - 1))
                        s_mm.inc(inst)

                    @blk.scalar
                    def _(scalar, dp=dp, slot=slot):
                        scalar.wait_ge(s_mm.h, s_mm.n)
                        self.wait_guard(scalar, f"kSbuf{dp}")
                        s_act.inc(scalar.activation(
                            kS[:, 1024 * dp:1024 * dp + 1024],
                            self.PS_A[:, slot:slot + 1024], AF.Relu))

                    @blk.vector
                    def _(vector, dp=dp):
                        vector.wait_ge(s_act.h, s_act.n)
                        s_v.inc(vector.tensor_mul(
                            kS[:, 1024 * dp:1024 * dp + 1024],
                            kS[:, 1024 * dp:1024 * dp + 1024],
                            kS[:, 1024 * dp:1024 * dp + 1024]))
                    self.set_guard(f"kslot{dp % 2}", s_v)
                    self.set_guard(f"kSbuf{dp}", s_v)
                kdone = s_v.n
                self.set_guard("PS_A", s_v)
                self.set_guard(f"mixbuf{tc % 2}", s_mm)

                for tl in range(4):
                    ti = 4 * tc + tl
                    dv = self.prefetch_x(xsrc, ti)
                    pa = self.PS_B1k if tl % 2 == 0 else self.PS_A1k
                    pkey = "PS_B" if tl % 2 == 0 else "PS_A"
                    self._ln_psum_key = pkey

                    @blk.tensor
                    def _(tensor, tl=tl, pa=pa, pkey=pkey, kdone=kdone):
                        tensor.wait_ge(s_v.h, kdone)
                        self.wait_guard(tensor, pkey)
                        for dj in range(NCH):
                            for n in range(2):
                                inst = nc.tensor.matmul(
                                    pa[:, 512 * n:512 * n + 512],
                                    kS[:, 512 * dj + 128 * tl:
                                       512 * dj + 128 * tl + 128],
                                    wvT[:, 1024 * dj + 512 * n:
                                        1024 * dj + 512 * n + 512],
                                    start=(dj == 0), stop=(dj == NCH - 1))
                        s_mm.inc(inst)

                    def prod_y(vector, xsl, pa=pa, tl=tl):
                        self.vsync(vector, vector.scalar_tensor_tensor(
                            self.ffntmp[:], pa[:], 1.0,
                            rS[:, 1024 * tl:1024 * tl + 1024],
                            ALU.bypass, ALU.mult))
                        self.vsync(vector, vector.tensor_add(
                            xsl, xsl, self.ffntmp[:]))

                    self.emit_ln_tile(ti, pa, xb, build_xt, s_mm.n,
                                      produce_y=prod_y, dma_val=dv)
                    self.set_guard(f"rSbuf{tl}", s_v)

    def phase_block2(self):
        blk, nc, io = self.blk, self.nc, self.io
        s_dma, s_gdma, s_mm, s_v, s_act = (self.s_dma, self.s_gdma, self.s_mm,
                                           self.s_v, self.s_act)
        with ExitStack() as pctx:
            qT = self.sb("qT", [128, NCH * T], BF16, pctx)
            kT = self.sb("kT", [128, NCH * T], BF16, pctx)
            vTok = self.sb("vTok", [128, NT * 1024], BF16, pctx)
            wT = self.sb("wT", [128, NCH * 1024], BF16, pctx)
            wstage = self.sb("wstage2", [128, NCH * 1024], BF16, pctx)
            brow = self.sb("brow", [8, 256], F32, pctx)
            bcols = self.sb("bcols", [128, 16], F32, pctx)
            bvrow = self.sb("bvrow", [1, 1024], BF16, pctx)

            @blk.sync
            def _(sync):
                for wi2, bname in ((0, "bq"), (1, "bk")):
                    s_dma.inc(sync.dma_start(
                        out=brow[:, 128 * wi2:128 * wi2 + 128],
                        in_=io[bname].rearrange("(a p) -> a p", p=128)), 16)
            brval = s_dma.n
            for wi, (wname, bname) in enumerate(
                    (("Wq", "bq"), ("Wk", "bk"), ("Wv", "bv"))):
                @blk.gpsimd
                def _(gpsimd, wname=wname):
                    self.wait_guard(gpsimd, "wstage2")
                    self.s_wld.inc(gpsimd.dma_start(
                        out=wstage[:, :].rearrange("p (dj c) -> p dj c", c=1024),
                        in_=io[wname].rearrange("(dj p) c -> p dj c", p=128)),
                        16)
                wval = self.s_wld.n

                if wi < 2:
                    @blk.tensor
                    def _(tensor, wi=wi, bval=brval):
                        tensor.wait_ge(s_dma.h, bval)
                        self.wait_guard(tensor, "PS_C")
                        s_mm.inc(nc.tensor.transpose(
                            self.PS_C[:, 8 * wi:8 * wi + 8],
                            brow[:, 128 * wi:128 * wi + 128],
                            self.ident32[0:8, 0:8]))

                    @blk.vector
                    def _(vector, wi=wi):
                        vector.wait_ge(s_mm.h, s_mm.n)
                        s_v.inc(vector.tensor_copy(
                            bcols[:, 8 * wi:8 * wi + 8],
                            self.PS_C[:, 8 * wi:8 * wi + 8]))
                    self.set_guard("PS_C", s_v)
                else:
                    @blk.gpsimd
                    def _(gpsimd):
                        s_gdma.inc(gpsimd.dma_start(
                            out=bvrow[:],
                            in_=io["bv"].rearrange("(a c) -> a c", a=1)), 16)
                    bvg = s_gdma.n

                    @blk.vector
                    def _(vector, bvg=bvg):
                        vector.wait_ge(s_gdma.h, bvg)
                        s_v.inc(vector.tensor_copy(bvrow[:], bvrow[:]))

                wtval = self.emit_wT_transposes(wstage, wT, wval,
                                                busykey="wTbusy")
                self.set_guard("wstage2", s_mm)

                if wi < 2:
                    dst = qT if wi == 0 else kT
                    for m in range(NCH):
                        @blk.tensor
                        def _(tensor, m=m, wtval=wtval):
                            tensor.wait_ge(s_v.h, wtval)
                            self.wait_guard(tensor, "PS_A")
                            for ci in range(NCH):
                                for n in range(4):
                                    inst = nc.tensor.matmul(
                                        self.PS_A[:, 512 * n:512 * n + 512],
                                        wT[:, 1024 * ci + 128 * m:
                                           1024 * ci + 128 * m + 128],
                                        self.xT[:, XCW * ci + XO + 512 * n:
                                                XCW * ci + XO + 512 * n + 512],
                                        start=(ci == 0), stop=(ci == NCH - 1))
                            s_mm.inc(inst)

                        @blk.scalar
                        def _(scalar, m=m, dst=dst, wi=wi):
                            scalar.wait_ge(s_mm.h, s_mm.n)
                            s_act.inc(scalar.activation(
                                dst[:, T * m:T * m + T], self.PS_A[:],
                                AF.Identity,
                                bias=bcols[:, 8 * wi + m:8 * wi + m + 1]))
                        self.set_guard("PS_A", s_act)
                    self.set_guard("wTbusy", s_act)
                else:
                    vready = s_v.n
                    self._emit_l1_v(wT, bvrow, vTok, vready)

            xb, xsrc = self.xbuf[2], self.xsrc[2]
            build_xt = (3 in self.blocks)
            def finish2(ctx):
                qi = ctx["qi"]
                pn = self.attn_post(ctx)
                self.emit_p_transposes(qi, pn, ctx["P_s"], ctx["pskey"])

                @blk.tensor
                def _(tensor, qi=qi):
                    tensor.wait_ge(s_v.h, s_v.n)
                    tensor.wait_ge(s_act.h, s_act.n)
                    self.wait_guard(tensor, "PS_B")
                    for si in range(qi + 1):
                        for n in range(2):
                            inst = nc.tensor.matmul(
                                self.PS_B[:, 512 * n:512 * n + 512],
                                self.PT[:, 128 * si:128 * si + 128],
                                vTok[:, 1024 * si + 512 * n:
                                     1024 * si + 512 * n + 512],
                                start=(si == 0), stop=(si == qi))
                    s_mm.inc(inst)

                self._ln_psum_key = "PS_B"
                self.emit_ln_tile(qi, self.PS_B1k, xb, build_xt, s_mm.n,
                                  dma_val=ctx["dv"])

            pend = None
            for qi in range(NT):
                if qi % 4 == 0:
                    self.cut()
                dv = self.prefetch_x(xsrc, qi)

                def emit_S(tensor, c0, cw, poff, has_diag, qi=qi):
                    for ci in range(NCH):
                        for n0 in range(0, cw, 512):
                            nw = min(512, cw - n0)
                            inst = nc.tensor.matmul(
                                self.PS_A[:, poff + n0:poff + n0 + nw],
                                qT[:, T * ci + 128 * qi:T * ci + 128 * qi + 128],
                                kT[:, T * ci + c0 + n0:T * ci + c0 + n0 + nw],
                                start=(ci == 0), stop=False)
                    if has_diag:
                        inst = nc.tensor.matmul(
                            self.PS_A[:, poff + cw - 128:poff + cw],
                            self.identbf[:], self.maskbf[:],
                            start=False, stop=True)
                    s_mm.inc(inst)

                ctx = self.attn_pre(qi, self.cols[:, 2:3], emit_S)
                ctx["dv"] = dv
                if pend is not None:
                    finish2(pend)
                pend = ctx
            finish2(pend)

    def _emit_l1_v(self, wT, bvrow, vTok, vready):
        blk, nc = self.blk, self.nc
        s_mm, s_v = self.s_mm, self.s_v
        for ti in range(NT):
            pa = self.PS_B1k if ti % 2 == 0 else self.PS_A1k
            pkey = "PS_B" if ti % 2 == 0 else "PS_A"

            @blk.tensor
            def _(tensor, ti=ti, pa=pa, pkey=pkey, vready=vready):
                tensor.wait_ge(s_v.h, vready)
                self.wait_guard(tensor, pkey)
                for n in range(2):
                    for ci in range(NCH):
                        nc.tensor.matmul(
                            pa[:, 512 * n:512 * n + 512],
                            self.xT[:, XCW * ci + XO + 128 * ti:
                                    XCW * ci + XO + 128 * ti + 128],
                            wT[:, 1024 * ci + 512 * n:1024 * ci + 512 * n + 512],
                            start=(ci == 0), stop=False)
                    inst = nc.tensor.matmul(
                        pa[:, 512 * n:512 * n + 512], self.onesbf[0:1, :],
                        bvrow[0:1, 512 * n:512 * n + 512],
                        start=False, stop=True)
                s_mm.inc(inst)

            @blk.scalar
            def _(scalar, ti=ti, pa=pa):
                scalar.wait_ge(s_mm.h, s_mm.n)
                self.s_act.inc(scalar.activation(
                    vTok[:, 1024 * ti:1024 * ti + 1024], pa[:], AF.Identity))
            self.set_guard(pkey, self.s_act)

    def phase_block3(self):
        blk, nc, io = self.blk, self.nc, self.io
        s_dma, s_gdma, s_mm, s_v, s_act = (self.s_dma, self.s_gdma, self.s_mm,
                                           self.s_v, self.s_act)
        with ExitStack() as pctx:
            wfcT = self.sb("wfcT", [128, NCH * 4096], BF16, pctx)
            qstage = self.sb("qstage", [128, 2 * 1024], F32, pctx)
            qbf = self.sb("qbf", [128, 2048], BF16, pctx)
            wps = self.sb("wps", [128, 1024], BF16, pctx)
            hS = self.sb("hS", [128, 32 * 512], BF16, pctx)
            wpsl = self.sb("wpsl", [128, 4 * 1024], BF16, pctx)
            qcols = self.sb("qcols", [128, 8], F32, pctx)
            qrow = self.sb("qrow", [1, 8], F32, pctx)
            ones32 = self.sb("ones32", [1, 128], F32, pctx)
            afT = self.sb("afT", [128, NCH * 16], BF16, pctx)
            apT = self.sb("apT", [128, 32 * 16], BF16, pctx)
            bfT = self.sb("bfT", [16, 4096], BF16, pctx)
            bpT = self.sb("bpT", [16, 1024], BF16, pctx)
            bstage = self.sb("bstage", [128, 512], BF16, pctx)
            arow = self.sb("arow", [16, 4 * 1024], BF16, pctx)
            bfcrow = self.sb("bfcrow", [32, 128], F32, pctx)
            bfccols = self.sb("bfccols", [128, 32], F32, pctx)
            bprow = self.sb("bprow", [1, 1024], BF16, pctx)
            m1T = self.sb("m1T", [16, 512], BF16, pctx)
            m2T = self.sb("m2T", [16, 512], BF16, pctx)

            @blk.vector
            def _(vector):
                s_v.inc(vector.memset(ones32[:], 1.0))

            self._quant_weight(
                "W_fc", [(128 * sj, 0, 1024) for sj in range(32)],
                wfcT, None, qstage, qbf, wps, qcols, qrow, ones32, 0)
            self._quant_weight(
                "W_proj1", [(128 * dj, 1024 * h, 1024)
                            for dj in range(8) for h in range(4)],
                None, io["wpcache"], qstage, qbf, wps, qcols, qrow, ones32, 4)

            @blk.gpsimd
            def _(gpsimd):
                s_gdma.inc(gpsimd.dma_start(out=arow[0:16, 3072:4096],
                                            in_=io["A_fc"]), 16)
                s_gdma.inc(gpsimd.dma_start(
                    out=bstage[:, :].rearrange("p (c r) -> p c r", r=16),
                    in_=io["B_fc"].rearrange("(c p) r -> p c r", p=128)), 16)
            laval = s_gdma.n

            @blk.sync
            def _(sync):
                s_dma.inc(sync.dma_start(
                    out=bfcrow[:],
                    in_=io["b_fc"].rearrange("(a p) -> a p", p=128)), 16)
            bfval = s_dma.n

            @blk.gpsimd
            def _(gpsimd):
                s_gdma.inc(gpsimd.dma_start(
                    out=bprow[:],
                    in_=io["b_proj"].rearrange("(a c) -> a c", a=1)), 16)
            bpg = s_gdma.n

            @blk.tensor
            def _(tensor):
                tensor.wait_ge(s_dma.h, bfval)
                self.wait_guard(tensor, "PS_C")
                s_mm.inc(nc.tensor.transpose(self.PS_C[:, 0:32], bfcrow[:],
                                             self.ident32[0:32, 0:32]))

            @blk.vector
            def _(vector, bpg=bpg):
                vector.wait_ge(s_mm.h, s_mm.n)
                vector.wait_ge(s_gdma.h, bpg)
                s_v.inc(vector.tensor_copy(bfccols[:], self.PS_C[:, 0:32]))
            self.set_guard("PS_C", s_v)

            for j in range(8):
                @blk.tensor
                def _(tensor, j=j, laval=laval):
                    tensor.wait_ge(s_gdma.h, laval)
                    self.wait_guard(tensor, "PS_D")
                    s_mm.inc(nc.tensor.transpose(
                        self.PS_D[:, 0:16],
                        arow[0:16, 3072 + 128 * j:3072 + 128 * j + 128],
                        self.identbf[0:16, 0:16]))

                @blk.vector
                def _(vector, j=j):
                    vector.wait_ge(s_mm.h, s_mm.n)
                    s_v.inc(vector.tensor_copy(afT[:, 16 * j:16 * j + 16],
                                               self.PS_D[:, 0:16]))
                self.set_guard("PS_D", s_v)
            afdone = s_mm.n

            @blk.gpsimd
            def _(gpsimd, afdone=afdone):
                gpsimd.wait_ge(s_mm.h, afdone)
                s_gdma.inc(gpsimd.dma_start(out=arow[0:16, :],
                                            in_=io["A_proj"]), 16)
            apval = s_gdma.n
            for j in range(32):
                @blk.tensor
                def _(tensor, j=j, apval=apval):
                    tensor.wait_ge(s_gdma.h, apval)
                    self.wait_guard(tensor, "PS_D")
                    s_mm.inc(nc.tensor.transpose(
                        self.PS_D[:, 0:16],
                        arow[0:16, 128 * j:128 * j + 128],
                        self.identbf[0:16, 0:16]))

                @blk.vector
                def _(vector, j=j):
                    vector.wait_ge(s_mm.h, s_mm.n)
                    s_v.inc(vector.tensor_copy(apT[:, 16 * j:16 * j + 16],
                                               self.PS_D[:, 0:16]))
                self.set_guard("PS_D", s_v)

            for cJ in range(32):
                @blk.tensor
                def _(tensor, cJ=cJ, laval=laval):
                    tensor.wait_ge(s_gdma.h, laval)
                    self.wait_guard(tensor, "PS_D")
                    s_mm.inc(nc.tensor.transpose(
                        self.PS_D[0:16, 0:128],
                        bstage[:, 16 * cJ:16 * cJ + 16], self.identbf[:]))

                @blk.vector
                def _(vector, cJ=cJ):
                    vector.wait_ge(s_mm.h, s_mm.n)
                    s_v.inc(vector.tensor_copy(
                        bfT[0:16, 128 * cJ:128 * cJ + 128],
                        self.PS_D[0:16, 0:128]))
                self.set_guard("PS_D", s_v)
            self.set_guard("bstage", s_mm)

            @blk.gpsimd
            def _(gpsimd):
                self.wait_guard(gpsimd, "bstage")
                s_gdma.inc(gpsimd.dma_start(
                    out=bstage[:, 0:128].rearrange("p (c r) -> p c r", r=16),
                    in_=io["B_proj"].rearrange("(c p) r -> p c r", p=128)), 16)
            bpval = s_gdma.n
            for cJ in range(8):
                @blk.tensor
                def _(tensor, cJ=cJ, bpval=bpval):
                    tensor.wait_ge(s_gdma.h, bpval)
                    self.wait_guard(tensor, "PS_D")
                    s_mm.inc(nc.tensor.transpose(
                        self.PS_D[0:16, 0:128],
                        bstage[:, 16 * cJ:16 * cJ + 16], self.identbf[:]))

                @blk.vector
                def _(vector, cJ=cJ):
                    vector.wait_ge(s_mm.h, s_mm.n)
                    s_v.inc(vector.tensor_copy(
                        bpT[0:16, 128 * cJ:128 * cJ + 128],
                        self.PS_D[0:16, 0:128]))
                self.set_guard("PS_D", s_v)

            xb, xsrc = self.xbuf[3], self.xsrc[3]
            for tc in range(4):
                self.cut()
                hvals = []
                @blk.tensor
                def _(tensor, tc=tc):
                    tensor.wait_ge(s_v.h, s_v.n)
                    self.wait_guard(tensor, "PS_C")
                    self.wait_guard(tensor, "m1T")
                    for ci in range(NCH):
                        inst = nc.tensor.matmul(
                            self.PS_C[0:16, 0:512],
                            afT[:, 16 * ci:16 * ci + 16],
                            self.xT[:, XCW * ci + XO + 512 * tc:
                                    XCW * ci + XO + 512 * tc + 512],
                            start=(ci == 0), stop=(ci == NCH - 1))
                    s_mm.inc(inst)

                @blk.vector
                def _(vector):
                    vector.wait_ge(s_mm.h, s_mm.n)
                    s_v.inc(vector.tensor_scalar(m1T[0:16, :],
                                                 self.PS_C[0:16, 0:512],
                                                 2.0, None, ALU.mult))
                self.set_guard("PS_C", s_v)
                m1val = s_v.n

                for fi in range(32):
                    slot = 512 * (fi % 4)

                    @blk.tensor
                    def _(tensor, tc=tc, fi=fi, slot=slot, m1val=m1val):
                        tensor.wait_ge(s_v.h, m1val)
                        self.wait_guard(tensor, f"hslot{fi % 4}")
                        if fi == 0:
                            self.wait_guard(tensor, "PS_A")
                            self.wait_guard(tensor, "hS")
                        for ci in range(NCH):
                            nc.tensor.matmul(
                                self.PS_A[:, slot:slot + 512],
                                wfcT[:, 4096 * ci + 128 * fi:
                                     4096 * ci + 128 * fi + 128],
                                self.xT[:, XCW * ci + XO + 512 * tc:
                                        XCW * ci + XO + 512 * tc + 512],
                                start=(ci == 0), stop=False)
                        inst = nc.tensor.matmul(
                            self.PS_A[:, slot:slot + 512],
                            bfT[0:16, 128 * fi:128 * fi + 128],
                            m1T[0:16, 0:512], start=False, stop=True)
                        s_mm.inc(inst)

                    @blk.scalar
                    def _(scalar, fi=fi, slot=slot):
                        scalar.wait_ge(s_mm.h, s_mm.n)
                        s_act.inc(scalar.activation(
                            hS[:, 512 * fi:512 * fi + 512],
                            self.PS_A[:, slot:slot + 512], AF.Gelu,
                            bias=bfccols[:, fi:fi + 1]))
                    self.set_guard(f"hslot{fi % 4}", s_act)
                    hvals.append(s_act.n)
                hval = s_act.n

                @blk.tensor
                def _(tensor, hvals=hvals):
                    self.wait_guard(tensor, "PS_C")
                    self.wait_guard(tensor, "m2T")
                    for fj in range(32):
                        tensor.wait_ge(s_act.h, hvals[fj])
                        inst = nc.tensor.matmul(
                            self.PS_C[0:16, 0:512],
                            apT[:, 16 * fj:16 * fj + 16],
                            hS[:, 512 * fj:512 * fj + 512],
                            start=(fj == 0), stop=(fj == 31))
                    s_mm.inc(inst)

                @blk.vector
                def _(vector):
                    vector.wait_ge(s_mm.h, s_mm.n)
                    s_v.inc(vector.tensor_scalar(m2T[0:16, :],
                                                 self.PS_C[0:16, 0:512],
                                                 2.0, None, ALU.mult))
                self.set_guard("PS_C", s_v)
                m2val = s_v.n

                for hc in (0, 1):
                    tls = (2 * hc, 2 * hc + 1)
                    dvs = [self.prefetch_x(xsrc, 4 * tc + tl) for tl in tls]
                    pas = [self.PS_B1k, self.PS_A1k]
                    pkeys = ["PS_B", "PS_A"]
                    mmbase = s_mm.n
                    wpv = []
                    for fj in range(32):
                        wk = fj % 4

                        sp = self.s_p[wk]

                        @blk.sync
                        def _(sync, fj=fj, wk=wk, mmbase=mmbase, sp=sp):
                            if fj >= 4:
                                sync.wait_ge(s_mm.h, mmbase + fj - 3)
                            else:
                                self.wait_guard(sync, f"wpsl{wk}")
                            sp.inc(sync.dma_start(
                                out=wpsl[:, 1024 * wk:1024 * wk + 1024],
                                in_=io["wpcache"][128 * fj:128 * fj + 128, :]),
                                16)
                        wpv.append((sp, sp.n))

                    @blk.tensor
                    def _(tensor, tc=tc, tls=tls, pas=pas, m2val=m2val,
                          hvals=hvals, wpv=wpv):
                        tensor.wait_ge(s_v.h, m2val)
                        self.wait_guard(tensor, "PS_B")
                        self.wait_guard(tensor, "PS_A")
                        for fj in range(32):
                            tensor.wait_ge(s_act.h, hvals[fj])
                            tensor.wait_ge(wpv[fj][0].h, wpv[fj][1])
                            wk = fj % 4
                            for j, tl in enumerate(tls):
                                for n in range(2):
                                    inst = nc.tensor.matmul(
                                        pas[j][:, 512 * n:512 * n + 512],
                                        hS[:, 512 * fj + 128 * tl:
                                           512 * fj + 128 * tl + 128],
                                        wpsl[:, 1024 * wk + 512 * n:
                                             1024 * wk + 512 * n + 512],
                                        start=(fj == 0), stop=False)
                            s_mm.inc(inst)
                        for j, tl in enumerate(tls):
                            for n in range(2):
                                nc.tensor.matmul(
                                    pas[j][:, 512 * n:512 * n + 512],
                                    m2T[0:16, 128 * tl:128 * tl + 128],
                                    bpT[0:16, 512 * n:512 * n + 512],
                                    start=False, stop=False)
                                inst = nc.tensor.matmul(
                                    pas[j][:, 512 * n:512 * n + 512],
                                    self.onesbf[0:1, :],
                                    bprow[0:1, 512 * n:512 * n + 512],
                                    start=False, stop=True)
                        s_mm.inc(inst)
                    for wk2 in range(4):
                        self.set_guard(f"wpsl{wk2}", s_mm)
                    mmv = s_mm.n

                    for j, tl in enumerate(tls):
                        self._ln_psum_key = pkeys[j]
                        self.emit_ln_tile(4 * tc + tl, pas[j], xb, False, mmv,
                                          dma_val=dvs[j])
                self.set_guard("m1T", s_mm)
                self.set_guard("m2T", s_mm)
                self.set_guard("hS", s_mm)

    def _quant_weight(self, wname, slices, wT_dst, cache_dst, qstage,
                      qbf, wps, qcols, qrow, ones32, qc0):
        blk, nc, io = self.blk, self.nc, self.io
        s_dma, s_mm, s_v = self.s_dma, self.s_mm, self.s_v
        Wap = io[wname]
        O, I = Wap.shape

        @blk.vector
        def _(vector):
            s_v.inc(vector.memset(qcols[:, qc0:qc0 + 1], 0.0))
        for si_, (r0, c0, ncl) in enumerate(slices):
            qs = qstage[:, 1024 * (si_ % 2):1024 * (si_ % 2) + ncl]

            sq = self.s_p[2 + si_ % 2]

            @blk.sync
            def _(sync, r0=r0, c0=c0, ncl=ncl, qs=qs, si_=si_, sq=sq):
                self.wait_guard(sync, f"qstage{si_ % 2}")
                sq.inc(sync.dma_start(
                    out=qs, in_=Wap[r0:r0 + 128, c0:c0 + ncl]), 16)
            dval = sq.n

            @blk.vector
            def _(vector, dval=dval, qs=qs, sq=sq):
                vector.wait_ge(sq.h, dval)
                self.vsync(vector, vector.reduce_max(
                    qcols[:, qc0 + 1:qc0 + 2], qs,
                    axis=AX.X, apply_absolute_value=True))
                s_v.inc(vector.tensor_max(qcols[:, qc0:qc0 + 1],
                                          qcols[:, qc0:qc0 + 1],
                                          qcols[:, qc0 + 1:qc0 + 2]))
            self.set_guard(f"qstage{si_ % 2}", s_v)

        @blk.tensor
        def _(tensor):
            tensor.wait_ge(s_v.h, s_v.n)
            self.wait_guard(tensor, "PS_C")
            s_mm.inc(nc.tensor.transpose(self.PS_C[0:1, 0:128],
                                         qcols[:, qc0:qc0 + 1],
                                         self.ident32[:]))

        @blk.vector
        def _(vector):
            vector.wait_ge(s_mm.h, s_mm.n)
            self.vsync(vector, vector.reduce_max(qrow[0:1, 0:1],
                                                 self.PS_C[0:1, 0:128],
                                                 axis=AX.X))
            vector.tensor_scalar(qrow[0:1, 1:2], qrow[0:1, 0:1],
                                 1.0 / 127.0, None, ALU.mult)
            self.vsync(vector, vector.reciprocal(qrow[0:1, 2:3],
                                                 qrow[0:1, 0:1]))
            s_v.inc(vector.tensor_scalar(qrow[0:1, 2:3], qrow[0:1, 2:3],
                                         127.0, None, ALU.mult))
        self.set_guard("PS_C", s_v)

        @blk.tensor
        def _(tensor):
            tensor.wait_ge(s_v.h, s_v.n)
            self.wait_guard(tensor, "PS_C")
            s_mm.inc(nc.tensor.matmul(self.PS_C[:, 0:2], ones32[0:1, :],
                                      qrow[0:1, 1:3], start=True, stop=True))

        @blk.vector
        def _(vector):
            vector.wait_ge(s_mm.h, s_mm.n)
            s_v.inc(vector.tensor_copy(qcols[:, qc0 + 2:qc0 + 4],
                                       self.PS_C[:, 0:2]))
        self.set_guard("PS_C", s_v)
        scol = qcols[:, qc0 + 2:qc0 + 3]
        rcol = qcols[:, qc0 + 3:qc0 + 4]

        for si_, (r0, c0, ncl) in enumerate(slices):
            qs = qstage[:, 1024 * (si_ % 2):1024 * (si_ % 2) + ncl]
            sq = self.s_p[2 + si_ % 2]

            @blk.sync
            def _(sync, r0=r0, c0=c0, ncl=ncl, qs=qs, si_=si_, sq=sq):
                self.wait_guard(sync, f"qstage{si_ % 2}")
                sq.inc(sync.dma_start(
                    out=qs, in_=Wap[r0:r0 + 128, c0:c0 + ncl]), 16)
            dval = sq.n

            @blk.vector
            def _(vector, dval=dval, qs=qs, ncl=ncl, sq=sq):
                vector.wait_ge(sq.h, dval)
                self.wait_guard(vector, "qbf")
                self.vsync(vector, vector.tensor_scalar(
                    qs, qs, rcol, MAGIC, ALU.mult, ALU.add))
                self.vsync(vector, vector.tensor_scalar(
                    qs, qs, MAGIC, -128.0, ALU.subtract, ALU.max))
                s_v.inc(vector.tensor_scalar(qbf[:, 0:ncl], qs,
                                             127.0, scol, ALU.min, ALU.mult))
            self.set_guard(f"qstage{si_ % 2}", s_v)
            qval = s_v.n

            nI = ncl // 128
            for w0 in range(0, nI, 8):
                w1 = min(w0 + 8, nI)

                @blk.tensor
                def _(tensor, w0=w0, w1=w1, qval=qval):
                    tensor.wait_ge(s_v.h, qval)
                    self.wait_guard(tensor, "PS_D")
                    for j in range(w0, w1):
                        inst = nc.tensor.transpose(
                            self.PS_D[:, 128 * (j - w0):128 * (j - w0) + 128],
                            qbf[:, 128 * j:128 * j + 128], self.identbf[:])
                    s_mm.inc(inst)

                if wT_dst is not None:
                    @blk.vector
                    def _(vector, r0=r0, w0=w0, w1=w1):
                        vector.wait_ge(s_mm.h, s_mm.n)
                        dv_ = wT_dst[:, :].rearrange(
                            "p (ci f) -> p ci f", f=O)[
                            :, w0:w1, r0:r0 + 128]
                        sv_ = self.PS_D[:, 0:128 * (w1 - w0)].rearrange(
                            "p (kk c) -> p kk c", c=128)
                        s_v.inc(vector.tensor_copy(dv_, sv_))
                    self.set_guard("PS_D", s_v)
                else:
                    @blk.vector
                    def _(vector, w0=w0, w1=w1):
                        vector.wait_ge(s_mm.h, s_mm.n)
                        s_v.inc(vector.tensor_copy(
                            wps[:, 128 * w0:128 * w1],
                            self.PS_D[:, 0:128 * (w1 - w0)]))
                    self.set_guard("PS_D", s_v)
            if cache_dst is not None:
                wval = s_v.n

                @blk.sync
                def _(sync, r0=r0, c0=c0, nI=nI, wval=wval):
                    sync.wait_ge(s_v.h, wval)
                    f0 = c0 // 128
                    dst = cache_dst.rearrange("(fj p) d -> p fj d", p=128)[
                        :, f0:f0 + nI, r0:r0 + 128]
                    src = wps[:, 0:128 * nI].rearrange("p (fj d) -> p fj d",
                                                       d=128)
                    self.s_out.inc(sync.dma_start(out=dst, in_=src), 16)
                self.set_guard("qbf", self.s_out)
            else:
                self.set_guard("qbf", self.s_mm)


_consts_cache = None


def _consts():
    global _consts_cache
    if _consts_cache is None:
        ident32 = np.eye(128, dtype=np.float32)
        identbf = np.eye(128).astype(ml_dtypes.bfloat16)
        mask = np.zeros((128, 128), np.float32)
        for p in range(128):
            mask[p, p + 1:] = -1e30
        _consts_cache = dict(
            ident32=ident32, identbf=identbf,
            maskbf=mask.astype(ml_dtypes.bfloat16),
            onesbf=np.ones((1, 128), ml_dtypes.bfloat16))
    return _consts_cache


_BLOCK_INPUTS = {
    0: {"W_attn": "L0_attn_W_attn", "W_proj0": "L0_attn_W_proj"},
    1: {"W_r": "L0_ffn_W_r", "W_k": "L0_ffn_W_k", "W_v": "L0_ffn_W_v",
        "mu_r": "L0_ffn_mu_r", "mu_k": "L0_ffn_mu_k"},
    2: {"Wq": "L1_attn_Wq", "Wk": "L1_attn_Wk", "Wv": "L1_attn_Wv",
        "bq": "L1_attn_bq", "bk": "L1_attn_bk", "bv": "L1_attn_bv"},
    3: {"W_fc": "L1_ffn_W_fc", "b_fc": "L1_ffn_b_fc", "A_fc": "L1_ffn_A_fc",
        "B_fc": "L1_ffn_B_fc", "W_proj1": "L1_ffn_W_proj",
        "b_proj": "L1_ffn_b_proj", "A_proj": "L1_ffn_A_proj",
        "B_proj": "L1_ffn_B_proj"},
}

_prog_cache = {}


def _get_prog(blocks):
    key = tuple(blocks)
    if key not in _prog_cache:
        _prog_cache[key] = B(list(blocks)).build()
    return _prog_cache[key]


def run_blocks(blocks, x, inputs):
    nc = _get_prog(blocks)
    cns = _consts()
    maps = []
    for c in range(NCORE):
        m = dict(x_in=np.ascontiguousarray(x[c]))
        m.update(cns)
        for b in blocks:
            for kk, src in _BLOCK_INPUTS[b].items():
                m[kk] = np.ascontiguousarray(inputs[src])
        maps.append(m)
    res = run_bass_kernel_spmd(nc, maps, core_ids=list(range(NCORE)))
    return np.stack([np.asarray(r["y_out"]) for r in res.results], 0)


def kernel(**inputs):
    x = np.ascontiguousarray(inputs["L0_attn_x"], np.float32)
    y = run_blocks((0, 1, 2, 3), x, inputs)
    return y.astype(np.float32)


# version 19 (final)
# speedup vs baseline: 1.0142x; 1.0142x over previous
import numpy as np
import ml_dtypes
from contextlib import ExitStack
import concourse.bass as bass
import concourse.mybir as mybir
from concourse.bass_utils import run_bass_kernel_spmd

F32 = mybir.dt.float32
BF16 = mybir.dt.bfloat16
AF = mybir.ActivationFunctionType
ALU = mybir.AluOpType
AX = mybir.AxisListType

T, C = 2048, 1024
NT = T // 128
NCH = C // 128
XO = 4
XCW = T + XO
MAGIC = 12582912.0
NCORE = 8


class Ctr:
    __slots__ = ("h", "n")

    def __init__(self, h):
        self.h = h
        self.n = 0

    def inc(self, inst, k=1):
        inst.then_inc(self.h, k)
        self.n += k
        return inst




class _Rec:
    __slots__ = ("meth", "args", "kwargs", "incs")

    def __init__(self, meth, args, kwargs):
        self.meth = meth
        self.args = args
        self.kwargs = kwargs
        self.incs = []

    def then_inc(self, h, k):
        self.incs.append((h, k))
        return self


class _ShimEngine:
    def __init__(self, recs):
        self._recs = recs

    def wait_ge(self, h, v):
        r = _Rec("wait_ge", (h, v), {})
        self._recs.append(r)
        return r

    def __getattr__(self, meth):
        def call(*a, **k):
            r = _Rec(meth, a, k)
            self._recs.append(r)
            return r
        return call


class _FakeBlk:

    def __init__(self, shims):
        for name, sh in shims.items():
            setattr(self, name, (lambda sh: (lambda fn: fn(sh)))(sh))


class _ProxyNC:

    def __init__(self, real, shims):
        object.__setattr__(self, "_real", real)
        object.__setattr__(self, "_shims", shims)

    def __getattr__(self, name):
        sh = self._shims.get(name)
        if sh is not None:
            return sh
        return getattr(self._real, name)


class B:
    def __init__(self, blocks):
        self.blocks = sorted(blocks)

    def build(self):
        nc = self.nc = bass.Bass("TRN2", target_bir_lowering=False)
        io = self.io = {}

        def dram(name, shape, dt=F32, kind="ExternalInput"):
            io[name] = nc.dram_tensor(name, shape, dt, kind=kind).ap()

        dram("x_in", [T, C])
        dram("y_out", [T, C], kind="ExternalOutput")
        dram("ident32", [128, 128])
        dram("identbf", [128, 128], BF16)
        dram("maskbf", [128, 128], BF16)
        dram("onesbf", [1, 128], BF16)
        nblk = len(self.blocks)
        self.xbuf, self.xsrc = {}, {}
        prev = io["x_in"]
        for i, b in enumerate(self.blocks):
            self.xsrc[b] = prev
            if i == nblk - 1:
                self.xbuf[b] = io["y_out"]
            else:
                dram(f"xb{b}", [T, C], kind="Internal")
                self.xbuf[b] = io[f"xb{b}"]
            prev = self.xbuf[b]
        if 0 in self.blocks:
            dram("W_attn", [C, 3 * C])
            dram("W_proj0", [C, C])
        if 1 in self.blocks:
            dram("W_r", [C, C]); dram("W_k", [C, C]); dram("W_v", [C, C])
            dram("mu_r", [C]); dram("mu_k", [C])
        if 2 in self.blocks:
            dram("Wq", [C, C]); dram("Wk", [C, C]); dram("Wv", [C, C])
            dram("bq", [C]); dram("bk", [C]); dram("bv", [C])
        if 3 in self.blocks:
            dram("W_fc", [4 * C, C]); dram("b_fc", [4 * C])
            dram("A_fc", [16, C]); dram("B_fc", [4 * C, 16])
            dram("W_proj1", [C, 4 * C]); dram("b_proj", [C])
            dram("A_proj", [16, 4 * C]); dram("B_proj", [C, 16])
            dram("wpcache", [4 * C, C], BF16, kind="Internal")

        with ExitStack() as ctx:
            self.ctx = ctx
            self.guard = {}
            self.alloc_common()
            self.prologue()
            engs = ("tensor", "vector", "scalar", "gpsimd", "sync")
            self.chunks = []
            self._lists = {e: [] for e in engs}
            shims = {e: _ShimEngine(self._lists[e]) for e in engs}
            self._engs = engs
            self._shims = shims
            self.cut()
            realnc = self.nc
            self.nc = _ProxyNC(realnc, shims)
            self.blk = _FakeBlk(shims)
            self.load_consts()
            self.phase_build_xt(self.xsrc[self.blocks[0]])
            for b in self.blocks:
                self.barrier()
                self.cut()
                getattr(self, f"phase_block{b}")()
            self.barrier()
            self.epilogue()
            self.cut()
            self.nc = realnc
            with realnc.Block() as blk:
                for chunk in self.chunks:
                    for e in engs:
                        if not chunk[e]:
                            continue

                        def body(eng, recs=chunk[e]):
                            for r in recs:
                                if r.meth == "wait_ge":
                                    eng.wait_ge(*r.args)
                                else:
                                    inst = getattr(eng, r.meth)(*r.args,
                                                                **r.kwargs)
                                    for h, k in r.incs:
                                        inst.then_inc(h, k)
                        getattr(blk, e)(body)
        return nc

    def cut(self):
        chunk = {}
        for e in self._engs:
            lst = self._lists[e]
            chunk[e] = lst
            self._lists[e] = []
            self._shims[e]._recs = self._lists[e]
        if any(chunk.values()):
            self.chunks.append(chunk)

    def sb(self, name, shape, dt=F32, ctx=None):
        return (ctx or self.ctx).enter_context(
            self.nc.sbuf_tensor("sb_" + name, shape, dt))

    def ps(self, name, shape, dt=F32):
        return self.ctx.enter_context(self.nc.psum_tensor(name, shape, dt))

    def sem(self, name):
        return Ctr(self.ctx.enter_context(self.nc.semaphore(name)))

    def alloc_common(self):
        self.s_dma = self.sem("s_dma")
        self.s_gdma = self.sem("s_gdma")
        self.s_mm = self.sem("s_mm")
        self.s_v = self.sem("s_v")
        self.s_act = self.sem("s_act")
        self.s_out = self.sem("s_out")
        self.s_x = [self.sem("s_x0"), self.sem("s_x1")]
        self.s_w = [self.sem("s_w0"), self.sem("s_w1")]
        self.s_wld = self.sem("s_wld")
        self.s_p = [self.sem("s_p0"), self.sem("s_p1"),
                    self.sem("s_p2"), self.sem("s_p3")]
        self.s_pool = self.sem("s_pool")
        self.all_sems = [self.s_dma, self.s_gdma, self.s_mm, self.s_v,
                         self.s_act, self.s_out, self.s_wld,
                         self.s_x[0], self.s_x[1], self.s_w[0], self.s_w[1],
                         self.s_p[0], self.s_p[1], self.s_p[2], self.s_p[3],
                         self.s_pool]
        self.ident32 = self.sb("ident32", [128, 128])
        self.identbf = self.sb("identbf", [128, 128], BF16)
        self.maskbf = self.sb("maskbf", [128, 128], BF16)
        self.onesbf = self.sb("onesbf", [1, 128], BF16)
        self.cols = self.sb("cols", [128, 8])
        self.lncols = self.sb("lncols", [128, 48])
        self.atcols = self.sb("atcols", [128, 24])
        self.attn_it = 0
        self.xstage = self.sb("xstage", [128, 2 * C])
        self.lnout = self.sb("lnout", [128, C])
        self.sqtmp = self.sb("sqtmp", [128, C], BF16)
        self.ffntmp = self.sb("ffntmp", [128, C])
        self.xT = self.sb("xT", [128, NCH * XCW], BF16)
        if 0 in self.blocks or 2 in self.blocks:
            self.P_s2 = [self.sb("P_s", [128, T], BF16),
                         self.sb("P_sb", [128, T], BF16)]
            self.PT = self.sb("PT", [128, T], BF16)
        self.PS_A = self.ps("PS_A", [128, 2048])
        self.PS_B = self.ps("PS_B", [128, 1024])
        self.PS_C = self.ps("PS_C", [128, 512])
        self.PS_D = self.ps("PS_D", [128, 1024], BF16)
        self.PS_A1k = self.PS_A[:, 0:1024]
        self.PS_B1k = self.PS_B[:, 0:1024]

    def prologue(self):
        for s in self.all_sems:
            self.nc.gpsimd.sem_clear(s.h)
        self.nc.all_engine_barrier()

    def epilogue(self):
        s_out = self.s_out

        @self.blk.sync
        def _(sync):
            sync.wait_ge(s_out.h, s_out.n)

    def barrier(self):
        blk = self.blk
        vals = [(c.h, c.n) for c in self.all_sems]
        for eng in ("tensor", "vector", "scalar", "sync", "gpsimd"):
            def mk(e, vals=vals):
                for h, v in vals:
                    e.wait_ge(h, v)
            getattr(blk, eng)(mk)

    def vsync(self, vector, inst):
        self.s_v.inc(inst)
        vector.wait_ge(self.s_v.h, self.s_v.n)

    def vpad(self, vector):
        return vector.tensor_copy(self.cols[:, 4:5], self.cols[:, 3:4])

    def spad(self, scalar):
        return scalar.copy(self.cols[:, 5:6], self.cols[:, 3:4])

    _PSA_KEYS = ("PS_A", "PSAh0", "PSAh1")

    def wait_guard(self, eng, key):
        keys = self._PSA_KEYS if key in self._PSA_KEYS else (key,)
        for kk2 in keys:
            g = self.guard.get(kk2)
            if g is not None:
                eng.wait_ge(g[0].h, g[1])

    def set_guard(self, key, ctr):
        self.guard[key] = (ctr, ctr.n)

    def load_consts(self):
        blk, io = self.blk, self.io
        s_dma, s_v = self.s_dma, self.s_v

        @blk.sync
        def _(sync):
            s_dma.inc(sync.dma_start(out=self.ident32[:], in_=io["ident32"]), 16)
            s_dma.inc(sync.dma_start(out=self.identbf[:], in_=io["identbf"]), 16)
            s_dma.inc(sync.dma_start(out=self.maskbf[:], in_=io["maskbf"]), 16)
            s_dma.inc(sync.dma_start(out=self.onesbf[:], in_=io["onesbf"]), 16)

        @blk.vector
        def _(vector):
            vector.memset(self.cols[:, 0:1], 1e-5)
            vector.memset(self.cols[:, 1:2], 0.125)
            vector.memset(self.cols[:, 2:3], 0.03125)
            vector.memset(self.cols[:, 6:7], -0.5)
            vector.memset(self.lncols[:], 0.0)
            vector.memset(self.atcols[:], 0.0)
            for ci in range(NCH):
                vector.memset(self.xT[:, XCW * ci:XCW * ci + XO], 0.0)
            if 0 in self.blocks or 2 in self.blocks:
                vector.memset(self.P_s2[0][:, 0:2], 0.0)
            s_v.inc(vector.memset(self.cols[:, 3:4], 1.0))

    def xtv(self):
        return self.xT[:, :].rearrange("p (ci w) -> p ci w", w=XCW)

    def emit_xt_tile(self, src2d, ti, pre=()):
        blk, nc = self.blk, self.nc
        s_mm, s_v = self.s_mm, self.s_v
        for half in (0, 1):
            @blk.tensor
            def _(tensor, half=half):
                if half == 0:
                    for cc, vv in pre:
                        tensor.wait_ge(cc.h, vv)
                self.wait_guard(tensor, "PS_C")
                for k in range(4):
                    ci = 4 * half + k
                    inst = nc.tensor.transpose(
                        self.PS_C[:, 128 * k:128 * k + 128],
                        src2d[:, 128 * ci:128 * ci + 128], self.ident32[:])
                s_mm.inc(inst)

            @blk.vector
            def _(vector, half=half):
                vector.wait_ge(s_mm.h, s_mm.n)
                dst = self.xtv()[:, 4 * half:4 * half + 4,
                                 XO + 128 * ti:XO + 128 * ti + 128]
                src = self.PS_C[:, :].rearrange("p (k t) -> p k t", t=128)
                s_v.inc(vector.tensor_copy(dst, src))
            self.set_guard("PS_C", s_v)

    def phase_build_xt(self, xsrc):
        cval = self.s_dma.n
        for ti in range(NT):
            sx, dv = self.prefetch_x(xsrc, ti)
            k = ti % 2
            self.emit_xt_tile(self.xstage[:, C * k:C * k + C], ti,
                              pre=[(sx, dv), (self.s_dma, cval)])
            self.set_guard(f"xst{k}", self.s_mm)

    def prefetch_x(self, xsrc, ti):
        blk = self.blk
        k = ti % 2
        sx = self.s_x[k]

        @blk.sync
        def _(sync):
            self.wait_guard(sync, f"xst{k}")
            sx.inc(sync.dma_start(
                out=self.xstage[:, C * k:C * k + C],
                in_=xsrc[128 * ti:128 * ti + 128, :]), 16)
        return (sx, sx.n)

    def emit_ln_tile(self, ti, psum_ap, out_hbm, build_xt, mm_val,
                     produce_y=None, dma_val=None):
        blk = self.blk
        s_v, s_act, s_dma, s_out = self.s_v, self.s_act, self.s_dma, self.s_out
        xsl = self.xstage[:, C * (ti % 2):C * (ti % 2) + C]
        lc = 12 * (ti % 4)
        L = self.lncols

        @blk.vector
        def _(vector):
            vector.wait_ge(dma_val[0].h, dma_val[1])
            vector.wait_ge(self.s_mm.h, mm_val)
            if produce_y is None:
                self.vsync(vector, vector.tensor_add(xsl, xsl, psum_ap))
            else:
                produce_y(vector, xsl)
            s_v.inc(vector.reduce_sum(L[:, lc:lc + 1], xsl, axis=AX.X))
        self.set_guard(self._ln_psum_key, s_v)

        @blk.scalar
        def _(scalar):
            scalar.wait_ge(s_v.h, s_v.n)
            self.wait_guard(scalar, "ffntmp")
            s_act.inc(scalar.activation(self.ffntmp[:], xsl, AF.Square))

        @blk.vector
        def _(vector):
            vector.wait_ge(s_act.h, s_act.n)
            self.vsync(vector, vector.reduce_sum(L[:, lc + 1:lc + 2],
                                                 self.ffntmp[:], axis=AX.X))
            vector.tensor_scalar(L[:, lc + 2:lc + 3], L[:, lc:lc + 1],
                                 1.0 / C, None, ALU.mult)
            self.vsync(vector, vector.tensor_scalar(
                L[:, lc + 3:lc + 4], L[:, lc + 1:lc + 2],
                1.0 / C, None, ALU.mult))
            self.vsync(vector, vector.tensor_scalar(
                L[:, lc + 4:lc + 5], L[:, lc + 2:lc + 3],
                L[:, lc + 2:lc + 3], None, ALU.mult))
            s_v.inc(vector.tensor_sub(L[:, lc + 5:lc + 6], L[:, lc + 3:lc + 4],
                                      L[:, lc + 4:lc + 5]))
        self.set_guard("ffntmp", s_v)

        @blk.scalar
        def _(scalar):
            scalar.wait_ge(s_v.h, s_v.n)
            s_act.inc(scalar.activation(L[:, lc + 6:lc + 7],
                                        L[:, lc + 5:lc + 6],
                                        AF.Ln, bias=self.cols[:, 0:1]))
            scalar.wait_ge(s_act.h, s_act.n)
            s_act.inc(scalar.activation(L[:, lc + 7:lc + 8],
                                        L[:, lc + 6:lc + 7],
                                        AF.Exp, scale=self.cols[:, 6:7]))

        @blk.vector
        def _(vector):
            vector.wait_ge(s_act.h, s_act.n)
            s_v.inc(vector.tensor_scalar(
                L[:, lc + 8:lc + 9], L[:, lc + 2:lc + 3],
                L[:, lc + 7:lc + 8], -1.0, ALU.mult, ALU.mult))

        @blk.scalar
        def _(scalar):
            scalar.wait_ge(s_v.h, s_v.n)
            self.wait_guard(scalar, "lnout_mm")
            self.wait_guard(scalar, "lnout_out")
            s_act.inc(scalar.activation(self.lnout[:], xsl, AF.Identity,
                                        scale=L[:, lc + 7:lc + 8],
                                        bias=L[:, lc + 8:lc + 9]))
        self.set_guard(f"xst{ti % 2}", s_act)
        act_val = s_act.n

        @blk.sync
        def _(sync):
            sync.wait_ge(s_act.h, act_val)
            s_out.inc(sync.dma_start(
                out=out_hbm[128 * ti:128 * ti + 128, :], in_=self.lnout[:]), 16)
        self.set_guard("lnout_out", s_out)
        if build_xt:
            self.emit_xt_tile(self.lnout[:], ti, pre=[(s_act, act_val)])
            self.set_guard("lnout_mm", self.s_mm)

    def attn_pre(self, qi, scale_col, emit_S):
        blk = self.blk
        s_mm, s_v, s_act = self.s_mm, self.s_v, self.s_act
        W = (qi + 1) * 128
        halves = [(0, min(W, 1024))]
        if W > 1024:
            halves.append((1024, W - 1024))
        a0 = 12 * (self.attn_it % 2)
        P_s = self.P_s2[self.attn_it % 2]
        pskey = f"P_s{self.attn_it % 2}"
        self.attn_it += 1
        A = self.atcols
        for sh, (c0, cw) in enumerate(halves):
            it2 = self.attn_it * 2 + sh
            poff = 1024 * (it2 % 2)
            pk = f"PSAh{it2 % 2}"

            @blk.tensor
            def _(tensor, c0=c0, cw=cw, poff=poff, pk=pk, W=W):
                self.wait_guard(tensor, pk)
                emit_S(tensor, c0, cw, poff, c0 + cw == W)

            @blk.vector
            def _(vector, sh=sh, poff=poff, cw=cw):
                vector.wait_ge(s_mm.h, s_mm.n)
                self.vsync(vector, vector.reduce_max(
                    A[:, a0 + sh:a0 + sh + 1],
                    self.PS_A[:, poff:poff + cw], axis=AX.X))
                s_v.inc(vector.tensor_scalar(
                    A[:, a0 + 2 + sh:a0 + 3 + sh], A[:, a0 + sh:a0 + sh + 1],
                    scale_col, -1.0, ALU.mult, ALU.mult))

            @blk.scalar
            def _(scalar, sh=sh, c0=c0, cw=cw, poff=poff, P_s=P_s, pskey=pskey):
                scalar.wait_ge(s_v.h, s_v.n)
                if sh == 0:
                    self.wait_guard(scalar, pskey)
                s_act.inc(scalar.activation(
                    P_s[:, c0:c0 + cw], self.PS_A[:, poff:poff + cw],
                    AF.Exp, scale=scale_col, bias=A[:, a0 + 2 + sh:a0 + 3 + sh]))
            self.set_guard(pk, s_act)
        return dict(qi=qi, W=W, halves=halves, a0=a0, P_s=P_s, pskey=pskey,
                    scale_col=scale_col, exp_val=s_act.n)

    def attn_post(self, ctx):
        blk = self.blk
        s_v, s_act, s_pool = self.s_v, self.s_act, self.s_pool
        qi, W, halves, a0 = ctx["qi"], ctx["W"], ctx["halves"], ctx["a0"]
        P_s, scale_col = ctx["P_s"], ctx["scale_col"]
        A = self.atcols

        @blk.vector
        def _(vector):
            vector.wait_ge(s_act.h, ctx["exp_val"])
            for sh, (c0, cw) in enumerate(halves):
                self.vsync(vector, vector.reduce_sum(
                    A[:, a0 + 4 + sh:a0 + 5 + sh],
                    P_s[:, c0:c0 + cw], axis=AX.X))
            if len(halves) == 1:
                self.vsync(vector, vector.reciprocal(A[:, a0 + 6:a0 + 7],
                                                     A[:, a0 + 4:a0 + 5]))
                s_v.inc(vector.tensor_scalar(
                    P_s[:, 0:W], P_s[:, 0:W],
                    A[:, a0 + 6:a0 + 7], None, ALU.mult))
            else:
                self.vsync(vector, vector.tensor_max(
                    A[:, a0 + 6:a0 + 7], A[:, a0:a0 + 1], A[:, a0 + 1:a0 + 2]))
                s_v.inc(vector.tensor_scalar(
                    A[:, a0 + 7:a0 + 9], A[:, a0:a0 + 2],
                    A[:, a0 + 6:a0 + 7], scale_col, ALU.subtract, ALU.mult))
        if len(halves) > 1:
            @blk.scalar
            def _(scalar):
                scalar.wait_ge(s_v.h, s_v.n)
                s_act.inc(scalar.activation(A[:, a0 + 9:a0 + 11],
                                            A[:, a0 + 7:a0 + 9], AF.Exp))

            @blk.vector
            def _(vector):
                vector.wait_ge(s_act.h, s_act.n)
                self.vsync(vector, vector.tensor_mul(
                    A[:, a0 + 7:a0 + 9], A[:, a0 + 9:a0 + 11],
                    A[:, a0 + 4:a0 + 6]))
                self.vsync(vector, vector.tensor_add(
                    A[:, a0 + 11:a0 + 12], A[:, a0 + 7:a0 + 8],
                    A[:, a0 + 8:a0 + 9]))
                self.vsync(vector, vector.reciprocal(A[:, a0 + 6:a0 + 7],
                                                     A[:, a0 + 11:a0 + 12]))
                self.vsync(vector, vector.tensor_scalar(
                    A[:, a0 + 2:a0 + 4], A[:, a0 + 9:a0 + 11],
                    A[:, a0 + 6:a0 + 7], None, ALU.mult))
                vector.tensor_scalar(P_s[:, 0:1024], P_s[:, 0:1024],
                                     A[:, a0 + 2:a0 + 3], None, ALU.mult)
                s_v.inc(vector.tensor_scalar(
                    P_s[:, 1024:W], P_s[:, 1024:W],
                    A[:, a0 + 3:a0 + 4], None, ALU.mult))
        return s_v.n

    def emit_p_transposes(self, qi, pnorm_val, P_s=None, pskey="P_s"):
        if P_s is None:
            P_s = self.P_s2[0]
        blk, nc = self.blk, self.nc
        s_mm, s_v = self.s_mm, self.s_v
        for w0 in range(0, qi + 1, 8):
            w1 = min(w0 + 8, qi + 1)

            @blk.tensor
            def _(tensor, w0=w0, w1=w1):
                tensor.wait_ge(s_v.h, pnorm_val)
                self.wait_guard(tensor, "PS_D")
                for si in range(w0, w1):
                    inst = nc.tensor.transpose(
                        self.PS_D[:, 128 * (si - w0):128 * (si - w0) + 128],
                        P_s[:, 128 * si:128 * si + 128], self.identbf[:])
                s_mm.inc(inst)

            @blk.scalar
            def _(scalar, w0=w0, w1=w1):
                scalar.wait_ge(self.s_mm.h, self.s_mm.n)
                self.s_act.inc(scalar.activation(
                    self.PT[:, 128 * w0:128 * w1],
                    self.PS_D[:, 0:128 * (w1 - w0)], AF.Identity))
            self.set_guard("PS_D", self.s_act)
        self.set_guard(pskey, self.s_act)
        return ("act", self.s_act.n)

    def emit_wT_transposes(self, wstage, dst, wval, busykey=None):
        blk, nc = self.blk, self.nc
        s_mm, s_v = self.s_mm, self.s_v
        for dj in range(NCH):
            @blk.tensor
            def _(tensor, dj=dj):
                tensor.wait_ge(self.s_wld.h, wval)
                self.wait_guard(tensor, "PS_D")
                if dj == 0 and busykey:
                    self.wait_guard(tensor, busykey)
                for ci in range(NCH):
                    inst = nc.tensor.transpose(
                        self.PS_D[:, 128 * ci:128 * ci + 128],
                        wstage[:, 1024 * dj + 128 * ci:
                               1024 * dj + 128 * ci + 128],
                        self.identbf[:])
                s_mm.inc(inst)

            @blk.vector
            def _(vector, dj=dj):
                vector.wait_ge(s_mm.h, s_mm.n)
                dv_ = dst[:, :].rearrange("p (ci d) -> p ci d", d=1024)[
                    :, :, 128 * dj:128 * dj + 128]
                sv_ = self.PS_D[:, :].rearrange("p (ci c) -> p ci c", c=128)
                s_v.inc(vector.tensor_copy(dv_, sv_))
            self.set_guard("PS_D", s_v)
        return s_v.n

    def phase_block0(self):
        blk, nc, io = self.blk, self.nc, self.io
        s_dma, s_gdma, s_mm, s_v, s_act = (self.s_dma, self.s_gdma, self.s_mm,
                                           self.s_v, self.s_act)
        with ExitStack() as pctx:
            qTp = [self.sb(f"qTp{k}", [128, T], BF16, pctx) for k in (0, 1)]
            kTp = [self.sb(f"kTp{k}", [128, T], BF16, pctx) for k in (0, 1)]
            vTp = [self.sb(f"vTp{k}", [128, T], BF16, pctx) for k in (0, 1)]
            wAq = [self.sb(f"wAq{k}", [128, NCH * 128], BF16, pctx)
                   for k in (0, 1)]
            wAk = [self.sb(f"wAk{k}", [128, NCH * 128], BF16, pctx)
                   for k in (0, 1)]
            wAv = [self.sb(f"wAv{k}", [128, NCH * 128], BF16, pctx)
                   for k in (0, 1)]
            oT = self.sb("oT", [128, NCH * T], BF16, pctx)
            wP = self.sb("wP", [128, NCH * 1024], BF16, pctx)

            WA = io["W_attn"].rearrange("(ci p) e -> p ci e", p=128)
            for p in range(8):
                self.cut()
                k = p % 2

                sw = self.s_w[k]

                @blk.gpsimd
                def _(gpsimd, p=p, k=k, sw=sw):
                    self.wait_guard(gpsimd, f"wA{k}")
                    for dst, off in ((wAq[k], 0), (wAk[k], C), (wAv[k], 2 * C)):
                        sw.inc(gpsimd.dma_start(
                            out=dst[:, :].rearrange("p (ci c) -> p ci c", c=128),
                            in_=WA[:, :, off + 128 * p:off + 128 * p + 128]), 16)
                waval = sw.n

                for wi2, (wsrc, dst) in enumerate(
                        ((wAq[k], qTp[k]), (wAk[k], kTp[k]))):
                    for hh in range(2):
                        par = (2 * wi2 + hh) % 2
                        poff = 1024 * par

                        @blk.tensor
                        def _(tensor, wsrc=wsrc, waval=waval, k=k, sw=sw,
                              hh=hh, poff=poff, par=par):
                            tensor.wait_ge(sw.h, waval)
                            self.wait_guard(tensor, f"PSAh{par}")
                            self.wait_guard(tensor, f"qkv{k}")
                            for ci in range(NCH):
                                for n in range(2):
                                    inst = nc.tensor.matmul(
                                        self.PS_A[:, poff + 512 * n:
                                                  poff + 512 * n + 512],
                                        wsrc[:, 128 * ci:128 * ci + 128],
                                        self.xT[:, XCW * ci + XO + 1024 * hh
                                                + 512 * n:
                                                XCW * ci + XO + 1024 * hh
                                                + 512 * n + 512],
                                        start=(ci == 0), stop=(ci == NCH - 1))
                            s_mm.inc(inst)

                        @blk.scalar
                        def _(scalar, dst=dst, hh=hh, poff=poff):
                            scalar.wait_ge(s_mm.h, s_mm.n)
                            s_act.inc(scalar.activation(
                                dst[:, 1024 * hh:1024 * hh + 1024],
                                self.PS_A[:, poff:poff + 1024], AF.Identity))
                        self.set_guard(f"PSAh{par}", s_act)

                for g in range(4):
                    @blk.tensor
                    def _(tensor, g=g, k=k, waval=waval, sw=sw):
                        tensor.wait_ge(sw.h, waval)
                        self.wait_guard(tensor, "PS_B")
                        self.wait_guard(tensor, f"qkv{k}")
                        for j in range(4):
                            ti = 4 * g + j
                            for ci in range(NCH):
                                inst = nc.tensor.matmul(
                                    self.PS_B[:, 128 * j:128 * j + 128],
                                    self.xT[:, XCW * ci + XO + 128 * ti:
                                            XCW * ci + XO + 128 * ti + 128],
                                    wAv[k][:, 128 * ci:128 * ci + 128],
                                    start=(ci == 0), stop=(ci == NCH - 1))
                        s_mm.inc(inst)

                    @blk.vector
                    def _(vector, g=g, k=k):
                        vector.wait_ge(s_mm.h, s_mm.n)
                        s_v.inc(vector.tensor_copy(
                            vTp[k][:, 512 * g:512 * g + 512],
                            self.PS_B[:, 0:512]))
                    self.set_guard("PS_B", s_v)
                self.set_guard(f"wA{k}", s_v)

                def finish0(ctx):
                    qi, hl, kk = ctx["qi"], ctx["hl"], ctx["k"]
                    pp = ctx["p"]
                    pn = self.attn_post(ctx)
                    self.emit_p_transposes(qi, pn, ctx["P_s"], ctx["pskey"])
                    qc, qr = qi // 4, qi % 4

                    @blk.tensor
                    def _(tensor, qi=qi, kk=kk, hl=hl, qr=qr):
                        tensor.wait_ge(s_v.h, s_v.n)
                        tensor.wait_ge(s_act.h, s_act.n)
                        if qr == 0:
                            self.wait_guard(tensor, "PS_C")
                        for si in range(qi + 1):
                            inst = nc.tensor.matmul(
                                self.PS_C[64 * hl:64 * hl + 64,
                                          128 * qr:128 * qr + 128],
                                vTp[kk][:, 128 * si + 64 * hl:
                                        128 * si + 64 * hl + 64],
                                self.PT[:, 128 * si:128 * si + 128],
                                start=(si == 0), stop=(si == qi),
                                tile_position=(0, 64 * hl))
                        s_mm.inc(inst)

                    if qr == 3:
                        @blk.scalar
                        def _(scalar, pp=pp, hl=hl, qc=qc):
                            scalar.wait_ge(s_mm.h, s_mm.n)
                            s_act.inc(scalar.activation(
                                oT[64 * hl:64 * hl + 64,
                                   T * pp + 512 * qc:T * pp + 512 * qc + 512],
                                self.PS_C[64 * hl:64 * hl + 64, 0:512],
                                AF.Identity))
                        self.set_guard("PS_C", s_act)

                pend = None
                for hl in (0, 1):
                    hs = slice(64 * hl, 64 * hl + 64)
                    for qi in range(NT):
                        def emit_S(tensor, c0, cw, poff, has_diag,
                                   k=k, hs=hs, hl=hl, qi=qi):
                            for n0 in range(0, cw, 512):
                                nw = min(512, cw - n0)
                                inst = nc.tensor.matmul(
                                    self.PS_A[:, poff + n0:poff + n0 + nw],
                                    qTp[k][hs, 128 * qi:128 * qi + 128],
                                    kTp[k][hs, c0 + n0:c0 + n0 + nw],
                                    start=True, stop=False,
                                    tile_position=(64 * hl, 0))
                            if has_diag:
                                inst = nc.tensor.matmul(
                                    self.PS_A[:, poff + cw - 128:poff + cw],
                                    self.identbf[:], self.maskbf[:],
                                    start=False, stop=True)
                            s_mm.inc(inst)

                        ctx = self.attn_pre(qi, self.cols[:, 1:2], emit_S)
                        ctx.update(hl=hl, k=k, p=p)
                        if pend is not None:
                            finish0(pend)
                        pend = ctx
                finish0(pend)
                self.set_guard(f"qkv{k}", s_mm)

            @blk.gpsimd
            def _(gpsimd):
                s_gdma.inc(gpsimd.dma_start(
                    out=wP[:, :].rearrange("p (ci e) -> p ci e", e=1024),
                    in_=io["W_proj0"].rearrange("(ci p) e -> p ci e", p=128)),
                    16)
            wpval = s_gdma.n

            xb, xsrc = self.xbuf[0], self.xsrc[0]
            build_xt = (1 in self.blocks)
            for ti in range(NT):
                dv = self.prefetch_x(xsrc, ti)
                pa = self.PS_B1k if ti % 2 == 0 else self.PS_A1k
                pkey = "PS_B" if ti % 2 == 0 else "PS_A"
                self._ln_psum_key = pkey

                @blk.tensor
                def _(tensor, ti=ti, pa=pa, pkey=pkey, wpval=wpval):
                    tensor.wait_ge(s_gdma.h, wpval)
                    self.wait_guard(tensor, pkey)
                    for ci in range(NCH):
                        for n in range(2):
                            inst = nc.tensor.matmul(
                                pa[:, 512 * n:512 * n + 512],
                                oT[:, T * ci + 128 * ti:T * ci + 128 * ti + 128],
                                wP[:, 1024 * ci + 512 * n:
                                   1024 * ci + 512 * n + 512],
                                start=(ci == 0), stop=(ci == NCH - 1))
                    s_mm.inc(inst)

                self.emit_ln_tile(ti, pa, xb, build_xt, s_mm.n, dma_val=dv)

    def phase_block1(self):
        blk, nc, io = self.blk, self.nc, self.io
        s_dma, s_gdma, s_mm, s_v, s_act = (self.s_dma, self.s_gdma, self.s_mm,
                                           self.s_v, self.s_act)
        dbg = getattr(self, "debug_b1", False)
        with ExitStack() as pctx:
            if dbg:
                pctx = self.ctx
            wrT = self.sb("wrT", [128, NCH * 1024], BF16, pctx)
            wkT = self.sb("wkT", [128, NCH * 1024], BF16, pctx)
            wvT = self.sb("wvT", [128, NCH * 1024], BF16, pctx)
            wstage = self.sb("wstage", [128, NCH * 1024], BF16, pctx)
            xrS = self.sb("xrS", [128, 2 * NCH * 512], BF16, pctx)
            xkS = self.sb("xkS", [128, 2 * NCH * 512], BF16, pctx)
            rS = self.sb("rS", [128, 4 * 1024], BF16, pctx)
            kS = self.sb("kS", [128, NCH * 512], BF16, pctx)
            mixtmp = self.sb("mixtmp", [128, 512], BF16, pctx)
            murow = self.sb("murow", [8, 256], F32, pctx)
            mucols = self.sb("mucols", [128, 8], F32, pctx)
            mu1cols = self.sb("mu1cols", [128, 8], F32, pctx)
            mkcols = self.sb("mkcols", [128, 8], F32, pctx)
            mk1cols = self.sb("mk1cols", [128, 8], F32, pctx)

            @blk.sync
            def _(sync):
                s_dma.inc(sync.dma_start(
                    out=murow[:, 0:128],
                    in_=io["mu_r"].rearrange("(a p) -> a p", p=128)), 16)
                s_dma.inc(sync.dma_start(
                    out=murow[:, 128:256],
                    in_=io["mu_k"].rearrange("(a p) -> a p", p=128)), 16)
            muval = s_dma.n

            @blk.tensor
            def _(tensor):
                tensor.wait_ge(s_dma.h, muval)
                self.wait_guard(tensor, "PS_C")
                nc.tensor.transpose(self.PS_C[:, 0:8], murow[:, 0:128],
                                    self.ident32[0:8, 0:8])
                s_mm.inc(nc.tensor.transpose(self.PS_C[:, 8:16],
                                             murow[:, 128:256],
                                             self.ident32[0:8, 0:8]))

            @blk.vector
            def _(vector):
                vector.wait_ge(s_mm.h, s_mm.n)
                vector.tensor_copy(mucols[:], self.PS_C[:, 0:8])
                self.vsync(vector, vector.tensor_copy(mkcols[:],
                                                      self.PS_C[:, 8:16]))
                vector.tensor_scalar(mu1cols[:], mucols[:], 1.0, -1.0,
                                     ALU.subtract, ALU.mult)
                s_v.inc(vector.tensor_scalar(mk1cols[:], mkcols[:], 1.0, -1.0,
                                             ALU.subtract, ALU.mult))
            self.set_guard("PS_C", s_v)

            for wname, dst in (("W_r", wrT), ("W_k", wkT), ("W_v", wvT)):
                @blk.gpsimd
                def _(gpsimd, wname=wname):
                    self.wait_guard(gpsimd, "wstage")
                    self.s_wld.inc(gpsimd.dma_start(
                        out=wstage[:, :].rearrange("p (dj c) -> p dj c", c=1024),
                        in_=io[wname].rearrange("(dj p) c -> p dj c", p=128)),
                        16)
                self.emit_wT_transposes(wstage, dst, self.s_wld.n)
                self.set_guard("wstage", s_mm)

            if dbg:
                self.dbgx = self.sb("dbgx", [128, 2048], F32, pctx)
                self.dbg_t = dict(xrS=xrS, xkS=xkS, rS=rS, kS=kS, wrT=wrT,
                                  wkT=wkT, wvT=wvT, dbgx=self.dbgx)
            xb, xsrc = self.xbuf[1], self.xsrc[1]
            build_xt = (2 in self.blocks)
            mixvals = {}

            def emit_mix(tc):
                par = tc % 2
                moff = 4096 * par

                @blk.vector
                def _(vector, tc=tc, moff=moff, par=par):
                    self.wait_guard(vector, f"mixbuf{par}")
                    vector.wait_ge(s_v.h, s_v.n)
                    for ci in range(NCH):
                        base = XCW * ci + XO + 512 * tc
                        for cm, c1, dstS in ((mucols, mu1cols, xrS),
                                             (mkcols, mk1cols, xkS)):
                            self.vsync(vector, vector.tensor_scalar(
                                mixtmp[:], self.xT[:, base - 1:base - 1 + 512],
                                c1[:, ci:ci + 1], None, ALU.mult))
                            inst = vector.scalar_tensor_tensor(
                                dstS[:, moff + 512 * ci:moff + 512 * ci + 512],
                                self.xT[:, base:base + 512],
                                cm[:, ci:ci + 1], mixtmp[:],
                                ALU.mult, ALU.add)
                    s_v.inc(inst)
                mixvals[tc] = s_v.n

            emit_mix(0)
            for tc in range(1 if dbg else 4):
                self.cut()
                moff = 4096 * (tc % 2)
                mixval = mixvals[tc]

                for tl in range(4):
                    pa = self.PS_B1k if tl % 2 == 0 else self.PS_A1k
                    pkey = "PS_B" if tl % 2 == 0 else "PS_A"

                    @blk.tensor
                    def _(tensor, tl=tl, pa=pa, pkey=pkey, mixval=mixval,
                          moff=moff):
                        tensor.wait_ge(s_v.h, mixval)
                        self.wait_guard(tensor, pkey)
                        for ci in range(NCH):
                            for n in range(2):
                                inst = nc.tensor.matmul(
                                    pa[:, 512 * n:512 * n + 512],
                                    xrS[:, moff + 512 * ci + 128 * tl:
                                        moff + 512 * ci + 128 * tl + 128],
                                    wrT[:, 1024 * ci + 512 * n:
                                        1024 * ci + 512 * n + 512],
                                    start=(ci == 0), stop=(ci == NCH - 1))
                        s_mm.inc(inst)

                    @blk.scalar
                    def _(scalar, tl=tl, pa=pa):
                        scalar.wait_ge(s_mm.h, s_mm.n)
                        self.wait_guard(scalar, f"rSbuf{tl}")
                        s_act.inc(scalar.activation(
                            rS[:, 1024 * tl:1024 * tl + 1024], pa[:],
                            AF.Sigmoid))
                    self.set_guard(pkey, s_act)

                if tc + 1 < 4 and not dbg:
                    emit_mix(tc + 1)
                for dp in range(NCH // 2):
                    slot = 1024 * (dp % 2)

                    @blk.tensor
                    def _(tensor, dp=dp, slot=slot, mixval=mixval, moff=moff):
                        tensor.wait_ge(s_v.h, mixval)
                        self.wait_guard(tensor, f"kslot{dp % 2}")
                        if dp < 1:
                            self.wait_guard(tensor, "PS_A")
                        for j in range(2):
                            dj = 2 * dp + j
                            for ci in range(NCH):
                                inst = nc.tensor.matmul(
                                    self.PS_A[:, slot + 512 * j:
                                              slot + 512 * j + 512],
                                    wkT[:, 1024 * ci + 128 * dj:
                                        1024 * ci + 128 * dj + 128],
                                    xkS[:, moff + 512 * ci:
                                        moff + 512 * ci + 512],
                                    start=(ci == 0), stop=(ci == NCH - 1))
                        s_mm.inc(inst)

                    @blk.scalar
                    def _(scalar, dp=dp, slot=slot):
                        scalar.wait_ge(s_mm.h, s_mm.n)
                        self.wait_guard(scalar, f"kSbuf{dp}")
                        s_act.inc(scalar.activation(
                            kS[:, 1024 * dp:1024 * dp + 1024],
                            self.PS_A[:, slot:slot + 1024], AF.Relu))

                    @blk.vector
                    def _(vector, dp=dp):
                        vector.wait_ge(s_act.h, s_act.n)
                        s_v.inc(vector.tensor_mul(
                            kS[:, 1024 * dp:1024 * dp + 1024],
                            kS[:, 1024 * dp:1024 * dp + 1024],
                            kS[:, 1024 * dp:1024 * dp + 1024]))
                    self.set_guard(f"kslot{dp % 2}", s_v)
                    self.set_guard(f"kSbuf{dp}", s_v)
                kdone = s_v.n
                self.set_guard("PS_A", s_v)
                self.set_guard(f"mixbuf{tc % 2}", s_mm)

                for tl in range(4):
                    ti = 4 * tc + tl
                    dv = self.prefetch_x(xsrc, ti)
                    pa = self.PS_B1k if tl % 2 == 0 else self.PS_A1k
                    pkey = "PS_B" if tl % 2 == 0 else "PS_A"
                    self._ln_psum_key = pkey

                    @blk.tensor
                    def _(tensor, tl=tl, pa=pa, pkey=pkey, kdone=kdone):
                        tensor.wait_ge(s_v.h, kdone)
                        self.wait_guard(tensor, pkey)
                        for dj in range(NCH):
                            for n in range(2):
                                inst = nc.tensor.matmul(
                                    pa[:, 512 * n:512 * n + 512],
                                    kS[:, 512 * dj + 128 * tl:
                                       512 * dj + 128 * tl + 128],
                                    wvT[:, 1024 * dj + 512 * n:
                                        1024 * dj + 512 * n + 512],
                                    start=(dj == 0), stop=(dj == NCH - 1))
                        s_mm.inc(inst)

                    def prod_y(vector, xsl, pa=pa, tl=tl):
                        self.vsync(vector, vector.scalar_tensor_tensor(
                            self.ffntmp[:], pa[:], 1.0,
                            rS[:, 1024 * tl:1024 * tl + 1024],
                            ALU.bypass, ALU.mult))
                        self.vsync(vector, vector.tensor_add(
                            xsl, xsl, self.ffntmp[:]))

                    self.emit_ln_tile(ti, pa, xb, build_xt, s_mm.n,
                                      produce_y=prod_y, dma_val=dv)
                    self.set_guard(f"rSbuf{tl}", s_v)

    def phase_block2(self):
        blk, nc, io = self.blk, self.nc, self.io
        s_dma, s_gdma, s_mm, s_v, s_act = (self.s_dma, self.s_gdma, self.s_mm,
                                           self.s_v, self.s_act)
        with ExitStack() as pctx:
            qT = self.sb("qT", [128, NCH * T], BF16, pctx)
            kT = self.sb("kT", [128, NCH * T], BF16, pctx)
            vTok = self.sb("vTok", [128, NT * 1024], BF16, pctx)
            wT = self.sb("wT", [128, NCH * 1024], BF16, pctx)
            wstage = self.sb("wstage2", [128, NCH * 1024], BF16, pctx)
            brow = self.sb("brow", [8, 256], F32, pctx)
            bcols = self.sb("bcols", [128, 16], F32, pctx)
            bvrow = self.sb("bvrow", [1, 1024], BF16, pctx)

            @blk.sync
            def _(sync):
                for wi2, bname in ((0, "bq"), (1, "bk")):
                    s_dma.inc(sync.dma_start(
                        out=brow[:, 128 * wi2:128 * wi2 + 128],
                        in_=io[bname].rearrange("(a p) -> a p", p=128)), 16)
            brval = s_dma.n
            for wi, (wname, bname) in enumerate(
                    (("Wq", "bq"), ("Wk", "bk"), ("Wv", "bv"))):
                @blk.gpsimd
                def _(gpsimd, wname=wname):
                    self.wait_guard(gpsimd, "wstage2")
                    self.s_wld.inc(gpsimd.dma_start(
                        out=wstage[:, :].rearrange("p (dj c) -> p dj c", c=1024),
                        in_=io[wname].rearrange("(dj p) c -> p dj c", p=128)),
                        16)
                wval = self.s_wld.n

                if wi < 2:
                    @blk.tensor
                    def _(tensor, wi=wi, bval=brval):
                        tensor.wait_ge(s_dma.h, bval)
                        self.wait_guard(tensor, "PS_C")
                        s_mm.inc(nc.tensor.transpose(
                            self.PS_C[:, 8 * wi:8 * wi + 8],
                            brow[:, 128 * wi:128 * wi + 128],
                            self.ident32[0:8, 0:8]))

                    @blk.vector
                    def _(vector, wi=wi):
                        vector.wait_ge(s_mm.h, s_mm.n)
                        s_v.inc(vector.tensor_copy(
                            bcols[:, 8 * wi:8 * wi + 8],
                            self.PS_C[:, 8 * wi:8 * wi + 8]))
                    self.set_guard("PS_C", s_v)
                else:
                    @blk.gpsimd
                    def _(gpsimd):
                        s_gdma.inc(gpsimd.dma_start(
                            out=bvrow[:],
                            in_=io["bv"].rearrange("(a c) -> a c", a=1)), 16)
                    bvg = s_gdma.n

                    @blk.vector
                    def _(vector, bvg=bvg):
                        vector.wait_ge(s_gdma.h, bvg)
                        s_v.inc(vector.tensor_copy(bvrow[:], bvrow[:]))

                wtval = self.emit_wT_transposes(wstage, wT, wval,
                                                busykey="wTbusy")
                self.set_guard("wstage2", s_mm)

                if wi < 2:
                    dst = qT if wi == 0 else kT
                    for m in range(NCH):
                        @blk.tensor
                        def _(tensor, m=m, wtval=wtval):
                            tensor.wait_ge(s_v.h, wtval)
                            self.wait_guard(tensor, "PS_A")
                            for ci in range(NCH):
                                for n in range(4):
                                    inst = nc.tensor.matmul(
                                        self.PS_A[:, 512 * n:512 * n + 512],
                                        wT[:, 1024 * ci + 128 * m:
                                           1024 * ci + 128 * m + 128],
                                        self.xT[:, XCW * ci + XO + 512 * n:
                                                XCW * ci + XO + 512 * n + 512],
                                        start=(ci == 0), stop=(ci == NCH - 1))
                            s_mm.inc(inst)

                        @blk.scalar
                        def _(scalar, m=m, dst=dst, wi=wi):
                            scalar.wait_ge(s_mm.h, s_mm.n)
                            s_act.inc(scalar.activation(
                                dst[:, T * m:T * m + T], self.PS_A[:],
                                AF.Identity,
                                bias=bcols[:, 8 * wi + m:8 * wi + m + 1]))
                        self.set_guard("PS_A", s_act)
                    self.set_guard("wTbusy", s_act)
                else:
                    vready = s_v.n
                    self._emit_l1_v(wT, bvrow, vTok, vready)

            xb, xsrc = self.xbuf[2], self.xsrc[2]
            build_xt = (3 in self.blocks)
            def finish2(ctx):
                qi = ctx["qi"]
                pn = self.attn_post(ctx)
                self.emit_p_transposes(qi, pn, ctx["P_s"], ctx["pskey"])

                @blk.tensor
                def _(tensor, qi=qi):
                    tensor.wait_ge(s_v.h, s_v.n)
                    tensor.wait_ge(s_act.h, s_act.n)
                    self.wait_guard(tensor, "PS_B")
                    for si in range(qi + 1):
                        for n in range(2):
                            inst = nc.tensor.matmul(
                                self.PS_B[:, 512 * n:512 * n + 512],
                                self.PT[:, 128 * si:128 * si + 128],
                                vTok[:, 1024 * si + 512 * n:
                                     1024 * si + 512 * n + 512],
                                start=(si == 0), stop=(si == qi))
                    s_mm.inc(inst)

                self._ln_psum_key = "PS_B"
                self.emit_ln_tile(qi, self.PS_B1k, xb, build_xt, s_mm.n,
                                  dma_val=ctx["dv"])

            pend = None
            for qi in range(NT):
                if qi % 4 == 0:
                    self.cut()
                dv = self.prefetch_x(xsrc, qi)

                def emit_S(tensor, c0, cw, poff, has_diag, qi=qi):
                    for ci in range(NCH):
                        for n0 in range(0, cw, 512):
                            nw = min(512, cw - n0)
                            inst = nc.tensor.matmul(
                                self.PS_A[:, poff + n0:poff + n0 + nw],
                                qT[:, T * ci + 128 * qi:T * ci + 128 * qi + 128],
                                kT[:, T * ci + c0 + n0:T * ci + c0 + n0 + nw],
                                start=(ci == 0), stop=False)
                    if has_diag:
                        inst = nc.tensor.matmul(
                            self.PS_A[:, poff + cw - 128:poff + cw],
                            self.identbf[:], self.maskbf[:],
                            start=False, stop=True)
                    s_mm.inc(inst)

                ctx = self.attn_pre(qi, self.cols[:, 2:3], emit_S)
                ctx["dv"] = dv
                if pend is not None:
                    finish2(pend)
                pend = ctx
            finish2(pend)

    def _emit_l1_v(self, wT, bvrow, vTok, vready):
        blk, nc = self.blk, self.nc
        s_mm, s_v = self.s_mm, self.s_v
        for ti in range(NT):
            pa = self.PS_B1k if ti % 2 == 0 else self.PS_A1k
            pkey = "PS_B" if ti % 2 == 0 else "PS_A"

            @blk.tensor
            def _(tensor, ti=ti, pa=pa, pkey=pkey, vready=vready):
                tensor.wait_ge(s_v.h, vready)
                self.wait_guard(tensor, pkey)
                for n in range(2):
                    for ci in range(NCH):
                        nc.tensor.matmul(
                            pa[:, 512 * n:512 * n + 512],
                            self.xT[:, XCW * ci + XO + 128 * ti:
                                    XCW * ci + XO + 128 * ti + 128],
                            wT[:, 1024 * ci + 512 * n:1024 * ci + 512 * n + 512],
                            start=(ci == 0), stop=False)
                    inst = nc.tensor.matmul(
                        pa[:, 512 * n:512 * n + 512], self.onesbf[0:1, :],
                        bvrow[0:1, 512 * n:512 * n + 512],
                        start=False, stop=True)
                s_mm.inc(inst)

            @blk.scalar
            def _(scalar, ti=ti, pa=pa):
                scalar.wait_ge(s_mm.h, s_mm.n)
                self.s_act.inc(scalar.activation(
                    vTok[:, 1024 * ti:1024 * ti + 1024], pa[:], AF.Identity))
            self.set_guard(pkey, self.s_act)

    def phase_block3(self):
        blk, nc, io = self.blk, self.nc, self.io
        s_dma, s_gdma, s_mm, s_v, s_act = (self.s_dma, self.s_gdma, self.s_mm,
                                           self.s_v, self.s_act)
        with ExitStack() as pctx:
            wfcT = self.sb("wfcT", [128, NCH * 4096], BF16, pctx)
            qstage = self.sb("qstage", [128, 2 * 1024], F32, pctx)
            qbf = self.sb("qbf", [128, 2048], BF16, pctx)
            wps = self.sb("wps", [128, 1024], BF16, pctx)
            hS = self.sb("hS", [128, 32 * 512], BF16, pctx)
            wpsl = self.sb("wpsl", [128, 4 * 1024], BF16, pctx)
            qcols = self.sb("qcols", [128, 8], F32, pctx)
            qrow = self.sb("qrow", [1, 8], F32, pctx)
            ones32 = self.sb("ones32", [1, 128], F32, pctx)
            afT = self.sb("afT", [128, NCH * 16], BF16, pctx)
            apT = self.sb("apT", [128, 32 * 16], BF16, pctx)
            bfT = self.sb("bfT", [16, 4096], BF16, pctx)
            bpT = self.sb("bpT", [16, 1024], BF16, pctx)
            bstage = self.sb("bstage", [128, 512], BF16, pctx)
            arow = self.sb("arow", [16, 4 * 1024], BF16, pctx)
            bfcrow = self.sb("bfcrow", [32, 128], F32, pctx)
            bfccols = self.sb("bfccols", [128, 32], F32, pctx)
            bprow = self.sb("bprow", [1, 1024], BF16, pctx)
            m1T = self.sb("m1T", [16, 512], BF16, pctx)
            m2T = self.sb("m2T", [16, 512], BF16, pctx)

            @blk.vector
            def _(vector):
                s_v.inc(vector.memset(ones32[:], 1.0))

            self._quant_weight(
                "W_fc", [(128 * sj, 0, 1024) for sj in range(32)],
                wfcT, None, qstage, qbf, wps, qcols, qrow, ones32, 0)
            self._quant_weight(
                "W_proj1", [(128 * dj, 1024 * h, 1024)
                            for dj in range(8) for h in range(4)],
                None, io["wpcache"], qstage, qbf, wps, qcols, qrow, ones32, 4)

            @blk.gpsimd
            def _(gpsimd):
                s_gdma.inc(gpsimd.dma_start(out=arow[0:16, 3072:4096],
                                            in_=io["A_fc"]), 16)
                s_gdma.inc(gpsimd.dma_start(
                    out=bstage[:, :].rearrange("p (c r) -> p c r", r=16),
                    in_=io["B_fc"].rearrange("(c p) r -> p c r", p=128)), 16)
            laval = s_gdma.n

            @blk.sync
            def _(sync):
                s_dma.inc(sync.dma_start(
                    out=bfcrow[:],
                    in_=io["b_fc"].rearrange("(a p) -> a p", p=128)), 16)
            bfval = s_dma.n

            @blk.gpsimd
            def _(gpsimd):
                s_gdma.inc(gpsimd.dma_start(
                    out=bprow[:],
                    in_=io["b_proj"].rearrange("(a c) -> a c", a=1)), 16)
            bpg = s_gdma.n

            @blk.tensor
            def _(tensor):
                tensor.wait_ge(s_dma.h, bfval)
                self.wait_guard(tensor, "PS_C")
                s_mm.inc(nc.tensor.transpose(self.PS_C[:, 0:32], bfcrow[:],
                                             self.ident32[0:32, 0:32]))

            @blk.vector
            def _(vector, bpg=bpg):
                vector.wait_ge(s_mm.h, s_mm.n)
                vector.wait_ge(s_gdma.h, bpg)
                s_v.inc(vector.tensor_copy(bfccols[:], self.PS_C[:, 0:32]))
            self.set_guard("PS_C", s_v)

            for j in range(8):
                @blk.tensor
                def _(tensor, j=j, laval=laval):
                    tensor.wait_ge(s_gdma.h, laval)
                    self.wait_guard(tensor, "PS_D")
                    s_mm.inc(nc.tensor.transpose(
                        self.PS_D[:, 0:16],
                        arow[0:16, 3072 + 128 * j:3072 + 128 * j + 128],
                        self.identbf[0:16, 0:16]))

                @blk.vector
                def _(vector, j=j):
                    vector.wait_ge(s_mm.h, s_mm.n)
                    s_v.inc(vector.tensor_copy(afT[:, 16 * j:16 * j + 16],
                                               self.PS_D[:, 0:16]))
                self.set_guard("PS_D", s_v)
            afdone = s_mm.n

            @blk.gpsimd
            def _(gpsimd, afdone=afdone):
                gpsimd.wait_ge(s_mm.h, afdone)
                s_gdma.inc(gpsimd.dma_start(out=arow[0:16, :],
                                            in_=io["A_proj"]), 16)
            apval = s_gdma.n
            for j in range(32):
                @blk.tensor
                def _(tensor, j=j, apval=apval):
                    tensor.wait_ge(s_gdma.h, apval)
                    self.wait_guard(tensor, "PS_D")
                    s_mm.inc(nc.tensor.transpose(
                        self.PS_D[:, 0:16],
                        arow[0:16, 128 * j:128 * j + 128],
                        self.identbf[0:16, 0:16]))

                @blk.vector
                def _(vector, j=j):
                    vector.wait_ge(s_mm.h, s_mm.n)
                    s_v.inc(vector.tensor_copy(apT[:, 16 * j:16 * j + 16],
                                               self.PS_D[:, 0:16]))
                self.set_guard("PS_D", s_v)

            for cJ in range(32):
                @blk.tensor
                def _(tensor, cJ=cJ, laval=laval):
                    tensor.wait_ge(s_gdma.h, laval)
                    self.wait_guard(tensor, "PS_D")
                    s_mm.inc(nc.tensor.transpose(
                        self.PS_D[0:16, 0:128],
                        bstage[:, 16 * cJ:16 * cJ + 16], self.identbf[:]))

                @blk.vector
                def _(vector, cJ=cJ):
                    vector.wait_ge(s_mm.h, s_mm.n)
                    s_v.inc(vector.tensor_copy(
                        bfT[0:16, 128 * cJ:128 * cJ + 128],
                        self.PS_D[0:16, 0:128]))
                self.set_guard("PS_D", s_v)
            self.set_guard("bstage", s_mm)

            @blk.gpsimd
            def _(gpsimd):
                self.wait_guard(gpsimd, "bstage")
                s_gdma.inc(gpsimd.dma_start(
                    out=bstage[:, 0:128].rearrange("p (c r) -> p c r", r=16),
                    in_=io["B_proj"].rearrange("(c p) r -> p c r", p=128)), 16)
            bpval = s_gdma.n
            for cJ in range(8):
                @blk.tensor
                def _(tensor, cJ=cJ, bpval=bpval):
                    tensor.wait_ge(s_gdma.h, bpval)
                    self.wait_guard(tensor, "PS_D")
                    s_mm.inc(nc.tensor.transpose(
                        self.PS_D[0:16, 0:128],
                        bstage[:, 16 * cJ:16 * cJ + 16], self.identbf[:]))

                @blk.vector
                def _(vector, cJ=cJ):
                    vector.wait_ge(s_mm.h, s_mm.n)
                    s_v.inc(vector.tensor_copy(
                        bpT[0:16, 128 * cJ:128 * cJ + 128],
                        self.PS_D[0:16, 0:128]))
                self.set_guard("PS_D", s_v)

            xb, xsrc = self.xbuf[3], self.xsrc[3]
            for tc in range(4):
                self.cut()
                hvals = []
                @blk.tensor
                def _(tensor, tc=tc):
                    tensor.wait_ge(s_v.h, s_v.n)
                    self.wait_guard(tensor, "PS_C")
                    self.wait_guard(tensor, "m1T")
                    for ci in range(NCH):
                        inst = nc.tensor.matmul(
                            self.PS_C[0:16, 0:512],
                            afT[:, 16 * ci:16 * ci + 16],
                            self.xT[:, XCW * ci + XO + 512 * tc:
                                    XCW * ci + XO + 512 * tc + 512],
                            start=(ci == 0), stop=(ci == NCH - 1))
                    s_mm.inc(inst)

                @blk.vector
                def _(vector):
                    vector.wait_ge(s_mm.h, s_mm.n)
                    s_v.inc(vector.tensor_scalar(m1T[0:16, :],
                                                 self.PS_C[0:16, 0:512],
                                                 2.0, None, ALU.mult))
                self.set_guard("PS_C", s_v)
                m1val = s_v.n

                for fi in range(32):
                    slot = 512 * (fi % 4)

                    @blk.tensor
                    def _(tensor, tc=tc, fi=fi, slot=slot, m1val=m1val):
                        tensor.wait_ge(s_v.h, m1val)
                        self.wait_guard(tensor, f"hslot{fi % 4}")
                        if fi == 0:
                            self.wait_guard(tensor, "PS_A")
                            self.wait_guard(tensor, "hS")
                        for ci in range(NCH):
                            nc.tensor.matmul(
                                self.PS_A[:, slot:slot + 512],
                                wfcT[:, 4096 * ci + 128 * fi:
                                     4096 * ci + 128 * fi + 128],
                                self.xT[:, XCW * ci + XO + 512 * tc:
                                        XCW * ci + XO + 512 * tc + 512],
                                start=(ci == 0), stop=False)
                        inst = nc.tensor.matmul(
                            self.PS_A[:, slot:slot + 512],
                            bfT[0:16, 128 * fi:128 * fi + 128],
                            m1T[0:16, 0:512], start=False, stop=True)
                        s_mm.inc(inst)

                    @blk.scalar
                    def _(scalar, fi=fi, slot=slot):
                        scalar.wait_ge(s_mm.h, s_mm.n)
                        s_act.inc(scalar.activation(
                            hS[:, 512 * fi:512 * fi + 512],
                            self.PS_A[:, slot:slot + 512], AF.Gelu,
                            bias=bfccols[:, fi:fi + 1]))
                    self.set_guard(f"hslot{fi % 4}", s_act)
                    hvals.append(s_act.n)
                hval = s_act.n

                @blk.tensor
                def _(tensor, hvals=hvals):
                    self.wait_guard(tensor, "PS_C")
                    self.wait_guard(tensor, "m2T")
                    for fj in range(32):
                        tensor.wait_ge(s_act.h, hvals[fj])
                        inst = nc.tensor.matmul(
                            self.PS_C[0:16, 0:512],
                            apT[:, 16 * fj:16 * fj + 16],
                            hS[:, 512 * fj:512 * fj + 512],
                            start=(fj == 0), stop=(fj == 31))
                    s_mm.inc(inst)

                @blk.vector
                def _(vector):
                    vector.wait_ge(s_mm.h, s_mm.n)
                    s_v.inc(vector.tensor_scalar(m2T[0:16, :],
                                                 self.PS_C[0:16, 0:512],
                                                 2.0, None, ALU.mult))
                self.set_guard("PS_C", s_v)
                m2val = s_v.n

                for hc in (0, 1):
                    tls = (2 * hc, 2 * hc + 1)
                    dvs = [self.prefetch_x(xsrc, 4 * tc + tl) for tl in tls]
                    pas = [self.PS_B1k, self.PS_A1k]
                    pkeys = ["PS_B", "PS_A"]
                    mmbase = s_mm.n
                    wpv = []
                    for fj in range(32):
                        wk = fj % 4

                        sp = self.s_p[wk]

                        @blk.sync
                        def _(sync, fj=fj, wk=wk, mmbase=mmbase, sp=sp):
                            if fj >= 4:
                                sync.wait_ge(s_mm.h, mmbase + fj - 3)
                            else:
                                self.wait_guard(sync, f"wpsl{wk}")
                            sp.inc(sync.dma_start(
                                out=wpsl[:, 1024 * wk:1024 * wk + 1024],
                                in_=io["wpcache"][128 * fj:128 * fj + 128, :]),
                                16)
                        wpv.append((sp, sp.n))

                    @blk.tensor
                    def _(tensor, tc=tc, tls=tls, pas=pas, m2val=m2val,
                          hvals=hvals, wpv=wpv):
                        tensor.wait_ge(s_v.h, m2val)
                        self.wait_guard(tensor, "PS_B")
                        self.wait_guard(tensor, "PS_A")
                        for fj in range(32):
                            tensor.wait_ge(s_act.h, hvals[fj])
                            tensor.wait_ge(wpv[fj][0].h, wpv[fj][1])
                            wk = fj % 4
                            for j, tl in enumerate(tls):
                                for n in range(2):
                                    inst = nc.tensor.matmul(
                                        pas[j][:, 512 * n:512 * n + 512],
                                        hS[:, 512 * fj + 128 * tl:
                                           512 * fj + 128 * tl + 128],
                                        wpsl[:, 1024 * wk + 512 * n:
                                             1024 * wk + 512 * n + 512],
                                        start=(fj == 0), stop=False)
                            s_mm.inc(inst)
                        for j, tl in enumerate(tls):
                            for n in range(2):
                                nc.tensor.matmul(
                                    pas[j][:, 512 * n:512 * n + 512],
                                    m2T[0:16, 128 * tl:128 * tl + 128],
                                    bpT[0:16, 512 * n:512 * n + 512],
                                    start=False, stop=False)
                                inst = nc.tensor.matmul(
                                    pas[j][:, 512 * n:512 * n + 512],
                                    self.onesbf[0:1, :],
                                    bprow[0:1, 512 * n:512 * n + 512],
                                    start=False, stop=True)
                        s_mm.inc(inst)
                    for wk2 in range(4):
                        self.set_guard(f"wpsl{wk2}", s_mm)
                    mmv = s_mm.n

                    for j, tl in enumerate(tls):
                        self._ln_psum_key = pkeys[j]
                        self.emit_ln_tile(4 * tc + tl, pas[j], xb, False, mmv,
                                          dma_val=dvs[j])
                self.set_guard("m1T", s_mm)
                self.set_guard("m2T", s_mm)
                self.set_guard("hS", s_mm)

    def _quant_weight(self, wname, slices, wT_dst, cache_dst, qstage,
                      qbf, wps, qcols, qrow, ones32, qc0):
        blk, nc, io = self.blk, self.nc, self.io
        s_dma, s_mm, s_v = self.s_dma, self.s_mm, self.s_v
        Wap = io[wname]
        O, I = Wap.shape

        @blk.vector
        def _(vector):
            s_v.inc(vector.memset(qcols[:, qc0:qc0 + 1], 0.0))
        for si_, (r0, c0, ncl) in enumerate(slices):
            qs = qstage[:, 1024 * (si_ % 2):1024 * (si_ % 2) + ncl]

            sq = self.s_p[2 + si_ % 2]

            @blk.sync
            def _(sync, r0=r0, c0=c0, ncl=ncl, qs=qs, si_=si_, sq=sq):
                self.wait_guard(sync, f"qstage{si_ % 2}")
                sq.inc(sync.dma_start(
                    out=qs, in_=Wap[r0:r0 + 128, c0:c0 + ncl]), 16)
            dval = sq.n

            @blk.vector
            def _(vector, dval=dval, qs=qs, sq=sq):
                vector.wait_ge(sq.h, dval)
                self.vsync(vector, vector.reduce_max(
                    qcols[:, qc0 + 1:qc0 + 2], qs,
                    axis=AX.X, apply_absolute_value=True))
                s_v.inc(vector.tensor_max(qcols[:, qc0:qc0 + 1],
                                          qcols[:, qc0:qc0 + 1],
                                          qcols[:, qc0 + 1:qc0 + 2]))
            self.set_guard(f"qstage{si_ % 2}", s_v)

        @blk.tensor
        def _(tensor):
            tensor.wait_ge(s_v.h, s_v.n)
            self.wait_guard(tensor, "PS_C")
            s_mm.inc(nc.tensor.transpose(self.PS_C[0:1, 0:128],
                                         qcols[:, qc0:qc0 + 1],
                                         self.ident32[:]))

        @blk.vector
        def _(vector):
            vector.wait_ge(s_mm.h, s_mm.n)
            self.vsync(vector, vector.reduce_max(qrow[0:1, 0:1],
                                                 self.PS_C[0:1, 0:128],
                                                 axis=AX.X))
            vector.tensor_scalar(qrow[0:1, 1:2], qrow[0:1, 0:1],
                                 1.0 / 127.0, None, ALU.mult)
            self.vsync(vector, vector.reciprocal(qrow[0:1, 2:3],
                                                 qrow[0:1, 0:1]))
            s_v.inc(vector.tensor_scalar(qrow[0:1, 2:3], qrow[0:1, 2:3],
                                         127.0, None, ALU.mult))
        self.set_guard("PS_C", s_v)

        @blk.tensor
        def _(tensor):
            tensor.wait_ge(s_v.h, s_v.n)
            self.wait_guard(tensor, "PS_C")
            s_mm.inc(nc.tensor.matmul(self.PS_C[:, 0:2], ones32[0:1, :],
                                      qrow[0:1, 1:3], start=True, stop=True))

        @blk.vector
        def _(vector):
            vector.wait_ge(s_mm.h, s_mm.n)
            s_v.inc(vector.tensor_copy(qcols[:, qc0 + 2:qc0 + 4],
                                       self.PS_C[:, 0:2]))
        self.set_guard("PS_C", s_v)
        scol = qcols[:, qc0 + 2:qc0 + 3]
        rcol = qcols[:, qc0 + 3:qc0 + 4]

        for si_, (r0, c0, ncl) in enumerate(slices):
            qs = qstage[:, 1024 * (si_ % 2):1024 * (si_ % 2) + ncl]
            sq = self.s_p[2 + si_ % 2]

            @blk.sync
            def _(sync, r0=r0, c0=c0, ncl=ncl, qs=qs, si_=si_, sq=sq):
                self.wait_guard(sync, f"qstage{si_ % 2}")
                sq.inc(sync.dma_start(
                    out=qs, in_=Wap[r0:r0 + 128, c0:c0 + ncl]), 16)
            dval = sq.n

            @blk.vector
            def _(vector, dval=dval, qs=qs, ncl=ncl, sq=sq):
                vector.wait_ge(sq.h, dval)
                self.wait_guard(vector, "qbf")
                self.vsync(vector, vector.tensor_scalar(
                    qs, qs, rcol, MAGIC, ALU.mult, ALU.add))
                self.vsync(vector, vector.tensor_scalar(
                    qs, qs, MAGIC, -128.0, ALU.subtract, ALU.max))
                s_v.inc(vector.tensor_scalar(qbf[:, 0:ncl], qs,
                                             127.0, scol, ALU.min, ALU.mult))
            self.set_guard(f"qstage{si_ % 2}", s_v)
            qval = s_v.n

            nI = ncl // 128
            for w0 in range(0, nI, 8):
                w1 = min(w0 + 8, nI)

                @blk.tensor
                def _(tensor, w0=w0, w1=w1, qval=qval):
                    tensor.wait_ge(s_v.h, qval)
                    self.wait_guard(tensor, "PS_D")
                    for j in range(w0, w1):
                        inst = nc.tensor.transpose(
                            self.PS_D[:, 128 * (j - w0):128 * (j - w0) + 128],
                            qbf[:, 128 * j:128 * j + 128], self.identbf[:])
                    s_mm.inc(inst)

                if wT_dst is not None:
                    @blk.vector
                    def _(vector, r0=r0, w0=w0, w1=w1):
                        vector.wait_ge(s_mm.h, s_mm.n)
                        dv_ = wT_dst[:, :].rearrange(
                            "p (ci f) -> p ci f", f=O)[
                            :, w0:w1, r0:r0 + 128]
                        sv_ = self.PS_D[:, 0:128 * (w1 - w0)].rearrange(
                            "p (kk c) -> p kk c", c=128)
                        s_v.inc(vector.tensor_copy(dv_, sv_))
                    self.set_guard("PS_D", s_v)
                else:
                    @blk.vector
                    def _(vector, w0=w0, w1=w1):
                        vector.wait_ge(s_mm.h, s_mm.n)
                        s_v.inc(vector.tensor_copy(
                            wps[:, 128 * w0:128 * w1],
                            self.PS_D[:, 0:128 * (w1 - w0)]))
                    self.set_guard("PS_D", s_v)
            if cache_dst is not None:
                wval = s_v.n

                @blk.sync
                def _(sync, r0=r0, c0=c0, nI=nI, wval=wval):
                    sync.wait_ge(s_v.h, wval)
                    f0 = c0 // 128
                    dst = cache_dst.rearrange("(fj p) d -> p fj d", p=128)[
                        :, f0:f0 + nI, r0:r0 + 128]
                    src = wps[:, 0:128 * nI].rearrange("p (fj d) -> p fj d",
                                                       d=128)
                    self.s_out.inc(sync.dma_start(out=dst, in_=src), 16)
                self.set_guard("qbf", self.s_out)
            else:
                self.set_guard("qbf", self.s_mm)


_consts_cache = None


def _consts():
    global _consts_cache
    if _consts_cache is None:
        ident32 = np.eye(128, dtype=np.float32)
        identbf = np.eye(128).astype(ml_dtypes.bfloat16)
        mask = np.zeros((128, 128), np.float32)
        for p in range(128):
            mask[p, p + 1:] = -1e30
        _consts_cache = dict(
            ident32=ident32, identbf=identbf,
            maskbf=mask.astype(ml_dtypes.bfloat16),
            onesbf=np.ones((1, 128), ml_dtypes.bfloat16))
    return _consts_cache


_BLOCK_INPUTS = {
    0: {"W_attn": "L0_attn_W_attn", "W_proj0": "L0_attn_W_proj"},
    1: {"W_r": "L0_ffn_W_r", "W_k": "L0_ffn_W_k", "W_v": "L0_ffn_W_v",
        "mu_r": "L0_ffn_mu_r", "mu_k": "L0_ffn_mu_k"},
    2: {"Wq": "L1_attn_Wq", "Wk": "L1_attn_Wk", "Wv": "L1_attn_Wv",
        "bq": "L1_attn_bq", "bk": "L1_attn_bk", "bv": "L1_attn_bv"},
    3: {"W_fc": "L1_ffn_W_fc", "b_fc": "L1_ffn_b_fc", "A_fc": "L1_ffn_A_fc",
        "B_fc": "L1_ffn_B_fc", "W_proj1": "L1_ffn_W_proj",
        "b_proj": "L1_ffn_b_proj", "A_proj": "L1_ffn_A_proj",
        "B_proj": "L1_ffn_B_proj"},
}

_prog_cache = {}


def _get_prog(blocks):
    key = tuple(blocks)
    if key not in _prog_cache:
        _prog_cache[key] = B(list(blocks)).build()
    return _prog_cache[key]


def run_blocks(blocks, x, inputs):
    nc = _get_prog(blocks)
    cns = _consts()
    maps = []
    for c in range(NCORE):
        m = dict(x_in=np.ascontiguousarray(x[c]))
        m.update(cns)
        for b in blocks:
            for kk, src in _BLOCK_INPUTS[b].items():
                m[kk] = np.ascontiguousarray(inputs[src])
        maps.append(m)
    res = run_bass_kernel_spmd(nc, maps, core_ids=list(range(NCORE)))
    return np.stack([np.asarray(r["y_out"]) for r in res.results], 0)


def kernel(**inputs):
    x = np.ascontiguousarray(inputs["L0_attn_x"], np.float32)
    y = run_blocks((0, 1, 2, 3), x, inputs)
    return y.astype(np.float32)


# version 20
# speedup vs baseline: 1.0173x; 1.0031x over previous
import numpy as np
import ml_dtypes
from contextlib import ExitStack
import concourse.bass as bass
import concourse.mybir as mybir
from concourse.bass_utils import run_bass_kernel_spmd

F32 = mybir.dt.float32
BF16 = mybir.dt.bfloat16
AF = mybir.ActivationFunctionType
ALU = mybir.AluOpType
AX = mybir.AxisListType

T, C = 2048, 1024
NT = T // 128
NCH = C // 128
XO = 4
XCW = T + XO
MAGIC = 12582912.0
NCORE = 8


class Ctr:
    __slots__ = ("h", "n")

    def __init__(self, h):
        self.h = h
        self.n = 0

    def inc(self, inst, k=1):
        inst.then_inc(self.h, k)
        self.n += k
        return inst




class _Rec:
    __slots__ = ("meth", "args", "kwargs", "incs")

    def __init__(self, meth, args, kwargs):
        self.meth = meth
        self.args = args
        self.kwargs = kwargs
        self.incs = []

    def then_inc(self, h, k):
        self.incs.append((h, k))
        return self


class _ShimEngine:
    def __init__(self, recs):
        self._recs = recs

    def wait_ge(self, h, v):
        r = _Rec("wait_ge", (h, v), {})
        self._recs.append(r)
        return r

    def __getattr__(self, meth):
        def call(*a, **k):
            r = _Rec(meth, a, k)
            self._recs.append(r)
            return r
        return call


class _FakeBlk:

    def __init__(self, shims):
        for name, sh in shims.items():
            setattr(self, name, (lambda sh: (lambda fn: fn(sh)))(sh))


class _ProxyNC:

    def __init__(self, real, shims):
        object.__setattr__(self, "_real", real)
        object.__setattr__(self, "_shims", shims)

    def __getattr__(self, name):
        sh = self._shims.get(name)
        if sh is not None:
            return sh
        return getattr(self._real, name)


class B:
    def __init__(self, blocks):
        self.blocks = sorted(blocks)

    def build(self):
        nc = self.nc = bass.Bass("TRN2", target_bir_lowering=False)
        io = self.io = {}

        def dram(name, shape, dt=F32, kind="ExternalInput"):
            io[name] = nc.dram_tensor(name, shape, dt, kind=kind).ap()

        dram("x_in", [T, C])
        dram("y_out", [T, C], kind="ExternalOutput")
        dram("ident32", [128, 128])
        dram("identbf", [128, 128], BF16)
        dram("maskbf", [128, 128], BF16)
        dram("onesbf", [1, 128], BF16)
        nblk = len(self.blocks)
        self.xbuf, self.xsrc = {}, {}
        prev = io["x_in"]
        for i, b in enumerate(self.blocks):
            self.xsrc[b] = prev
            if i == nblk - 1:
                self.xbuf[b] = io["y_out"]
            else:
                dram(f"xb{b}", [T, C], kind="Internal")
                self.xbuf[b] = io[f"xb{b}"]
            prev = self.xbuf[b]
        if 0 in self.blocks:
            dram("W_attn", [C, 3 * C])
            dram("W_proj0", [C, C])
        if 1 in self.blocks:
            dram("W_r", [C, C]); dram("W_k", [C, C]); dram("W_v", [C, C])
            dram("mu_r", [C]); dram("mu_k", [C])
        if 2 in self.blocks:
            dram("Wq", [C, C]); dram("Wk", [C, C]); dram("Wv", [C, C])
            dram("bq", [C]); dram("bk", [C]); dram("bv", [C])
        if 3 in self.blocks:
            dram("W_fc", [4 * C, C]); dram("b_fc", [4 * C])
            dram("A_fc", [16, C]); dram("B_fc", [4 * C, 16])
            dram("W_proj1", [C, 4 * C]); dram("b_proj", [C])
            dram("A_proj", [16, 4 * C]); dram("B_proj", [C, 16])
            dram("wpcache", [4 * C, C], BF16, kind="Internal")

        with ExitStack() as ctx:
            self.ctx = ctx
            self.guard = {}
            self.alloc_common()
            self.prologue()
            engs = ("tensor", "vector", "scalar", "gpsimd", "sync")
            self.chunks = []
            self._lists = {e: [] for e in engs}
            shims = {e: _ShimEngine(self._lists[e]) for e in engs}
            self._engs = engs
            self._shims = shims
            self.cut()
            realnc = self.nc
            self.nc = _ProxyNC(realnc, shims)
            self.blk = _FakeBlk(shims)
            self.load_consts()
            self.phase_build_xt(self.xsrc[self.blocks[0]])
            for b in self.blocks:
                self.barrier()
                self.cut()
                getattr(self, f"phase_block{b}")()
            self.barrier()
            self.epilogue()
            self.cut()
            self.nc = realnc
            with realnc.Block() as blk:
                for chunk in self.chunks:
                    for e in engs:
                        if not chunk[e]:
                            continue

                        def body(eng, recs=chunk[e]):
                            for r in recs:
                                if r.meth == "wait_ge":
                                    eng.wait_ge(*r.args)
                                else:
                                    inst = getattr(eng, r.meth)(*r.args,
                                                                **r.kwargs)
                                    for h, k in r.incs:
                                        inst.then_inc(h, k)
                        getattr(blk, e)(body)
        return nc

    def cut(self):
        chunk = {}
        for e in self._engs:
            lst = self._lists[e]
            chunk[e] = lst
            self._lists[e] = []
            self._shims[e]._recs = self._lists[e]
        if any(chunk.values()):
            self.chunks.append(chunk)

    def sb(self, name, shape, dt=F32, ctx=None):
        return (ctx or self.ctx).enter_context(
            self.nc.sbuf_tensor("sb_" + name, shape, dt))

    def ps(self, name, shape, dt=F32):
        return self.ctx.enter_context(self.nc.psum_tensor(name, shape, dt))

    def sem(self, name):
        return Ctr(self.ctx.enter_context(self.nc.semaphore(name)))

    def alloc_common(self):
        self.s_dma = self.sem("s_dma")
        self.s_gdma = self.sem("s_gdma")
        self.s_mm = self.sem("s_mm")
        self.s_v = self.sem("s_v")
        self.s_act = self.sem("s_act")
        self.s_out = self.sem("s_out")
        self.s_x = [self.sem("s_x0"), self.sem("s_x1")]
        self.s_w = [self.sem("s_w0"), self.sem("s_w1")]
        self.s_wld = self.sem("s_wld")
        self.s_p = [self.sem("s_p0"), self.sem("s_p1"),
                    self.sem("s_p2"), self.sem("s_p3")]
        self.s_pool = self.sem("s_pool")
        self.all_sems = [self.s_dma, self.s_gdma, self.s_mm, self.s_v,
                         self.s_act, self.s_out, self.s_wld,
                         self.s_x[0], self.s_x[1], self.s_w[0], self.s_w[1],
                         self.s_p[0], self.s_p[1], self.s_p[2], self.s_p[3],
                         self.s_pool]
        self.ident32 = self.sb("ident32", [128, 128])
        self.identbf = self.sb("identbf", [128, 128], BF16)
        self.maskbf = self.sb("maskbf", [128, 128], BF16)
        self.onesbf = self.sb("onesbf", [1, 128], BF16)
        self.cols = self.sb("cols", [128, 8])
        self.lncols = self.sb("lncols", [128, 48])
        self.atcols = self.sb("atcols", [128, 24])
        self.attn_it = 0
        self.xstage = self.sb("xstage", [128, 2 * C])
        self.lnout = self.sb("lnout", [128, C])
        self.sqtmp = self.sb("sqtmp", [128, C], BF16)
        self.ffntmp = self.sb("ffntmp", [128, C])
        self.xT = self.sb("xT", [128, NCH * XCW], BF16)
        if 0 in self.blocks or 2 in self.blocks:
            self.P_s2 = [self.sb("P_s", [128, T], BF16),
                         self.sb("P_sb", [128, T], BF16)]
            self.PT = self.sb("PT", [128, T], BF16)
        self.PS_A = self.ps("PS_A", [128, 2048])
        self.PS_B = self.ps("PS_B", [128, 1024])
        self.PS_C = self.ps("PS_C", [128, 512])
        self.PS_D = self.ps("PS_D", [128, 1024], BF16)
        self.PS_A1k = self.PS_A[:, 0:1024]
        self.PS_B1k = self.PS_B[:, 0:1024]

    def prologue(self):
        for s in self.all_sems:
            self.nc.gpsimd.sem_clear(s.h)
        self.nc.all_engine_barrier()

    def epilogue(self):
        s_out = self.s_out

        @self.blk.sync
        def _(sync):
            sync.wait_ge(s_out.h, s_out.n)

    def barrier(self):
        blk = self.blk
        vals = [(c.h, c.n) for c in self.all_sems]
        for eng in ("tensor", "vector", "scalar", "sync", "gpsimd"):
            def mk(e, vals=vals):
                for h, v in vals:
                    e.wait_ge(h, v)
            getattr(blk, eng)(mk)

    def vsync(self, vector, inst):
        self.s_v.inc(inst)
        vector.wait_ge(self.s_v.h, self.s_v.n)

    def vpad(self, vector):
        return vector.tensor_copy(self.cols[:, 4:5], self.cols[:, 3:4])

    def spad(self, scalar):
        return scalar.copy(self.cols[:, 5:6], self.cols[:, 3:4])

    _PSA_KEYS = ("PS_A", "PSAh0", "PSAh1")
    _PSB_KEYS = ("PS_B", "PSBh0", "PSBh1")

    def wait_guard(self, eng, key):
        if key in self._PSA_KEYS:
            keys = self._PSA_KEYS
        elif key in self._PSB_KEYS:
            keys = self._PSB_KEYS
        else:
            keys = (key,)
        for kk2 in keys:
            g = self.guard.get(kk2)
            if g is not None:
                eng.wait_ge(g[0].h, g[1])

    def set_guard(self, key, ctr):
        self.guard[key] = (ctr, ctr.n)

    def load_consts(self):
        blk, io = self.blk, self.io
        s_dma, s_v = self.s_dma, self.s_v

        @blk.sync
        def _(sync):
            s_dma.inc(sync.dma_start(out=self.ident32[:], in_=io["ident32"]), 16)
            s_dma.inc(sync.dma_start(out=self.identbf[:], in_=io["identbf"]), 16)
            s_dma.inc(sync.dma_start(out=self.maskbf[:], in_=io["maskbf"]), 16)
            s_dma.inc(sync.dma_start(out=self.onesbf[:], in_=io["onesbf"]), 16)

        @blk.vector
        def _(vector):
            vector.memset(self.cols[:, 0:1], 1e-5)
            vector.memset(self.cols[:, 1:2], 0.125)
            vector.memset(self.cols[:, 2:3], 0.03125)
            vector.memset(self.cols[:, 6:7], -0.5)
            vector.memset(self.lncols[:], 0.0)
            vector.memset(self.atcols[:], 0.0)
            for ci in range(NCH):
                vector.memset(self.xT[:, XCW * ci:XCW * ci + XO], 0.0)
            if 0 in self.blocks or 2 in self.blocks:
                vector.memset(self.P_s2[0][:, 0:2], 0.0)
            s_v.inc(vector.memset(self.cols[:, 3:4], 1.0))

    def xtv(self):
        return self.xT[:, :].rearrange("p (ci w) -> p ci w", w=XCW)

    def emit_xt_tile(self, src2d, ti, pre=()):
        blk, nc = self.blk, self.nc
        s_mm, s_v = self.s_mm, self.s_v
        for half in (0, 1):
            @blk.tensor
            def _(tensor, half=half):
                if half == 0:
                    for cc, vv in pre:
                        tensor.wait_ge(cc.h, vv)
                self.wait_guard(tensor, "PS_C")
                for k in range(4):
                    ci = 4 * half + k
                    inst = nc.tensor.transpose(
                        self.PS_C[:, 128 * k:128 * k + 128],
                        src2d[:, 128 * ci:128 * ci + 128], self.ident32[:])
                s_mm.inc(inst)

            @blk.vector
            def _(vector, half=half):
                vector.wait_ge(s_mm.h, s_mm.n)
                dst = self.xtv()[:, 4 * half:4 * half + 4,
                                 XO + 128 * ti:XO + 128 * ti + 128]
                src = self.PS_C[:, :].rearrange("p (k t) -> p k t", t=128)
                s_v.inc(vector.tensor_copy(dst, src))
            self.set_guard("PS_C", s_v)

    def phase_build_xt(self, xsrc):
        cval = self.s_dma.n
        for ti in range(NT):
            sx, dv = self.prefetch_x(xsrc, ti)
            k = ti % 2
            self.emit_xt_tile(self.xstage[:, C * k:C * k + C], ti,
                              pre=[(sx, dv), (self.s_dma, cval)])
            self.set_guard(f"xst{k}", self.s_mm)

    def prefetch_x(self, xsrc, ti):
        blk = self.blk
        k = ti % 2
        sx = self.s_x[k]

        @blk.sync
        def _(sync):
            self.wait_guard(sync, f"xst{k}")
            sx.inc(sync.dma_start(
                out=self.xstage[:, C * k:C * k + C],
                in_=xsrc[128 * ti:128 * ti + 128, :]), 16)
        return (sx, sx.n)

    def emit_ln_tile(self, ti, psum_ap, out_hbm, build_xt, mm_val,
                     produce_y=None, dma_val=None):
        blk = self.blk
        s_v, s_act, s_dma, s_out = self.s_v, self.s_act, self.s_dma, self.s_out
        xsl = self.xstage[:, C * (ti % 2):C * (ti % 2) + C]
        lc = 12 * (ti % 4)
        L = self.lncols

        @blk.vector
        def _(vector):
            vector.wait_ge(dma_val[0].h, dma_val[1])
            vector.wait_ge(self.s_mm.h, mm_val)
            if produce_y is None:
                self.vsync(vector, vector.tensor_add(xsl, xsl, psum_ap))
            else:
                produce_y(vector, xsl)
            s_v.inc(vector.reduce_sum(L[:, lc:lc + 1], xsl, axis=AX.X))
        self.set_guard(self._ln_psum_key, s_v)

        @blk.scalar
        def _(scalar):
            scalar.wait_ge(s_v.h, s_v.n)
            self.wait_guard(scalar, "ffntmp")
            s_act.inc(scalar.activation(self.ffntmp[:], xsl, AF.Square))

        @blk.vector
        def _(vector):
            vector.wait_ge(s_act.h, s_act.n)
            self.vsync(vector, vector.reduce_sum(L[:, lc + 1:lc + 2],
                                                 self.ffntmp[:], axis=AX.X))
            vector.tensor_scalar(L[:, lc + 2:lc + 3], L[:, lc:lc + 1],
                                 1.0 / C, None, ALU.mult)
            self.vsync(vector, vector.tensor_scalar(
                L[:, lc + 3:lc + 4], L[:, lc + 1:lc + 2],
                1.0 / C, None, ALU.mult))
            self.vsync(vector, vector.tensor_scalar(
                L[:, lc + 4:lc + 5], L[:, lc + 2:lc + 3],
                L[:, lc + 2:lc + 3], None, ALU.mult))
            s_v.inc(vector.tensor_sub(L[:, lc + 5:lc + 6], L[:, lc + 3:lc + 4],
                                      L[:, lc + 4:lc + 5]))
        self.set_guard("ffntmp", s_v)

        @blk.scalar
        def _(scalar):
            scalar.wait_ge(s_v.h, s_v.n)
            s_act.inc(scalar.activation(L[:, lc + 6:lc + 7],
                                        L[:, lc + 5:lc + 6],
                                        AF.Ln, bias=self.cols[:, 0:1]))
            scalar.wait_ge(s_act.h, s_act.n)
            s_act.inc(scalar.activation(L[:, lc + 7:lc + 8],
                                        L[:, lc + 6:lc + 7],
                                        AF.Exp, scale=self.cols[:, 6:7]))

        @blk.vector
        def _(vector):
            vector.wait_ge(s_act.h, s_act.n)
            s_v.inc(vector.tensor_scalar(
                L[:, lc + 8:lc + 9], L[:, lc + 2:lc + 3],
                L[:, lc + 7:lc + 8], -1.0, ALU.mult, ALU.mult))

        @blk.scalar
        def _(scalar):
            scalar.wait_ge(s_v.h, s_v.n)
            self.wait_guard(scalar, "lnout_mm")
            self.wait_guard(scalar, "lnout_out")
            s_act.inc(scalar.activation(self.lnout[:], xsl, AF.Identity,
                                        scale=L[:, lc + 7:lc + 8],
                                        bias=L[:, lc + 8:lc + 9]))
        self.set_guard(f"xst{ti % 2}", s_act)
        act_val = s_act.n

        @blk.sync
        def _(sync):
            sync.wait_ge(s_act.h, act_val)
            s_out.inc(sync.dma_start(
                out=out_hbm[128 * ti:128 * ti + 128, :], in_=self.lnout[:]), 16)
        self.set_guard("lnout_out", s_out)
        if build_xt:
            self.emit_xt_tile(self.lnout[:], ti, pre=[(s_act, act_val)])
            self.set_guard("lnout_mm", self.s_mm)

    def attn_pre(self, qi, scale_col, emit_S):
        blk = self.blk
        s_mm, s_v, s_act = self.s_mm, self.s_v, self.s_act
        W = (qi + 1) * 128
        halves = [(0, min(W, 1024))]
        if W > 1024:
            halves.append((1024, W - 1024))
        a0 = 12 * (self.attn_it % 2)
        P_s = self.P_s2[self.attn_it % 2]
        pskey = f"P_s{self.attn_it % 2}"
        self.attn_it += 1
        A = self.atcols
        for sh, (c0, cw) in enumerate(halves):
            it2 = self.attn_it * 2 + sh
            poff = 1024 * (it2 % 2)
            pk = f"PSAh{it2 % 2}"

            @blk.tensor
            def _(tensor, c0=c0, cw=cw, poff=poff, pk=pk, W=W):
                self.wait_guard(tensor, pk)
                emit_S(tensor, c0, cw, poff, c0 + cw == W)

            @blk.vector
            def _(vector, sh=sh, poff=poff, cw=cw):
                vector.wait_ge(s_mm.h, s_mm.n)
                self.vsync(vector, vector.reduce_max(
                    A[:, a0 + sh:a0 + sh + 1],
                    self.PS_A[:, poff:poff + cw], axis=AX.X))
                s_v.inc(vector.tensor_scalar(
                    A[:, a0 + 2 + sh:a0 + 3 + sh], A[:, a0 + sh:a0 + sh + 1],
                    scale_col, -1.0, ALU.mult, ALU.mult))

            @blk.scalar
            def _(scalar, sh=sh, c0=c0, cw=cw, poff=poff, P_s=P_s, pskey=pskey):
                scalar.wait_ge(s_v.h, s_v.n)
                if sh == 0:
                    self.wait_guard(scalar, pskey)
                s_act.inc(scalar.activation(
                    P_s[:, c0:c0 + cw], self.PS_A[:, poff:poff + cw],
                    AF.Exp, scale=scale_col, bias=A[:, a0 + 2 + sh:a0 + 3 + sh]))
            self.set_guard(pk, s_act)
        return dict(qi=qi, W=W, halves=halves, a0=a0, P_s=P_s, pskey=pskey,
                    scale_col=scale_col, exp_val=s_act.n)

    def attn_post(self, ctx):
        blk = self.blk
        s_v, s_act, s_pool = self.s_v, self.s_act, self.s_pool
        qi, W, halves, a0 = ctx["qi"], ctx["W"], ctx["halves"], ctx["a0"]
        P_s, scale_col = ctx["P_s"], ctx["scale_col"]
        A = self.atcols

        @blk.vector
        def _(vector):
            vector.wait_ge(s_act.h, ctx["exp_val"])
            for sh, (c0, cw) in enumerate(halves):
                self.vsync(vector, vector.reduce_sum(
                    A[:, a0 + 4 + sh:a0 + 5 + sh],
                    P_s[:, c0:c0 + cw], axis=AX.X))
            if len(halves) == 1:
                self.vsync(vector, vector.reciprocal(A[:, a0 + 6:a0 + 7],
                                                     A[:, a0 + 4:a0 + 5]))
                s_v.inc(vector.tensor_scalar(
                    P_s[:, 0:W], P_s[:, 0:W],
                    A[:, a0 + 6:a0 + 7], None, ALU.mult))
            else:
                self.vsync(vector, vector.tensor_max(
                    A[:, a0 + 6:a0 + 7], A[:, a0:a0 + 1], A[:, a0 + 1:a0 + 2]))
                s_v.inc(vector.tensor_scalar(
                    A[:, a0 + 7:a0 + 9], A[:, a0:a0 + 2],
                    A[:, a0 + 6:a0 + 7], scale_col, ALU.subtract, ALU.mult))
        if len(halves) > 1:
            @blk.scalar
            def _(scalar):
                scalar.wait_ge(s_v.h, s_v.n)
                s_act.inc(scalar.activation(A[:, a0 + 9:a0 + 11],
                                            A[:, a0 + 7:a0 + 9], AF.Exp))

            @blk.vector
            def _(vector):
                vector.wait_ge(s_act.h, s_act.n)
                self.vsync(vector, vector.tensor_mul(
                    A[:, a0 + 7:a0 + 9], A[:, a0 + 9:a0 + 11],
                    A[:, a0 + 4:a0 + 6]))
                self.vsync(vector, vector.tensor_add(
                    A[:, a0 + 11:a0 + 12], A[:, a0 + 7:a0 + 8],
                    A[:, a0 + 8:a0 + 9]))
                self.vsync(vector, vector.reciprocal(A[:, a0 + 6:a0 + 7],
                                                     A[:, a0 + 11:a0 + 12]))
                self.vsync(vector, vector.tensor_scalar(
                    A[:, a0 + 2:a0 + 4], A[:, a0 + 9:a0 + 11],
                    A[:, a0 + 6:a0 + 7], None, ALU.mult))
                vector.tensor_scalar(P_s[:, 0:1024], P_s[:, 0:1024],
                                     A[:, a0 + 2:a0 + 3], None, ALU.mult)
                s_v.inc(vector.tensor_scalar(
                    P_s[:, 1024:W], P_s[:, 1024:W],
                    A[:, a0 + 3:a0 + 4], None, ALU.mult))
        return s_v.n

    def emit_p_transposes(self, qi, pnorm_val, P_s=None, pskey="P_s"):
        if P_s is None:
            P_s = self.P_s2[0]
        blk, nc = self.blk, self.nc
        s_mm, s_v = self.s_mm, self.s_v
        for w0 in range(0, qi + 1, 8):
            w1 = min(w0 + 8, qi + 1)

            @blk.tensor
            def _(tensor, w0=w0, w1=w1):
                tensor.wait_ge(s_v.h, pnorm_val)
                self.wait_guard(tensor, "PS_D")
                for si in range(w0, w1):
                    inst = nc.tensor.transpose(
                        self.PS_D[:, 128 * (si - w0):128 * (si - w0) + 128],
                        P_s[:, 128 * si:128 * si + 128], self.identbf[:])
                s_mm.inc(inst)

            @blk.scalar
            def _(scalar, w0=w0, w1=w1):
                scalar.wait_ge(self.s_mm.h, self.s_mm.n)
                self.s_act.inc(scalar.activation(
                    self.PT[:, 128 * w0:128 * w1],
                    self.PS_D[:, 0:128 * (w1 - w0)], AF.Identity))
            self.set_guard("PS_D", self.s_act)
        self.set_guard(pskey, self.s_act)
        return ("act", self.s_act.n)

    def emit_wT_transposes(self, wstage, dst, wval, busykey=None,
                           dma_ctr=None):
        blk, nc = self.blk, self.nc
        s_mm, s_v = self.s_mm, self.s_v
        for dj in range(NCH):
            @blk.tensor
            def _(tensor, dj=dj):
                tensor.wait_ge((dma_ctr or self.s_wld).h, wval)
                self.wait_guard(tensor, "PS_D")
                if dj == 0 and busykey:
                    self.wait_guard(tensor, busykey)
                for ci in range(NCH):
                    inst = nc.tensor.transpose(
                        self.PS_D[:, 128 * ci:128 * ci + 128],
                        wstage[:, 1024 * dj + 128 * ci:
                               1024 * dj + 128 * ci + 128],
                        self.identbf[:])
                s_mm.inc(inst)

            @blk.vector
            def _(vector, dj=dj):
                vector.wait_ge(s_mm.h, s_mm.n)
                dv_ = dst[:, :].rearrange("p (ci d) -> p ci d", d=1024)[
                    :, :, 128 * dj:128 * dj + 128]
                sv_ = self.PS_D[:, :].rearrange("p (ci c) -> p ci c", c=128)
                s_v.inc(vector.tensor_copy(dv_, sv_))
            self.set_guard("PS_D", s_v)
        return s_v.n

    def phase_block0(self):
        blk, nc, io = self.blk, self.nc, self.io
        s_dma, s_gdma, s_mm, s_v, s_act = (self.s_dma, self.s_gdma, self.s_mm,
                                           self.s_v, self.s_act)
        with ExitStack() as pctx:
            qTp = [self.sb(f"qTp{k}", [128, T], BF16, pctx) for k in (0, 1)]
            kTp = [self.sb(f"kTp{k}", [128, T], BF16, pctx) for k in (0, 1)]
            vTp = [self.sb(f"vTp{k}", [128, T], BF16, pctx) for k in (0, 1)]
            wAq = [self.sb(f"wAq{k}", [128, NCH * 128], BF16, pctx)
                   for k in (0, 1)]
            wAk = [self.sb(f"wAk{k}", [128, NCH * 128], BF16, pctx)
                   for k in (0, 1)]
            wAv = [self.sb(f"wAv{k}", [128, NCH * 128], BF16, pctx)
                   for k in (0, 1)]
            oT = self.sb("oT", [128, NCH * T], BF16, pctx)
            wP = self.sb("wP", [128, NCH * 1024], BF16, pctx)

            WA = io["W_attn"].rearrange("(ci p) e -> p ci e", p=128)
            for p in range(8):
                self.cut()
                k = p % 2

                sw = self.s_w[k]

                @blk.gpsimd
                def _(gpsimd, p=p, k=k, sw=sw):
                    self.wait_guard(gpsimd, f"wA{k}")
                    for dst, off in ((wAq[k], 0), (wAk[k], C), (wAv[k], 2 * C)):
                        sw.inc(gpsimd.dma_start(
                            out=dst[:, :].rearrange("p (ci c) -> p ci c", c=128),
                            in_=WA[:, :, off + 128 * p:off + 128 * p + 128]), 16)
                waval = sw.n

                for wi2, (wsrc, dst) in enumerate(
                        ((wAq[k], qTp[k]), (wAk[k], kTp[k]))):
                    for hh in range(2):
                        par = (2 * wi2 + hh) % 2
                        poff = 1024 * par

                        @blk.tensor
                        def _(tensor, wsrc=wsrc, waval=waval, k=k, sw=sw,
                              hh=hh, poff=poff, par=par):
                            tensor.wait_ge(sw.h, waval)
                            self.wait_guard(tensor, f"PSAh{par}")
                            self.wait_guard(tensor, f"qkv{k}")
                            for ci in range(NCH):
                                for n in range(2):
                                    inst = nc.tensor.matmul(
                                        self.PS_A[:, poff + 512 * n:
                                                  poff + 512 * n + 512],
                                        wsrc[:, 128 * ci:128 * ci + 128],
                                        self.xT[:, XCW * ci + XO + 1024 * hh
                                                + 512 * n:
                                                XCW * ci + XO + 1024 * hh
                                                + 512 * n + 512],
                                        start=(ci == 0), stop=(ci == NCH - 1))
                            s_mm.inc(inst)

                        @blk.scalar
                        def _(scalar, dst=dst, hh=hh, poff=poff):
                            scalar.wait_ge(s_mm.h, s_mm.n)
                            s_act.inc(scalar.activation(
                                dst[:, 1024 * hh:1024 * hh + 1024],
                                self.PS_A[:, poff:poff + 1024], AF.Identity))
                        self.set_guard(f"PSAh{par}", s_act)

                for g in range(4):
                    par = g % 2
                    boff = 512 * par

                    @blk.tensor
                    def _(tensor, g=g, k=k, waval=waval, sw=sw, boff=boff,
                          par=par):
                        tensor.wait_ge(sw.h, waval)
                        self.wait_guard(tensor, f"PSBh{par}")
                        self.wait_guard(tensor, f"qkv{k}")
                        for j in range(4):
                            ti = 4 * g + j
                            for ci in range(NCH):
                                inst = nc.tensor.matmul(
                                    self.PS_B[:, boff + 128 * j:
                                              boff + 128 * j + 128],
                                    self.xT[:, XCW * ci + XO + 128 * ti:
                                            XCW * ci + XO + 128 * ti + 128],
                                    wAv[k][:, 128 * ci:128 * ci + 128],
                                    start=(ci == 0), stop=(ci == NCH - 1))
                        s_mm.inc(inst)

                    @blk.scalar
                    def _(scalar, g=g, k=k, boff=boff):
                        scalar.wait_ge(s_mm.h, s_mm.n)
                        s_act.inc(scalar.activation(
                            vTp[k][:, 512 * g:512 * g + 512],
                            self.PS_B[:, boff:boff + 512], AF.Identity))
                    self.set_guard(f"PSBh{par}", s_act)
                self.set_guard(f"wA{k}", s_act)

                def finish0(ctx):
                    qi, hl, kk = ctx["qi"], ctx["hl"], ctx["k"]
                    pp = ctx["p"]
                    pn = self.attn_post(ctx)
                    self.emit_p_transposes(qi, pn, ctx["P_s"], ctx["pskey"])
                    qc, qr = qi // 4, qi % 4

                    @blk.tensor
                    def _(tensor, qi=qi, kk=kk, hl=hl, qr=qr):
                        tensor.wait_ge(s_v.h, s_v.n)
                        tensor.wait_ge(s_act.h, s_act.n)
                        if qr == 0:
                            self.wait_guard(tensor, "PS_C")
                        for si in range(qi + 1):
                            inst = nc.tensor.matmul(
                                self.PS_C[64 * hl:64 * hl + 64,
                                          128 * qr:128 * qr + 128],
                                vTp[kk][:, 128 * si + 64 * hl:
                                        128 * si + 64 * hl + 64],
                                self.PT[:, 128 * si:128 * si + 128],
                                start=(si == 0), stop=(si == qi),
                                tile_position=(0, 64 * hl))
                        s_mm.inc(inst)

                    if qr == 3:
                        @blk.scalar
                        def _(scalar, pp=pp, hl=hl, qc=qc):
                            scalar.wait_ge(s_mm.h, s_mm.n)
                            s_act.inc(scalar.activation(
                                oT[64 * hl:64 * hl + 64,
                                   T * pp + 512 * qc:T * pp + 512 * qc + 512],
                                self.PS_C[64 * hl:64 * hl + 64, 0:512],
                                AF.Identity))
                        self.set_guard("PS_C", s_act)

                pend = None
                for hl in (0, 1):
                    hs = slice(64 * hl, 64 * hl + 64)
                    for qi in range(NT):
                        def emit_S(tensor, c0, cw, poff, has_diag,
                                   k=k, hs=hs, hl=hl, qi=qi):
                            for n0 in range(0, cw, 512):
                                nw = min(512, cw - n0)
                                inst = nc.tensor.matmul(
                                    self.PS_A[:, poff + n0:poff + n0 + nw],
                                    qTp[k][hs, 128 * qi:128 * qi + 128],
                                    kTp[k][hs, c0 + n0:c0 + n0 + nw],
                                    start=True, stop=False,
                                    tile_position=(64 * hl, 0))
                            if has_diag:
                                inst = nc.tensor.matmul(
                                    self.PS_A[:, poff + cw - 128:poff + cw],
                                    self.identbf[:], self.maskbf[:],
                                    start=False, stop=True)
                            s_mm.inc(inst)

                        ctx = self.attn_pre(qi, self.cols[:, 1:2], emit_S)
                        ctx.update(hl=hl, k=k, p=p)
                        if pend is not None:
                            finish0(pend)
                        pend = ctx
                finish0(pend)
                self.set_guard(f"qkv{k}", s_mm)

            @blk.gpsimd
            def _(gpsimd):
                s_gdma.inc(gpsimd.dma_start(
                    out=wP[:, :].rearrange("p (ci e) -> p ci e", e=1024),
                    in_=io["W_proj0"].rearrange("(ci p) e -> p ci e", p=128)),
                    16)
            wpval = s_gdma.n

            xb, xsrc = self.xbuf[0], self.xsrc[0]
            build_xt = (1 in self.blocks)
            for ti in range(NT):
                dv = self.prefetch_x(xsrc, ti)
                pa = self.PS_B1k if ti % 2 == 0 else self.PS_A1k
                pkey = "PS_B" if ti % 2 == 0 else "PS_A"
                self._ln_psum_key = pkey

                @blk.tensor
                def _(tensor, ti=ti, pa=pa, pkey=pkey, wpval=wpval):
                    tensor.wait_ge(s_gdma.h, wpval)
                    self.wait_guard(tensor, pkey)
                    for ci in range(NCH):
                        for n in range(2):
                            inst = nc.tensor.matmul(
                                pa[:, 512 * n:512 * n + 512],
                                oT[:, T * ci + 128 * ti:T * ci + 128 * ti + 128],
                                wP[:, 1024 * ci + 512 * n:
                                   1024 * ci + 512 * n + 512],
                                start=(ci == 0), stop=(ci == NCH - 1))
                    s_mm.inc(inst)

                self.emit_ln_tile(ti, pa, xb, build_xt, s_mm.n, dma_val=dv)

    def phase_block1(self):
        blk, nc, io = self.blk, self.nc, self.io
        s_dma, s_gdma, s_mm, s_v, s_act = (self.s_dma, self.s_gdma, self.s_mm,
                                           self.s_v, self.s_act)
        dbg = getattr(self, "debug_b1", False)
        with ExitStack() as pctx:
            if dbg:
                pctx = self.ctx
            wrT = self.sb("wrT", [128, NCH * 1024], BF16, pctx)
            wkT = self.sb("wkT", [128, NCH * 1024], BF16, pctx)
            wvT = self.sb("wvT", [128, NCH * 1024], BF16, pctx)
            wstage = self.sb("wstage", [128, 2 * NCH * 1024], BF16, pctx)
            xrS = self.sb("xrS", [128, 2 * NCH * 512], BF16, pctx)
            xkS = self.sb("xkS", [128, 2 * NCH * 512], BF16, pctx)
            rS = self.sb("rS", [128, 4 * 1024], BF16, pctx)
            kS = self.sb("kS", [128, NCH * 512], BF16, pctx)
            mixtmp = self.sb("mixtmp", [128, 512], BF16, pctx)
            murow = self.sb("murow", [8, 256], F32, pctx)
            mucols = self.sb("mucols", [128, 8], F32, pctx)
            mu1cols = self.sb("mu1cols", [128, 8], F32, pctx)
            mkcols = self.sb("mkcols", [128, 8], F32, pctx)
            mk1cols = self.sb("mk1cols", [128, 8], F32, pctx)

            @blk.sync
            def _(sync):
                s_dma.inc(sync.dma_start(
                    out=murow[:, 0:128],
                    in_=io["mu_r"].rearrange("(a p) -> a p", p=128)), 16)
                s_dma.inc(sync.dma_start(
                    out=murow[:, 128:256],
                    in_=io["mu_k"].rearrange("(a p) -> a p", p=128)), 16)
            muval = s_dma.n

            @blk.tensor
            def _(tensor):
                tensor.wait_ge(s_dma.h, muval)
                self.wait_guard(tensor, "PS_C")
                nc.tensor.transpose(self.PS_C[:, 0:8], murow[:, 0:128],
                                    self.ident32[0:8, 0:8])
                s_mm.inc(nc.tensor.transpose(self.PS_C[:, 8:16],
                                             murow[:, 128:256],
                                             self.ident32[0:8, 0:8]))

            @blk.vector
            def _(vector):
                vector.wait_ge(s_mm.h, s_mm.n)
                vector.tensor_copy(mucols[:], self.PS_C[:, 0:8])
                self.vsync(vector, vector.tensor_copy(mkcols[:],
                                                      self.PS_C[:, 8:16]))
                vector.tensor_scalar(mu1cols[:], mucols[:], 1.0, -1.0,
                                     ALU.subtract, ALU.mult)
                s_v.inc(vector.tensor_scalar(mk1cols[:], mkcols[:], 1.0, -1.0,
                                             ALU.subtract, ALU.mult))
            self.set_guard("PS_C", s_v)

            for wj, (wname, dst) in enumerate(
                    (("W_r", wrT), ("W_k", wkT), ("W_v", wvT))):
                wpar = wj % 2
                wsl = wstage[:, 8192 * wpar:8192 * wpar + 8192]
                swd = self.s_p[2 + wpar]

                @blk.gpsimd
                def _(gpsimd, wname=wname, wsl=wsl, wpar=wpar, swd=swd):
                    self.wait_guard(gpsimd, f"wstage{wpar}")
                    swd.inc(gpsimd.dma_start(
                        out=wsl.rearrange("p (dj c) -> p dj c", c=1024),
                        in_=io[wname].rearrange("(dj p) c -> p dj c", p=128)),
                        16)
                self.emit_wT_transposes(wsl, dst, swd.n, dma_ctr=swd)
                self.set_guard(f"wstage{wpar}", s_mm)

            if dbg:
                self.dbgx = self.sb("dbgx", [128, 2048], F32, pctx)
                self.dbg_t = dict(xrS=xrS, xkS=xkS, rS=rS, kS=kS, wrT=wrT,
                                  wkT=wkT, wvT=wvT, dbgx=self.dbgx)
            xb, xsrc = self.xbuf[1], self.xsrc[1]
            build_xt = (2 in self.blocks)
            mixvals = {}

            def emit_mix(tc):
                par = tc % 2
                moff = 4096 * par

                @blk.vector
                def _(vector, tc=tc, moff=moff, par=par):
                    self.wait_guard(vector, f"mixbuf{par}")
                    vector.wait_ge(s_v.h, s_v.n)
                    for ci in range(NCH):
                        base = XCW * ci + XO + 512 * tc
                        for cm, c1, dstS in ((mucols, mu1cols, xrS),
                                             (mkcols, mk1cols, xkS)):
                            self.vsync(vector, vector.tensor_scalar(
                                mixtmp[:], self.xT[:, base - 1:base - 1 + 512],
                                c1[:, ci:ci + 1], None, ALU.mult))
                            inst = vector.scalar_tensor_tensor(
                                dstS[:, moff + 512 * ci:moff + 512 * ci + 512],
                                self.xT[:, base:base + 512],
                                cm[:, ci:ci + 1], mixtmp[:],
                                ALU.mult, ALU.add)
                    s_v.inc(inst)
                mixvals[tc] = s_v.n

            emit_mix(0)
            for tc in range(1 if dbg else 4):
                self.cut()
                moff = 4096 * (tc % 2)
                mixval = mixvals[tc]

                for tl in range(4):
                    pa = self.PS_B1k if tl % 2 == 0 else self.PS_A1k
                    pkey = "PS_B" if tl % 2 == 0 else "PS_A"

                    @blk.tensor
                    def _(tensor, tl=tl, pa=pa, pkey=pkey, mixval=mixval,
                          moff=moff):
                        tensor.wait_ge(s_v.h, mixval)
                        self.wait_guard(tensor, pkey)
                        for ci in range(NCH):
                            for n in range(2):
                                inst = nc.tensor.matmul(
                                    pa[:, 512 * n:512 * n + 512],
                                    xrS[:, moff + 512 * ci + 128 * tl:
                                        moff + 512 * ci + 128 * tl + 128],
                                    wrT[:, 1024 * ci + 512 * n:
                                        1024 * ci + 512 * n + 512],
                                    start=(ci == 0), stop=(ci == NCH - 1))
                        s_mm.inc(inst)

                    @blk.scalar
                    def _(scalar, tl=tl, pa=pa):
                        scalar.wait_ge(s_mm.h, s_mm.n)
                        self.wait_guard(scalar, f"rSbuf{tl}")
                        s_act.inc(scalar.activation(
                            rS[:, 1024 * tl:1024 * tl + 1024], pa[:],
                            AF.Sigmoid))
                    self.set_guard(pkey, s_act)

                if tc + 1 < 4 and not dbg:
                    emit_mix(tc + 1)
                for dp in range(NCH // 2):
                    slot = 1024 * (dp % 2)

                    @blk.tensor
                    def _(tensor, dp=dp, slot=slot, mixval=mixval, moff=moff):
                        tensor.wait_ge(s_v.h, mixval)
                        self.wait_guard(tensor, f"kslot{dp % 2}")
                        if dp < 1:
                            self.wait_guard(tensor, "PS_A")
                        for j in range(2):
                            dj = 2 * dp + j
                            for ci in range(NCH):
                                inst = nc.tensor.matmul(
                                    self.PS_A[:, slot + 512 * j:
                                              slot + 512 * j + 512],
                                    wkT[:, 1024 * ci + 128 * dj:
                                        1024 * ci + 128 * dj + 128],
                                    xkS[:, moff + 512 * ci:
                                        moff + 512 * ci + 512],
                                    start=(ci == 0), stop=(ci == NCH - 1))
                        s_mm.inc(inst)

                    @blk.scalar
                    def _(scalar, dp=dp, slot=slot):
                        scalar.wait_ge(s_mm.h, s_mm.n)
                        self.wait_guard(scalar, f"kSbuf{dp}")
                        s_act.inc(scalar.activation(
                            kS[:, 1024 * dp:1024 * dp + 1024],
                            self.PS_A[:, slot:slot + 1024], AF.Relu))

                    @blk.vector
                    def _(vector, dp=dp):
                        vector.wait_ge(s_act.h, s_act.n)
                        s_v.inc(vector.tensor_mul(
                            kS[:, 1024 * dp:1024 * dp + 1024],
                            kS[:, 1024 * dp:1024 * dp + 1024],
                            kS[:, 1024 * dp:1024 * dp + 1024]))
                    self.set_guard(f"kslot{dp % 2}", s_v)
                    self.set_guard(f"kSbuf{dp}", s_v)
                kdone = s_v.n
                self.set_guard("PS_A", s_v)
                self.set_guard(f"mixbuf{tc % 2}", s_mm)

                for tl in range(4):
                    ti = 4 * tc + tl
                    dv = self.prefetch_x(xsrc, ti)
                    pa = self.PS_B1k if tl % 2 == 0 else self.PS_A1k
                    pkey = "PS_B" if tl % 2 == 0 else "PS_A"
                    self._ln_psum_key = pkey

                    @blk.tensor
                    def _(tensor, tl=tl, pa=pa, pkey=pkey, kdone=kdone):
                        tensor.wait_ge(s_v.h, kdone)
                        self.wait_guard(tensor, pkey)
                        for dj in range(NCH):
                            for n in range(2):
                                inst = nc.tensor.matmul(
                                    pa[:, 512 * n:512 * n + 512],
                                    kS[:, 512 * dj + 128 * tl:
                                       512 * dj + 128 * tl + 128],
                                    wvT[:, 1024 * dj + 512 * n:
                                        1024 * dj + 512 * n + 512],
                                    start=(dj == 0), stop=(dj == NCH - 1))
                        s_mm.inc(inst)

                    def prod_y(vector, xsl, pa=pa, tl=tl):
                        self.vsync(vector, vector.scalar_tensor_tensor(
                            self.ffntmp[:], pa[:], 1.0,
                            rS[:, 1024 * tl:1024 * tl + 1024],
                            ALU.bypass, ALU.mult))
                        self.vsync(vector, vector.tensor_add(
                            xsl, xsl, self.ffntmp[:]))

                    self.emit_ln_tile(ti, pa, xb, build_xt, s_mm.n,
                                      produce_y=prod_y, dma_val=dv)
                    self.set_guard(f"rSbuf{tl}", s_v)

    def phase_block2(self):
        blk, nc, io = self.blk, self.nc, self.io
        s_dma, s_gdma, s_mm, s_v, s_act = (self.s_dma, self.s_gdma, self.s_mm,
                                           self.s_v, self.s_act)
        with ExitStack() as pctx:
            qT = self.sb("qT", [128, NCH * T], BF16, pctx)
            kT = self.sb("kT", [128, NCH * T], BF16, pctx)
            vTok = self.sb("vTok", [128, NT * 1024], BF16, pctx)
            wT = self.sb("wT", [128, NCH * 1024], BF16, pctx)
            wstage = self.sb("wstage2", [128, NCH * 1024], BF16, pctx)
            brow = self.sb("brow", [8, 256], F32, pctx)
            bcols = self.sb("bcols", [128, 16], F32, pctx)
            bvrow = self.sb("bvrow", [1, 1024], BF16, pctx)

            @blk.sync
            def _(sync):
                for wi2, bname in ((0, "bq"), (1, "bk")):
                    s_dma.inc(sync.dma_start(
                        out=brow[:, 128 * wi2:128 * wi2 + 128],
                        in_=io[bname].rearrange("(a p) -> a p", p=128)), 16)
            brval = s_dma.n
            for wi, (wname, bname) in enumerate(
                    (("Wq", "bq"), ("Wk", "bk"), ("Wv", "bv"))):
                @blk.gpsimd
                def _(gpsimd, wname=wname):
                    self.wait_guard(gpsimd, "wstage2")
                    self.s_wld.inc(gpsimd.dma_start(
                        out=wstage[:, :].rearrange("p (dj c) -> p dj c", c=1024),
                        in_=io[wname].rearrange("(dj p) c -> p dj c", p=128)),
                        16)
                wval = self.s_wld.n

                if wi < 2:
                    @blk.tensor
                    def _(tensor, wi=wi, bval=brval):
                        tensor.wait_ge(s_dma.h, bval)
                        self.wait_guard(tensor, "PS_C")
                        s_mm.inc(nc.tensor.transpose(
                            self.PS_C[:, 8 * wi:8 * wi + 8],
                            brow[:, 128 * wi:128 * wi + 128],
                            self.ident32[0:8, 0:8]))

                    @blk.vector
                    def _(vector, wi=wi):
                        vector.wait_ge(s_mm.h, s_mm.n)
                        s_v.inc(vector.tensor_copy(
                            bcols[:, 8 * wi:8 * wi + 8],
                            self.PS_C[:, 8 * wi:8 * wi + 8]))
                    self.set_guard("PS_C", s_v)
                else:
                    @blk.gpsimd
                    def _(gpsimd):
                        s_gdma.inc(gpsimd.dma_start(
                            out=bvrow[:],
                            in_=io["bv"].rearrange("(a c) -> a c", a=1)), 16)
                    bvg = s_gdma.n

                    @blk.vector
                    def _(vector, bvg=bvg):
                        vector.wait_ge(s_gdma.h, bvg)
                        s_v.inc(vector.tensor_copy(bvrow[:], bvrow[:]))

                wtval = self.emit_wT_transposes(wstage, wT, wval,
                                                busykey="wTbusy")
                self.set_guard("wstage2", s_mm)

                if wi < 2:
                    dst = qT if wi == 0 else kT
                    for m in range(NCH):
                        @blk.tensor
                        def _(tensor, m=m, wtval=wtval):
                            tensor.wait_ge(s_v.h, wtval)
                            self.wait_guard(tensor, "PS_A")
                            for ci in range(NCH):
                                for n in range(4):
                                    inst = nc.tensor.matmul(
                                        self.PS_A[:, 512 * n:512 * n + 512],
                                        wT[:, 1024 * ci + 128 * m:
                                           1024 * ci + 128 * m + 128],
                                        self.xT[:, XCW * ci + XO + 512 * n:
                                                XCW * ci + XO + 512 * n + 512],
                                        start=(ci == 0), stop=(ci == NCH - 1))
                            s_mm.inc(inst)

                        @blk.scalar
                        def _(scalar, m=m, dst=dst, wi=wi):
                            scalar.wait_ge(s_mm.h, s_mm.n)
                            s_act.inc(scalar.activation(
                                dst[:, T * m:T * m + T], self.PS_A[:],
                                AF.Identity,
                                bias=bcols[:, 8 * wi + m:8 * wi + m + 1]))
                        self.set_guard("PS_A", s_act)
                    self.set_guard("wTbusy", s_act)
                else:
                    vready = s_v.n
                    self._emit_l1_v(wT, bvrow, vTok, vready)

            xb, xsrc = self.xbuf[2], self.xsrc[2]
            build_xt = (3 in self.blocks)
            def finish2(ctx):
                qi = ctx["qi"]
                pn = self.attn_post(ctx)
                self.emit_p_transposes(qi, pn, ctx["P_s"], ctx["pskey"])

                @blk.tensor
                def _(tensor, qi=qi):
                    tensor.wait_ge(s_v.h, s_v.n)
                    tensor.wait_ge(s_act.h, s_act.n)
                    self.wait_guard(tensor, "PS_B")
                    for si in range(qi + 1):
                        for n in range(2):
                            inst = nc.tensor.matmul(
                                self.PS_B[:, 512 * n:512 * n + 512],
                                self.PT[:, 128 * si:128 * si + 128],
                                vTok[:, 1024 * si + 512 * n:
                                     1024 * si + 512 * n + 512],
                                start=(si == 0), stop=(si == qi))
                    s_mm.inc(inst)

                self._ln_psum_key = "PS_B"
                self.emit_ln_tile(qi, self.PS_B1k, xb, build_xt, s_mm.n,
                                  dma_val=ctx["dv"])

            pend = None
            for qi in range(NT):
                if qi % 4 == 0:
                    self.cut()
                dv = self.prefetch_x(xsrc, qi)

                def emit_S(tensor, c0, cw, poff, has_diag, qi=qi):
                    for ci in range(NCH):
                        for n0 in range(0, cw, 512):
                            nw = min(512, cw - n0)
                            inst = nc.tensor.matmul(
                                self.PS_A[:, poff + n0:poff + n0 + nw],
                                qT[:, T * ci + 128 * qi:T * ci + 128 * qi + 128],
                                kT[:, T * ci + c0 + n0:T * ci + c0 + n0 + nw],
                                start=(ci == 0), stop=False)
                    if has_diag:
                        inst = nc.tensor.matmul(
                            self.PS_A[:, poff + cw - 128:poff + cw],
                            self.identbf[:], self.maskbf[:],
                            start=False, stop=True)
                    s_mm.inc(inst)

                ctx = self.attn_pre(qi, self.cols[:, 2:3], emit_S)
                ctx["dv"] = dv
                if pend is not None:
                    finish2(pend)
                pend = ctx
            finish2(pend)

    def _emit_l1_v(self, wT, bvrow, vTok, vready):
        blk, nc = self.blk, self.nc
        s_mm, s_v = self.s_mm, self.s_v
        for ti in range(NT):
            pa = self.PS_B1k if ti % 2 == 0 else self.PS_A1k
            pkey = "PS_B" if ti % 2 == 0 else "PS_A"

            @blk.tensor
            def _(tensor, ti=ti, pa=pa, pkey=pkey, vready=vready):
                tensor.wait_ge(s_v.h, vready)
                self.wait_guard(tensor, pkey)
                for n in range(2):
                    for ci in range(NCH):
                        nc.tensor.matmul(
                            pa[:, 512 * n:512 * n + 512],
                            self.xT[:, XCW * ci + XO + 128 * ti:
                                    XCW * ci + XO + 128 * ti + 128],
                            wT[:, 1024 * ci + 512 * n:1024 * ci + 512 * n + 512],
                            start=(ci == 0), stop=False)
                    inst = nc.tensor.matmul(
                        pa[:, 512 * n:512 * n + 512], self.onesbf[0:1, :],
                        bvrow[0:1, 512 * n:512 * n + 512],
                        start=False, stop=True)
                s_mm.inc(inst)

            @blk.scalar
            def _(scalar, ti=ti, pa=pa):
                scalar.wait_ge(s_mm.h, s_mm.n)
                self.s_act.inc(scalar.activation(
                    vTok[:, 1024 * ti:1024 * ti + 1024], pa[:], AF.Identity))
            self.set_guard(pkey, self.s_act)

    def phase_block3(self):
        blk, nc, io = self.blk, self.nc, self.io
        s_dma, s_gdma, s_mm, s_v, s_act = (self.s_dma, self.s_gdma, self.s_mm,
                                           self.s_v, self.s_act)
        with ExitStack() as pctx:
            wfcT = self.sb("wfcT", [128, NCH * 4096], BF16, pctx)
            qstage = self.sb("qstage", [128, 2 * 1024], F32, pctx)
            qbf = self.sb("qbf", [128, 2048], BF16, pctx)
            wps = self.sb("wps", [128, 1024], BF16, pctx)
            hS = self.sb("hS", [128, 32 * 512], BF16, pctx)
            wpsl = self.sb("wpsl", [128, 4 * 1024], BF16, pctx)
            qcols = self.sb("qcols", [128, 8], F32, pctx)
            qrow = self.sb("qrow", [1, 8], F32, pctx)
            ones32 = self.sb("ones32", [1, 128], F32, pctx)
            afT = self.sb("afT", [128, NCH * 16], BF16, pctx)
            apT = self.sb("apT", [128, 32 * 16], BF16, pctx)
            bfT = self.sb("bfT", [16, 4096], BF16, pctx)
            bpT = self.sb("bpT", [16, 1024], BF16, pctx)
            bstage = self.sb("bstage", [128, 512], BF16, pctx)
            arow = self.sb("arow", [16, 4 * 1024], BF16, pctx)
            bfcrow = self.sb("bfcrow", [32, 128], F32, pctx)
            bfccols = self.sb("bfccols", [128, 32], F32, pctx)
            bprow = self.sb("bprow", [1, 1024], BF16, pctx)
            m1T = self.sb("m1T", [16, 512], BF16, pctx)
            m2T = self.sb("m2T", [16, 512], BF16, pctx)

            @blk.vector
            def _(vector):
                s_v.inc(vector.memset(ones32[:], 1.0))

            self._quant_weight(
                "W_fc", [(128 * sj, 0, 1024) for sj in range(32)],
                wfcT, None, qstage, qbf, wps, qcols, qrow, ones32, 0)
            self._quant_weight(
                "W_proj1", [(128 * dj, 1024 * h, 1024)
                            for dj in range(8) for h in range(4)],
                None, io["wpcache"], qstage, qbf, wps, qcols, qrow, ones32, 4)

            @blk.gpsimd
            def _(gpsimd):
                s_gdma.inc(gpsimd.dma_start(out=arow[0:16, 3072:4096],
                                            in_=io["A_fc"]), 16)
                s_gdma.inc(gpsimd.dma_start(
                    out=bstage[:, :].rearrange("p (c r) -> p c r", r=16),
                    in_=io["B_fc"].rearrange("(c p) r -> p c r", p=128)), 16)
            laval = s_gdma.n

            @blk.sync
            def _(sync):
                s_dma.inc(sync.dma_start(
                    out=bfcrow[:],
                    in_=io["b_fc"].rearrange("(a p) -> a p", p=128)), 16)
            bfval = s_dma.n

            @blk.gpsimd
            def _(gpsimd):
                s_gdma.inc(gpsimd.dma_start(
                    out=bprow[:],
                    in_=io["b_proj"].rearrange("(a c) -> a c", a=1)), 16)
            bpg = s_gdma.n

            @blk.tensor
            def _(tensor):
                tensor.wait_ge(s_dma.h, bfval)
                self.wait_guard(tensor, "PS_C")
                s_mm.inc(nc.tensor.transpose(self.PS_C[:, 0:32], bfcrow[:],
                                             self.ident32[0:32, 0:32]))

            @blk.vector
            def _(vector, bpg=bpg):
                vector.wait_ge(s_mm.h, s_mm.n)
                vector.wait_ge(s_gdma.h, bpg)
                s_v.inc(vector.tensor_copy(bfccols[:], self.PS_C[:, 0:32]))
            self.set_guard("PS_C", s_v)

            for j in range(8):
                @blk.tensor
                def _(tensor, j=j, laval=laval):
                    tensor.wait_ge(s_gdma.h, laval)
                    self.wait_guard(tensor, "PS_D")
                    s_mm.inc(nc.tensor.transpose(
                        self.PS_D[:, 0:16],
                        arow[0:16, 3072 + 128 * j:3072 + 128 * j + 128],
                        self.identbf[0:16, 0:16]))

                @blk.vector
                def _(vector, j=j):
                    vector.wait_ge(s_mm.h, s_mm.n)
                    s_v.inc(vector.tensor_copy(afT[:, 16 * j:16 * j + 16],
                                               self.PS_D[:, 0:16]))
                self.set_guard("PS_D", s_v)
            afdone = s_mm.n

            @blk.gpsimd
            def _(gpsimd, afdone=afdone):
                gpsimd.wait_ge(s_mm.h, afdone)
                s_gdma.inc(gpsimd.dma_start(out=arow[0:16, :],
                                            in_=io["A_proj"]), 16)
            apval = s_gdma.n
            for j in range(32):
                @blk.tensor
                def _(tensor, j=j, apval=apval):
                    tensor.wait_ge(s_gdma.h, apval)
                    self.wait_guard(tensor, "PS_D")
                    s_mm.inc(nc.tensor.transpose(
                        self.PS_D[:, 0:16],
                        arow[0:16, 128 * j:128 * j + 128],
                        self.identbf[0:16, 0:16]))

                @blk.vector
                def _(vector, j=j):
                    vector.wait_ge(s_mm.h, s_mm.n)
                    s_v.inc(vector.tensor_copy(apT[:, 16 * j:16 * j + 16],
                                               self.PS_D[:, 0:16]))
                self.set_guard("PS_D", s_v)

            for cJ in range(32):
                @blk.tensor
                def _(tensor, cJ=cJ, laval=laval):
                    tensor.wait_ge(s_gdma.h, laval)
                    self.wait_guard(tensor, "PS_D")
                    s_mm.inc(nc.tensor.transpose(
                        self.PS_D[0:16, 0:128],
                        bstage[:, 16 * cJ:16 * cJ + 16], self.identbf[:]))

                @blk.vector
                def _(vector, cJ=cJ):
                    vector.wait_ge(s_mm.h, s_mm.n)
                    s_v.inc(vector.tensor_copy(
                        bfT[0:16, 128 * cJ:128 * cJ + 128],
                        self.PS_D[0:16, 0:128]))
                self.set_guard("PS_D", s_v)
            self.set_guard("bstage", s_mm)

            @blk.gpsimd
            def _(gpsimd):
                self.wait_guard(gpsimd, "bstage")
                s_gdma.inc(gpsimd.dma_start(
                    out=bstage[:, 0:128].rearrange("p (c r) -> p c r", r=16),
                    in_=io["B_proj"].rearrange("(c p) r -> p c r", p=128)), 16)
            bpval = s_gdma.n
            for cJ in range(8):
                @blk.tensor
                def _(tensor, cJ=cJ, bpval=bpval):
                    tensor.wait_ge(s_gdma.h, bpval)
                    self.wait_guard(tensor, "PS_D")
                    s_mm.inc(nc.tensor.transpose(
                        self.PS_D[0:16, 0:128],
                        bstage[:, 16 * cJ:16 * cJ + 16], self.identbf[:]))

                @blk.vector
                def _(vector, cJ=cJ):
                    vector.wait_ge(s_mm.h, s_mm.n)
                    s_v.inc(vector.tensor_copy(
                        bpT[0:16, 128 * cJ:128 * cJ + 128],
                        self.PS_D[0:16, 0:128]))
                self.set_guard("PS_D", s_v)

            xb, xsrc = self.xbuf[3], self.xsrc[3]
            for tc in range(4):
                self.cut()
                hvals = []
                @blk.tensor
                def _(tensor, tc=tc):
                    tensor.wait_ge(s_v.h, s_v.n)
                    self.wait_guard(tensor, "PS_C")
                    self.wait_guard(tensor, "m1T")
                    for ci in range(NCH):
                        inst = nc.tensor.matmul(
                            self.PS_C[0:16, 0:512],
                            afT[:, 16 * ci:16 * ci + 16],
                            self.xT[:, XCW * ci + XO + 512 * tc:
                                    XCW * ci + XO + 512 * tc + 512],
                            start=(ci == 0), stop=(ci == NCH - 1))
                    s_mm.inc(inst)

                @blk.vector
                def _(vector):
                    vector.wait_ge(s_mm.h, s_mm.n)
                    s_v.inc(vector.tensor_scalar(m1T[0:16, :],
                                                 self.PS_C[0:16, 0:512],
                                                 2.0, None, ALU.mult))
                self.set_guard("PS_C", s_v)
                m1val = s_v.n

                for fi in range(32):
                    slot = 512 * (fi % 4)

                    @blk.tensor
                    def _(tensor, tc=tc, fi=fi, slot=slot, m1val=m1val):
                        tensor.wait_ge(s_v.h, m1val)
                        self.wait_guard(tensor, f"hslot{fi % 4}")
                        if fi == 0:
                            self.wait_guard(tensor, "PS_A")
                            self.wait_guard(tensor, "hS")
                        for ci in range(NCH):
                            nc.tensor.matmul(
                                self.PS_A[:, slot:slot + 512],
                                wfcT[:, 4096 * ci + 128 * fi:
                                     4096 * ci + 128 * fi + 128],
                                self.xT[:, XCW * ci + XO + 512 * tc:
                                        XCW * ci + XO + 512 * tc + 512],
                                start=(ci == 0), stop=False)
                        inst = nc.tensor.matmul(
                            self.PS_A[:, slot:slot + 512],
                            bfT[0:16, 128 * fi:128 * fi + 128],
                            m1T[0:16, 0:512], start=False, stop=True)
                        s_mm.inc(inst)

                    @blk.scalar
                    def _(scalar, fi=fi, slot=slot):
                        scalar.wait_ge(s_mm.h, s_mm.n)
                        s_act.inc(scalar.activation(
                            hS[:, 512 * fi:512 * fi + 512],
                            self.PS_A[:, slot:slot + 512], AF.Gelu,
                            bias=bfccols[:, fi:fi + 1]))
                    self.set_guard(f"hslot{fi % 4}", s_act)
                    hvals.append(s_act.n)
                hval = s_act.n

                @blk.tensor
                def _(tensor, hvals=hvals):
                    self.wait_guard(tensor, "PS_C")
                    self.wait_guard(tensor, "m2T")
                    for fj in range(32):
                        tensor.wait_ge(s_act.h, hvals[fj])
                        inst = nc.tensor.matmul(
                            self.PS_C[0:16, 0:512],
                            apT[:, 16 * fj:16 * fj + 16],
                            hS[:, 512 * fj:512 * fj + 512],
                            start=(fj == 0), stop=(fj == 31))
                    s_mm.inc(inst)

                @blk.vector
                def _(vector):
                    vector.wait_ge(s_mm.h, s_mm.n)
                    s_v.inc(vector.tensor_scalar(m2T[0:16, :],
                                                 self.PS_C[0:16, 0:512],
                                                 2.0, None, ALU.mult))
                self.set_guard("PS_C", s_v)
                m2val = s_v.n

                for hc in (0, 1):
                    tls = (2 * hc, 2 * hc + 1)
                    dvs = [self.prefetch_x(xsrc, 4 * tc + tl) for tl in tls]
                    pas = [self.PS_B1k, self.PS_A1k]
                    pkeys = ["PS_B", "PS_A"]
                    mmbase = s_mm.n
                    wpv = []
                    for fj in range(32):
                        wk = fj % 4

                        sp = self.s_p[wk]

                        @blk.sync
                        def _(sync, fj=fj, wk=wk, mmbase=mmbase, sp=sp):
                            if fj >= 4:
                                sync.wait_ge(s_mm.h, mmbase + fj - 3)
                            else:
                                self.wait_guard(sync, f"wpsl{wk}")
                            sp.inc(sync.dma_start(
                                out=wpsl[:, 1024 * wk:1024 * wk + 1024],
                                in_=io["wpcache"][128 * fj:128 * fj + 128, :]),
                                16)
                        wpv.append((sp, sp.n))

                    @blk.tensor
                    def _(tensor, tc=tc, tls=tls, pas=pas, m2val=m2val,
                          hvals=hvals, wpv=wpv):
                        tensor.wait_ge(s_v.h, m2val)
                        self.wait_guard(tensor, "PS_B")
                        self.wait_guard(tensor, "PS_A")
                        for fj in range(32):
                            tensor.wait_ge(s_act.h, hvals[fj])
                            tensor.wait_ge(wpv[fj][0].h, wpv[fj][1])
                            wk = fj % 4
                            for j, tl in enumerate(tls):
                                for n in range(2):
                                    inst = nc.tensor.matmul(
                                        pas[j][:, 512 * n:512 * n + 512],
                                        hS[:, 512 * fj + 128 * tl:
                                           512 * fj + 128 * tl + 128],
                                        wpsl[:, 1024 * wk + 512 * n:
                                             1024 * wk + 512 * n + 512],
                                        start=(fj == 0), stop=False)
                            s_mm.inc(inst)
                        for j, tl in enumerate(tls):
                            for n in range(2):
                                nc.tensor.matmul(
                                    pas[j][:, 512 * n:512 * n + 512],
                                    m2T[0:16, 128 * tl:128 * tl + 128],
                                    bpT[0:16, 512 * n:512 * n + 512],
                                    start=False, stop=False)
                                inst = nc.tensor.matmul(
                                    pas[j][:, 512 * n:512 * n + 512],
                                    self.onesbf[0:1, :],
                                    bprow[0:1, 512 * n:512 * n + 512],
                                    start=False, stop=True)
                        s_mm.inc(inst)
                    for wk2 in range(4):
                        self.set_guard(f"wpsl{wk2}", s_mm)
                    mmv = s_mm.n

                    for j, tl in enumerate(tls):
                        self._ln_psum_key = pkeys[j]
                        self.emit_ln_tile(4 * tc + tl, pas[j], xb, False, mmv,
                                          dma_val=dvs[j])
                self.set_guard("m1T", s_mm)
                self.set_guard("m2T", s_mm)
                self.set_guard("hS", s_mm)

    def _quant_weight(self, wname, slices, wT_dst, cache_dst, qstage,
                      qbf, wps, qcols, qrow, ones32, qc0):
        blk, nc, io = self.blk, self.nc, self.io
        s_dma, s_mm, s_v = self.s_dma, self.s_mm, self.s_v
        Wap = io[wname]
        O, I = Wap.shape

        @blk.vector
        def _(vector):
            s_v.inc(vector.memset(qcols[:, qc0:qc0 + 1], 0.0))
        for si_, (r0, c0, ncl) in enumerate(slices):
            qs = qstage[:, 1024 * (si_ % 2):1024 * (si_ % 2) + ncl]

            sq = self.s_p[2 + si_ % 2]

            @blk.sync
            def _(sync, r0=r0, c0=c0, ncl=ncl, qs=qs, si_=si_, sq=sq):
                self.wait_guard(sync, f"qstage{si_ % 2}")
                sq.inc(sync.dma_start(
                    out=qs, in_=Wap[r0:r0 + 128, c0:c0 + ncl]), 16)
            dval = sq.n

            @blk.vector
            def _(vector, dval=dval, qs=qs, sq=sq):
                vector.wait_ge(sq.h, dval)
                self.vsync(vector, vector.reduce_max(
                    qcols[:, qc0 + 1:qc0 + 2], qs,
                    axis=AX.X, apply_absolute_value=True))
                s_v.inc(vector.tensor_max(qcols[:, qc0:qc0 + 1],
                                          qcols[:, qc0:qc0 + 1],
                                          qcols[:, qc0 + 1:qc0 + 2]))
            self.set_guard(f"qstage{si_ % 2}", s_v)

        @blk.tensor
        def _(tensor):
            tensor.wait_ge(s_v.h, s_v.n)
            self.wait_guard(tensor, "PS_C")
            s_mm.inc(nc.tensor.transpose(self.PS_C[0:1, 0:128],
                                         qcols[:, qc0:qc0 + 1],
                                         self.ident32[:]))

        @blk.vector
        def _(vector):
            vector.wait_ge(s_mm.h, s_mm.n)
            self.vsync(vector, vector.reduce_max(qrow[0:1, 0:1],
                                                 self.PS_C[0:1, 0:128],
                                                 axis=AX.X))
            vector.tensor_scalar(qrow[0:1, 1:2], qrow[0:1, 0:1],
                                 1.0 / 127.0, None, ALU.mult)
            self.vsync(vector, vector.reciprocal(qrow[0:1, 2:3],
                                                 qrow[0:1, 0:1]))
            s_v.inc(vector.tensor_scalar(qrow[0:1, 2:3], qrow[0:1, 2:3],
                                         127.0, None, ALU.mult))
        self.set_guard("PS_C", s_v)

        @blk.tensor
        def _(tensor):
            tensor.wait_ge(s_v.h, s_v.n)
            self.wait_guard(tensor, "PS_C")
            s_mm.inc(nc.tensor.matmul(self.PS_C[:, 0:2], ones32[0:1, :],
                                      qrow[0:1, 1:3], start=True, stop=True))

        @blk.vector
        def _(vector):
            vector.wait_ge(s_mm.h, s_mm.n)
            s_v.inc(vector.tensor_copy(qcols[:, qc0 + 2:qc0 + 4],
                                       self.PS_C[:, 0:2]))
        self.set_guard("PS_C", s_v)
        scol = qcols[:, qc0 + 2:qc0 + 3]
        rcol = qcols[:, qc0 + 3:qc0 + 4]

        for si_, (r0, c0, ncl) in enumerate(slices):
            qs = qstage[:, 1024 * (si_ % 2):1024 * (si_ % 2) + ncl]
            sq = self.s_p[2 + si_ % 2]

            @blk.sync
            def _(sync, r0=r0, c0=c0, ncl=ncl, qs=qs, si_=si_, sq=sq):
                self.wait_guard(sync, f"qstage{si_ % 2}")
                sq.inc(sync.dma_start(
                    out=qs, in_=Wap[r0:r0 + 128, c0:c0 + ncl]), 16)
            dval = sq.n

            @blk.vector
            def _(vector, dval=dval, qs=qs, ncl=ncl, sq=sq):
                vector.wait_ge(sq.h, dval)
                self.wait_guard(vector, "qbf")
                self.vsync(vector, vector.tensor_scalar(
                    qs, qs, rcol, MAGIC, ALU.mult, ALU.add))
                self.vsync(vector, vector.tensor_scalar(
                    qs, qs, MAGIC, -128.0, ALU.subtract, ALU.max))
                s_v.inc(vector.tensor_scalar(qbf[:, 0:ncl], qs,
                                             127.0, scol, ALU.min, ALU.mult))
            self.set_guard(f"qstage{si_ % 2}", s_v)
            qval = s_v.n

            nI = ncl // 128
            for w0 in range(0, nI, 8):
                w1 = min(w0 + 8, nI)

                @blk.tensor
                def _(tensor, w0=w0, w1=w1, qval=qval):
                    tensor.wait_ge(s_v.h, qval)
                    self.wait_guard(tensor, "PS_D")
                    for j in range(w0, w1):
                        inst = nc.tensor.transpose(
                            self.PS_D[:, 128 * (j - w0):128 * (j - w0) + 128],
                            qbf[:, 128 * j:128 * j + 128], self.identbf[:])
                    s_mm.inc(inst)

                if wT_dst is not None:
                    @blk.vector
                    def _(vector, r0=r0, w0=w0, w1=w1):
                        vector.wait_ge(s_mm.h, s_mm.n)
                        dv_ = wT_dst[:, :].rearrange(
                            "p (ci f) -> p ci f", f=O)[
                            :, w0:w1, r0:r0 + 128]
                        sv_ = self.PS_D[:, 0:128 * (w1 - w0)].rearrange(
                            "p (kk c) -> p kk c", c=128)
                        s_v.inc(vector.tensor_copy(dv_, sv_))
                    self.set_guard("PS_D", s_v)
                else:
                    @blk.vector
                    def _(vector, w0=w0, w1=w1):
                        vector.wait_ge(s_mm.h, s_mm.n)
                        s_v.inc(vector.tensor_copy(
                            wps[:, 128 * w0:128 * w1],
                            self.PS_D[:, 0:128 * (w1 - w0)]))
                    self.set_guard("PS_D", s_v)
            if cache_dst is not None:
                wval = s_v.n

                @blk.sync
                def _(sync, r0=r0, c0=c0, nI=nI, wval=wval):
                    sync.wait_ge(s_v.h, wval)
                    f0 = c0 // 128
                    dst = cache_dst.rearrange("(fj p) d -> p fj d", p=128)[
                        :, f0:f0 + nI, r0:r0 + 128]
                    src = wps[:, 0:128 * nI].rearrange("p (fj d) -> p fj d",
                                                       d=128)
                    self.s_out.inc(sync.dma_start(out=dst, in_=src), 16)
                self.set_guard("qbf", self.s_out)
            else:
                self.set_guard("qbf", self.s_mm)


_consts_cache = None


def _consts():
    global _consts_cache
    if _consts_cache is None:
        ident32 = np.eye(128, dtype=np.float32)
        identbf = np.eye(128).astype(ml_dtypes.bfloat16)
        mask = np.zeros((128, 128), np.float32)
        for p in range(128):
            mask[p, p + 1:] = -1e30
        _consts_cache = dict(
            ident32=ident32, identbf=identbf,
            maskbf=mask.astype(ml_dtypes.bfloat16),
            onesbf=np.ones((1, 128), ml_dtypes.bfloat16))
    return _consts_cache


_BLOCK_INPUTS = {
    0: {"W_attn": "L0_attn_W_attn", "W_proj0": "L0_attn_W_proj"},
    1: {"W_r": "L0_ffn_W_r", "W_k": "L0_ffn_W_k", "W_v": "L0_ffn_W_v",
        "mu_r": "L0_ffn_mu_r", "mu_k": "L0_ffn_mu_k"},
    2: {"Wq": "L1_attn_Wq", "Wk": "L1_attn_Wk", "Wv": "L1_attn_Wv",
        "bq": "L1_attn_bq", "bk": "L1_attn_bk", "bv": "L1_attn_bv"},
    3: {"W_fc": "L1_ffn_W_fc", "b_fc": "L1_ffn_b_fc", "A_fc": "L1_ffn_A_fc",
        "B_fc": "L1_ffn_B_fc", "W_proj1": "L1_ffn_W_proj",
        "b_proj": "L1_ffn_b_proj", "A_proj": "L1_ffn_A_proj",
        "B_proj": "L1_ffn_B_proj"},
}

_prog_cache = {}


def _get_prog(blocks):
    key = tuple(blocks)
    if key not in _prog_cache:
        _prog_cache[key] = B(list(blocks)).build()
    return _prog_cache[key]


def run_blocks(blocks, x, inputs):
    nc = _get_prog(blocks)
    cns = _consts()
    maps = []
    for c in range(NCORE):
        m = dict(x_in=np.ascontiguousarray(x[c]))
        m.update(cns)
        for b in blocks:
            for kk, src in _BLOCK_INPUTS[b].items():
                m[kk] = np.ascontiguousarray(inputs[src])
        maps.append(m)
    res = run_bass_kernel_spmd(nc, maps, core_ids=list(range(NCORE)))
    return np.stack([np.asarray(r["y_out"]) for r in res.results], 0)


def kernel(**inputs):
    x = np.ascontiguousarray(inputs["L0_attn_x"], np.float32)
    y = run_blocks((0, 1, 2, 3), x, inputs)
    return y.astype(np.float32)
